# version 08
# baseline (speedup 1.0000x reference)
import numpy as np
import ml_dtypes
import concourse.bass as bass
import concourse.mybir as mybir
from concourse.bass_utils import run_bass_kernel_spmd

F32 = mybir.dt.float32
BF16 = mybir.dt.bfloat16
AF = mybir.ActivationFunctionType
ALU = mybir.AluOpType
AX = mybir.AxisListType
BF = ml_dtypes.bfloat16

D = 1024
NCH = 8
N_CORES = 8
TT = 512
STRICT_SYNC = True
FUSED = True


class Tok:
    __slots__ = ("sem", "val")

    def __init__(self, sem, val):
        self.sem = sem
        self.val = val


SEMS = ["sync_q", "gpsimd_q", "scalar_q", "sync", "tensor", "scalar",
        "vector", "gpsimd"]


class KB:
    ENGINES = ("sync", "tensor", "scalar", "vector", "gpsimd")

    def __init__(self, nc):
        self.nc = nc
        self.sem_h = {s: nc.alloc_semaphore(s) for s in SEMS}
        self.io = {}
        self.cur = None

    def begin(self, engine_name, engine_handle):
        self.cur = engine_name
        self.eng = engine_handle
        self.counts = {s: 0 for s in self.sem_h}
        self.watermark = {e: {} for e in self.ENGINES}

    def _waits(self, engine_name, waits):
        for t in waits:
            if t is None:
                continue
            if isinstance(t, (list, tuple)):
                self._waits(engine_name, t)
                continue
            if t.sem == engine_name and (
                    engine_name == "tensor" or not STRICT_SYNC):
                continue
            wm = self.watermark[engine_name]
            if wm.get(t.sem, -1) >= t.val:
                continue
            wm[t.sem] = t.val
            if self.cur == engine_name:
                self.eng.wait_ge(self.sem_h[t.sem], t.val)

    def op(self, engine_name, emit_fn, waits=(), inc=None, inc_by=1):
        if inc is None:
            inc = engine_name
        self._waits(engine_name, waits)
        self.counts[inc] += inc_by
        tok = Tok(inc, self.counts[inc])
        if self.cur == engine_name:
            ins = emit_fn(self.eng)
            if ins is not None:
                ins.then_inc(self.sem_h[inc], inc_by)
        return tok

    def noinc(self, engine_name, emit_fn, waits=()):
        self._waits(engine_name, waits)
        if self.cur == engine_name:
            emit_fn(self.eng)
        return None

    def fence(self, sem):
        return Tok(sem, self.counts[sem])

    def barrier(self):
        wm = self.watermark[self.cur]
        for s_, c_ in self.counts.items():
            if c_ == 0 or wm.get(s_, -1) >= c_:
                continue
            wm[s_] = c_
            self.eng.wait_ge(self.sem_h[s_], c_)

    def qend(self, queue):
        sem_q = queue + "_q"
        target = self.counts[sem_q]
        self.counts[queue] += 1
        tok = Tok(queue, self.counts[queue])
        wm = self.watermark[queue]
        if wm.get(sem_q, -1) < target:
            wm[sem_q] = target
        if self.cur == queue:
            self.eng.wait_ge(self.sem_h[sem_q], target).then_inc(
                self.sem_h[queue], 1)
        return tok

    def dma(self, queue, out, in_, waits=(), **kw):
        return self.op(queue, lambda e: e.dma_start(out, in_, **kw),
                       waits=waits, inc=queue + "_q", inc_by=16)

    def mm(self, out, lhsT, rhs, start, stop, waits=(), inc=True):
        fn = lambda e: e.matmul(out, lhsT, rhs, start=start, stop=stop)
        if stop and inc:
            return self.op("tensor", fn, waits=waits)
        return self.noinc("tensor", fn, waits=waits)

    def tp(self, out, in_, ident, waits=()):
        return self.op("tensor", lambda e: e.transpose(out, in_, ident),
                       waits=waits)

    def act(self, out, in_, func, bias=0.0, scale=1.0, waits=()):
        return self.op("scalar",
                       lambda e: e.activation(out, in_, func, bias=bias,
                                              scale=scale), waits=waits)

    def dve(self, method, *args, waits=(), **kw):
        return self.op("vector", lambda e: getattr(e, method)(*args, **kw),
                       waits=waits)

    def pool(self, method, *args, waits=(), **kw):
        return self.op("gpsimd", lambda e: getattr(e, method)(*args, **kw),
                       waits=waits)


def fm(v):
    v = np.asarray(v)
    r = v.shape[0]
    assert r % 128 == 0
    return np.ascontiguousarray(v.reshape(r // 128, 128).T)



def emit_layernorm(k, g, X_in, xn_out, T, var_sb=None, eps=1e-5):
    nt = T // TT
    xf = g["ln_xf"]
    sq = g["ln_sq"]
    xb = g["ln_xb"]
    row = g["ln_rows"]
    rowb = g["ln_rowb"]
    PS = g["PS"]
    ps_s, ps_q, ps_b1, ps_b2 = PS[0], PS[1], PS[2], PS[3]
    ct = g["const_tok"]
    xf_free = [None, None]
    stat_free = None
    brd_free = None
    sq_free = None
    st_tok = None
    for it in range(nt):
        buf = it % 2
        sl = slice(it * TT, (it + 1) * TT)
        for c in range(NCH):
            k.dma("sync", xf[:, buf, c, :], X_in[c, :, sl],
                  waits=[xf_free[buf]])
        ld = k.qend("sync")
        t_sq = []
        t_xb = []
        for c in range(NCH):
            t_sq.append(k.act(sq[:, c, :], xf[:, buf, c, :], AF.Square,
                              waits=[ld, sq_free]))
            t_xb.append(k.pool("tensor_copy", xb[:, c, :], xf[:, buf, c, :],
                               waits=[ld, sq_free]))
        tok_s = tok_q = None
        for c in range(NCH):
            tok_s = k.mm(ps_s[0:1, :], g["ones_col"], xb[:, c, :],
                         start=(c == 0), stop=(c == NCH - 1),
                         waits=[t_xb[c], ct, stat_free])
            tok_q = k.mm(ps_q[0:1, :], g["ones_col"], sq[:, c, :],
                         start=(c == 0), stop=(c == NCH - 1),
                         waits=[t_sq[c], ct, stat_free])
        m, q2, m2, var, sd, rr, mr = (row[0:1, i, :] for i in range(7))
        t1 = k.dve("tensor_single_scalar", m, ps_s[0:1, :], 1.0 / D,
                   ALU.mult, waits=[tok_s])
        t2 = k.dve("tensor_single_scalar", q2, ps_q[0:1, :], 1.0 / D,
                   ALU.mult, waits=[tok_q])
        stat_free = t2
        t3 = k.dve("tensor_mul", m2, m, m, waits=[t1])
        t4 = k.dve("tensor_sub", var, q2, m2, waits=[t2, t3])
        if var_sb is not None:
            k.dve("tensor_copy", var_sb[0:1, sl], var, waits=[t4])
        eps_ap, eps_tok = g["cc"][float(eps)]
        t5 = k.act(sd, var, AF.Sqrt, bias=eps_ap[0:1, :],
                   waits=[t4, eps_tok])
        t6 = k.dve("reciprocal", rr, sd, waits=[t5])
        t7 = k.dve("tensor_mul", mr, m, rr, waits=[t6])
        t8 = k.dve("tensor_copy", rowb[0:1, 0, :], rr, waits=[t6])
        t9 = k.dve("tensor_copy", rowb[0:1, 1, :], mr, waits=[t7])
        tb1 = k.mm(ps_b1, g["ones_row"], rowb[0:1, 0, :], start=True,
                   stop=True, waits=[t8, ct, brd_free])
        tb2 = k.mm(ps_b2, g["ones_row"], rowb[0:1, 1, :], start=True,
                   stop=True, waits=[t9, ct, brd_free])
        ta_l = tb_l = None
        for c in range(NCH):
            ta_l = k.dve("tensor_mul", sq[:, c, :], xf[:, buf, c, :], ps_b1,
                         waits=[tb1, tok_q, t_sq[c]])
            tb_l = k.dve("tensor_sub", xb[:, c, :], sq[:, c, :], ps_b2,
                         waits=[tb2, ta_l, tok_s, t_xb[c]])
            k.dma("sync", xn_out[c, :, sl], xb[:, c, :], waits=[tb_l])
        st_tok = k.qend("sync")
        brd_free = tb_l
        xf_free[buf] = [ta_l, t_sq[-1], t_xb[-1]]
        sq_free = [st_tok, tb_l]
    return st_tok



def emit_ffn1(k, g, io, T):
    nt = T // TT
    xn = g["xn_dram"]
    acc = g["acc_dram"]
    PS = g["PS"]
    ct = g["const_tok"]

    ln_done = emit_layernorm(k, g, io["x_in"], xn, T)

    ps_l, ps_t = PS[4], PS[5]
    gw = g["f_gw"]
    gb = g["f_gb"]
    r8 = g["f_r8"]
    ct8 = g["f_ct8"]
    ct8_free = None
    cmbTb = g["f_cmbTb"]
    xt2 = g["f_xt"]
    k.dma("gpsimd", gw[:], io["gate_w"].rearrange("(c p) e -> p c e", p=128))
    tgw = k.qend("gpsimd")
    k.dma("sync", gb[:], io["gate_b128"][:], waits=[ln_done])
    xt_free = [None, None]
    psl_free = None
    pst_free = None
    for it in range(nt):
        buf = it % 2
        sl = slice(it * TT, (it + 1) * TT)
        for c in range(NCH):
            k.dma("sync", xt2[:, buf, c, :], xn[c, :, sl],
                  waits=[xt_free[buf]])
        ldx = k.qend("sync")
        mm_last = None
        for sub in range(TT // 128):
            s0 = sub * 128
            gsl = slice(it * TT + s0, it * TT + s0 + 128)
            tok = None
            for c in range(NCH):
                tok = k.mm(ps_l[:, 0:8], xt2[:, buf, c, s0:s0 + 128],
                           gw[:, c, :], start=(c == 0), stop=(c == NCH - 1),
                           waits=[ldx, tgw, psl_free])
            mm_last = tok
            lg = r8[:, 0, :]
            nm1 = r8[:, 1, 0:1]
            eq = r8[:, 2, :]
            l2 = r8[:, 3, :]
            nm2 = r8[:, 4, 0:1]
            msk = r8[:, 5, :]
            E = r8[:, 6, :]
            En = r8[:, 7, :]
            den = r8[:, 8, 0:1]
            dnv = r8[:, 9, 0:1]
            a1 = k.dve("tensor_add", lg, ps_l[:, 0:8], gb, waits=[tok, ldx])
            psl_free = a1
            a2 = k.dve("reduce_max", nm1, lg, axis=AX.X, negate=True,
                       waits=[a1])
            a3 = k.dve("tensor_scalar", eq, lg, nm1, 0.0, ALU.add,
                       ALU.is_equal, waits=[a2])
            a4 = k.dve("scalar_tensor_tensor", l2, eq, -1e9, lg, ALU.mult,
                       ALU.add, waits=[a3])
            a5 = k.dve("reduce_max", nm2, l2, axis=AX.X, negate=True,
                       waits=[a4])
            a6 = k.dve("tensor_scalar", msk, lg, nm2, 0.0, ALU.add,
                       ALU.is_ge, waits=[a5])
            a7 = k.act(E, lg, AF.Exp, bias=nm1, waits=[a2])
            a8 = k.dve("tensor_mul", En, E, msk, waits=[a6, a7])
            a9 = k.dve("reduce_sum", den, En, axis=AX.X, waits=[a8])
            a10 = k.dve("reciprocal", dnv, den, waits=[a9])
            a11 = k.dve("tensor_scalar", lg, En, dnv, None, ALU.mult,
                        waits=[a10])
            tp = k.tp(ps_t[0:8, 0:128], lg, g["ident_f32"],
                      waits=[a11, ct, pst_free])
            pst_free = k.dve("tensor_copy", ct8[:], ps_t[0:8, 0:128],
                             waits=[tp, ct8_free])
            k.dma("sync", g["cmb_dram"][0:8, gsl], ct8[:],
                  waits=[pst_free])
            ct8_free = k.qend("sync")
        xt_free[buf] = mm_last
    tcd = ct8_free
    k.dma("gpsimd", cmbTb[:], g["cmb_dram"][:], waits=[tcd])
    tcb = k.qend("gpsimd")

    HHALF, NHC = 2048, 16
    w1b = g["f_w1b"]
    w2b = g["f_w2b"]
    b1b = g["f_b1b"]
    crow = g["f_crow"]
    cbrd = g["f_cbrd"]
    Hg = g["f_Hg"]
    stg = g["f_stg"]
    ps_up = [PS[0], PS[1], PS[2]]
    ps_dn = [PS[3], PS[4], PS[5], PS[6]]
    w_free = None
    b1_free = None
    crow_free = None
    cbrd_free = None
    xt_free = [xt_free[0], xt_free[1]]
    stg_free = [None] * 4
    stg_grp = []
    up_free = [k.fence("vector"), k.fence("vector"), k.fence("vector")]
    dn_free = [k.fence("vector")] * 4
    hg_free = None
    cb_tok = None
    n_acc = 0
    istg = 0
    for e in range(9):
        shared = (e == 8)
        if not shared:
            k.dma("gpsimd", crow[:], g["cmb_dram"][e:e + 1, :],
                  waits=[tcd, crow_free])
            tcr = k.qend("gpsimd")
            cbs = [None]
            for it in range(nt):
                sl = slice(it * TT, (it + 1) * TT)
                tpb = k.mm(PS[2], g["ones_row"], crow[0:1, sl], start=True,
                           stop=True,
                           waits=[tcr, ct, cbrd_free, cbs[-1],
                                  k.fence("scalar"), k.fence("vector"),
                                  up_free[2]])
                cbs.append(k.dve("tensor_copy", cbrd[:, sl], PS[2],
                                 waits=[tpb]))
            cb_tok = cbs[-1]
            up_free[2] = cb_tok
            crow_free = k.fence("tensor")
        for half in range(2):
            h0 = half * HHALF
            for dc in range(NCH):
                srcw = (io["ws1"] if shared else io["w1"][e]).rearrange(
                    "(c p) h -> p c h", p=128)[:, dc, h0:h0 + HHALF]
                k.dma("gpsimd", w1b[:, dc, :], srcw, waits=[w_free])
            for hc in range(NHC):
                srcw = (io["ws2"] if shared else io["w2"][e]).rearrange(
                    "(c p) d -> p c d", p=128)[:, h0 // 128 + hc, :]
                k.dma("gpsimd", w2b[:, hc, :], srcw, waits=[w_free])
            w_tok = k.qend("gpsimd")
            bsrc = (io["bs1_r"] if shared else io["b1_r"][e])
            k.dma("sync", b1b[:], bsrc[:, half * NHC:(half + 1) * NHC],
                  waits=[b1_free])
            first = (e == 0 and half == 0)
            for it in range(nt):
                sl = slice(it * TT, (it + 1) * TT)
                buf = it % 2
                for c in range(NCH):
                    k.dma("sync", xt2[:, buf, c, :], xn[c, :, sl],
                          waits=[xt_free[buf]])
                ldx = k.qend("sync")
                gel = []
                for hc in range(NHC):
                    pu = ps_up[hc % 3]
                    tok = None
                    for dc in range(NCH):
                        tok = k.mm(pu, w1b[:, dc, hc * 128:hc * 128 + 128],
                                   xt2[:, buf, dc, :],
                                   start=(dc == 0), stop=(dc == NCH - 1),
                                   waits=[w_tok, ldx, up_free[hc % 3]])
                    gel.append(k.act(Hg[:, hc, :], pu, AF.Gelu,
                                     bias=b1b[:, hc:hc + 1],
                                     waits=[tok, ldx, hg_free]))
                    up_free[hc % 3] = gel[-1]
                for dp in range(2):
                    tok = None
                    for hc in range(NHC):
                        for dl in range(4):
                            dc = dp * 4 + dl
                            tok = k.mm(ps_dn[dl],
                                       w2b[:, hc, dc * 128:dc * 128 + 128],
                                       Hg[:, hc, :], start=(hc == 0),
                                       stop=(hc == NHC - 1),
                                       waits=[w_tok, gel[hc], dn_free[dl]],
                                       inc=(dl == 3))
                    slot = istg % 4
                    istg += 1
                    t1 = None
                    for dl in range(4):
                        if shared:
                            t1 = k.dve("tensor_copy", stg[:, slot, dl, :],
                                       ps_dn[dl],
                                       waits=[tok, stg_free[slot]])
                        else:
                            t1 = k.dve("tensor_mul", stg[:, slot, dl, :],
                                       ps_dn[dl], cbrd[:, sl],
                                       waits=[tok, cb_tok,
                                              stg_free[slot]])
                        dn_free[dl] = t1
                    dst = acc[dp * 4:dp * 4 + 4, :, sl].rearrange(
                        "c p t -> p c t")
                    k.dma("gpsimd", dst, stg[:, slot, :, :], waits=[t1],
                          accum_op=(ALU.bypass if first else ALU.add))
                    n_acc += 1
                    if n_acc % 2 == 0:
                        gtok = k.qend("gpsimd")
                        stg_grp.append(gtok)
                        if len(stg_grp) >= 2:
                            for j in range(2):
                                stg_free[(istg + j) % 4] = stg_grp[-2]
                hg_free = k.fence("tensor")
                xt_free[buf] = k.fence("tensor")
            w_free = k.fence("tensor")
            b1_free = k.fence("scalar")
        if not shared:
            cbrd_free = k.fence("vector")

    k.dma("sync", g["f_bs2"][:], io["bs2_fm"][:])
    g["bs2_tok"] = k.qend("sync")
    b2s = g["f_b2s"]
    k.dma("gpsimd", b2s[:], io["b2"][:])
    tb2 = k.qend("gpsimd")
    pe_b = k.fence("vector")
    ps2_free = k.fence("vector")
    for it in range(nt):
        sl = slice(it * TT, (it + 1) * TT)
        for dc in range(NCH):
            slot = istg % 4
            istg += 1
            tok = k.mm(PS[2], b2s[0:8, dc * 128:dc * 128 + 128],
                       cmbTb[0:8, sl], start=True, stop=True,
                       waits=[tb2, tcb, pe_b, ps2_free])
            t1 = k.dve("tensor_scalar_add", stg[:, slot, 0, :], PS[2],
                       g["f_bs2"][:, dc:dc + 1],
                       waits=[tok, stg_free[slot], g["bs2_tok"]])
            ps2_free = Tok("vector", k.counts["vector"])
            k.dma("gpsimd", acc[dc, :, sl], stg[:, slot, 0, :],
                  waits=[t1], accum_op=ALU.add)
            n_acc += 1
            if n_acc % 2 == 0:
                gtok = k.qend("gpsimd")
                stg_grp.append(gtok)
                if len(stg_grp) >= 2:
                    for j in range(2):
                        stg_free[(istg + j) % 4] = stg_grp[-2]

    gp_done = k.qend("gpsimd")
    res_v = k.fence("vector")
    res_t = k.fence("tensor")
    xf = g["ln_xf"]
    accb = g["f_accb"]
    prev = None
    outs = []
    for it in range(nt):
        sl = slice(it * TT, (it + 1) * TT)
        for c in range(NCH):
            k.dma("sync", xf[:, 0, c, :], io["x_in"][c, :, sl],
                  waits=[res_v, res_t, gp_done, prev])
            k.dma("sync", accb[:, c, :], acc[c, :, sl])
        lt = k.qend("sync")
        ts = []
        for c in range(NCH):
            t = k.dve("scalar_tensor_tensor", xf[:, 1, c, :],
                      accb[:, c, :], 0.5, xf[:, 0, c, :], ALU.mult,
                      ALU.add, waits=[lt])
            ts.append(t)
            k.dma("sync", io["x_out"][c, :, sl], xf[:, 1, c, :],
                  waits=[t])
        outs.append(k.qend("sync"))
        prev = [outs[-1]] + ts
    return outs



def emit_ffn0(k, g, io, T):
    nt = T // TT
    xn = g["xn_dram"]
    acc = g["acc_dram"]
    PS = g["PS"]
    ct = g["const_tok"]
    var_sb = g["f0_var"]

    ln_done = emit_layernorm(k, g, io["x_in"], xn, T, var_sb=var_sb)

    rrow = g["f0_rs"]
    vtok = k.fence("vector")
    cc6, tk6 = g["cc"][1e-6]
    rcol = g["f0_rcol"]
    ps_t = PS[5]
    rt_free = None
    rs_tok = None
    r6p = None
    for tt8 in range(nt):
        sl8 = slice(tt8 * TT, (tt8 + 1) * TT)
        r1 = k.dve("tensor_single_scalar", rrow[0:1, 0, :],
                   var_sb[0:1, sl8], 1e-5, ALU.add, waits=[vtok, r6p])
        r2 = k.dve("reciprocal", rrow[0:1, 1, :], rrow[0:1, 0, :],
                   waits=[r1])
        r3 = k.dve("tensor_mul", rrow[0:1, 2, :], var_sb[0:1, sl8],
                   rrow[0:1, 1, :], waits=[r2])
        r4 = k.act(rrow[0:1, 3, :], rrow[0:1, 2, :], AF.Sqrt,
                   bias=cc6[0:1, :], waits=[r3, tk6, r6p])
        r5 = k.dve("reciprocal", rrow[0:1, 0, :], rrow[0:1, 3, :],
                   waits=[r4])
        r6 = k.dve("tensor_single_scalar", rrow[0:1, 1, :],
                   rrow[0:1, 0, :], 1.0 / 32.0, ALU.mult, waits=[r5])
        r6p = r6
        for sub in range(TT // 128):
            it = tt8 * (TT // 128) + sub
            tp = k.tp(ps_t[0:128, 0:1],
                      rrow[0:1, 1, sub * 128:(sub + 1) * 128],
                      g["ident_f32"][0:1, 0:1], waits=[r6, ct, rt_free])
            rs_tok = k.dve("tensor_copy", rcol[:, it:it + 1],
                           ps_t[0:128, 0:1], waits=[tp])
            rt_free = rs_tok
            r6p = [r6p, rt_free]
    gw = g["f_gw"]
    gwf = g["f0_gwf"]
    k.dma("sync", gwf[:], io["router_w"].rearrange("(c p) e -> p c e", p=128))
    k.dma("sync", g["f0_rsc"][:], io["router_scale_fm"][:])
    trw = k.qend("sync")
    wp = []
    for c in range(NCH):
        wp.append(k.dve("tensor_scalar", gw[:, c, :], gwf[:, c, :],
                        g["f0_rsc"][:, c:c + 1], None, ALU.mult,
                        waits=[trw]))
    tgw = wp[-1]
    pesb = g["f0_pesb"]
    k.dma("gpsimd", g["f0_pes"][:], io["pes_row"][:])
    tpes0 = k.qend("gpsimd")
    tpb = k.mm(PS[2][:, 0:8], g["ones_row"], g["f0_pes"][0:1, :],
               start=True, stop=True, waits=[tpes0, ct, k.fence("vector")])
    tpes = k.dve("tensor_copy", pesb[:], PS[2][:, 0:8], waits=[tpb])

    ps_l = PS[4]
    r8 = g["f_r8"]
    ct8 = g["f_ct8"]
    ct8_free = None
    xt2 = g["f_xt"]
    xt_free = [None, None]
    psl_free = None
    pst_free = rs_tok
    for it in range(nt):
        buf = it % 2
        sl = slice(it * TT, (it + 1) * TT)
        for c in range(NCH):
            k.dma("sync", xt2[:, buf, c, :], xn[c, :, sl],
                  waits=[xt_free[buf], ln_done])
        ldx = k.qend("sync")
        mm_last = None
        for sub in range(TT // 128):
            s0 = sub * 128
            i128 = it * (TT // 128) + sub
            gsl = slice(i128 * 128, i128 * 128 + 128)
            tok = None
            for c in range(NCH):
                tok = k.mm(ps_l[:, 0:8], xt2[:, buf, c, s0:s0 + 128],
                           gw[:, c, :], start=(c == 0), stop=(c == NCH - 1),
                           waits=[ldx, tgw, psl_free])
            mm_last = tok
            lg = r8[:, 0, :]
            nm1 = r8[:, 1, 0:1]
            eq = r8[:, 2, :]
            l2 = r8[:, 3, :]
            nm2 = r8[:, 4, 0:1]
            msk = r8[:, 5, :]
            E = r8[:, 6, :]
            En = r8[:, 7, :]
            den = r8[:, 8, 0:1]
            dnv = r8[:, 9, 0:1]
            a1 = k.dve("tensor_scalar", lg, ps_l[:, 0:8],
                       rcol[:, i128:i128 + 1], None, ALU.mult,
                       waits=[tok, rs_tok])
            psl_free = a1
            a2 = k.dve("reduce_max", nm1, lg, axis=AX.X, negate=True,
                       waits=[a1])
            a3 = k.dve("tensor_scalar", eq, lg, nm1, 0.0, ALU.add,
                       ALU.is_equal, waits=[a2])
            a4 = k.dve("scalar_tensor_tensor", l2, eq, -1e9, lg, ALU.mult,
                       ALU.add, waits=[a3])
            a5 = k.dve("reduce_max", nm2, l2, axis=AX.X, negate=True,
                       waits=[a4])
            a6 = k.dve("tensor_scalar", msk, lg, nm2, 0.0, ALU.add,
                       ALU.is_ge, waits=[a5])
            a7 = k.act(E, lg, AF.Exp, bias=nm1, waits=[a2])
            a8 = k.dve("tensor_mul", En, E, msk, waits=[a6, a7])
            a9 = k.dve("reduce_sum", den, En, axis=AX.X, waits=[a8])
            a10 = k.dve("reciprocal", dnv, den, waits=[a9])
            a11 = k.dve("tensor_scalar", lg, En, dnv, None, ALU.mult,
                        waits=[a10])
            tp = k.tp(ps_t[0:8, 0:128], lg, g["ident_f32"],
                      waits=[a11, ct, pst_free])
            pst_free = k.dve("tensor_copy", ct8[:], ps_t[0:8, 0:128],
                             waits=[tp, ct8_free])
            k.dma("sync", g["cmb_dram"][0:8, gsl], ct8[:],
                  waits=[pst_free])
            ct8_free = k.qend("sync")
        xt_free[buf] = mm_last
    tcd = ct8_free

    HHALF, NHC = 1024, 8
    wg0 = g["f0_wg0"]
    wg1 = g["f0_wg1"]
    w2b = g["f_w2b"]
    crow = g["f_crow"]
    cbrd = g["f_cbrd"]
    Hg = g["f_Hg"]
    gt = g["f0_gt"]
    tmp = g["f_tmp"]
    NTMP = g["ntmp"]
    ps_g0, ps_g1 = PS[0], PS[1]
    ps_dn = [PS[3], PS[4], PS[5], PS[6]]
    w_free = None
    crow_free = None
    cbrd_free = k.fence("vector")
    tmp_free = [None] * NTMP
    tmp_grp = []
    g0_free = k.fence("vector")
    g1_free = k.fence("vector")
    dn_free = [k.fence("vector")] * 4
    hg_free = None
    gt_free = None
    cb_tok = None
    n_acc = 0
    itmp = 0
    for e in range(8):
        k.dma("gpsimd", crow[:], g["cmb_dram"][e:e + 1, :],
              waits=[tcd, crow_free])
        tcr = k.qend("gpsimd")
        cbs = [None]
        for it in range(nt):
            sl = slice(it * TT, (it + 1) * TT)
            tpb2 = k.mm(PS[2], g["ones_row"], crow[0:1, sl], start=True,
                        stop=True,
                        waits=[tcr, ct, cbrd_free, tpes, cbs[-1]])
            cbs.append(k.dve("tensor_scalar", cbrd[:, sl], PS[2],
                             pesb[:, e:e + 1], None, ALU.mult,
                             waits=[tpb2]))
        cb_tok = cbs[-1]
        crow_free = k.fence("tensor")
        for half in range(2):
            h0 = half * HHALF
            for dc in range(NCH):
                k.dma("gpsimd", wg0[:, dc, :],
                      io["gating_wT"][e, 0].rearrange(
                          "(c p) h -> p c h", p=128)[:, dc, h0:h0 + HHALF],
                      waits=[w_free])
                k.dma("gpsimd", wg1[:, dc, :],
                      io["gating_wT"][e, 1].rearrange(
                          "(c p) h -> p c h", p=128)[:, dc, h0:h0 + HHALF],
                      waits=[w_free])
            for hc in range(NHC):
                k.dma("gpsimd", w2b[:, hc, :],
                      io["linear_w"][e].rearrange(
                          "(c p) d -> p c d", p=128)[:, h0 // 128 + hc, :],
                      waits=[w_free])
            w_tok = k.qend("gpsimd")
            for it in range(nt):
                sl = slice(it * TT, (it + 1) * TT)
                buf = it % 2
                for c in range(NCH):
                    k.dma("sync", xt2[:, buf, c, :], xn[c, :, sl],
                          waits=[xt_free[buf]])
                ldx = k.qend("sync")
                gel = []
                for hc in range(NHC):
                    tok0 = tok1 = None
                    for dc in range(NCH):
                        tok0 = k.mm(ps_g0,
                                    wg0[:, dc, hc * 128:hc * 128 + 128],
                                    xt2[:, buf, dc, :],
                                    start=(dc == 0), stop=(dc == NCH - 1),
                                    waits=[w_tok, ldx, g0_free])
                        tok1 = k.mm(ps_g1,
                                    wg1[:, dc, hc * 128:hc * 128 + 128],
                                    xt2[:, buf, dc, :],
                                    start=(dc == 0), stop=(dc == NCH - 1),
                                    waits=[w_tok, ldx, g1_free])
                    tg = k.act(gt[:], ps_g0, AF.Gelu_apprx_tanh,
                               waits=[tok0, gt_free])
                    g0_free = tg
                    th = k.dve("tensor_mul", Hg[:, hc, :], gt[:], ps_g1,
                               waits=[tg, tok1, hg_free])
                    gt_free = th
                    g1_free = th
                    gel.append(th)
                for dp in range(2):
                    tok = None
                    for hc in range(NHC):
                        for dl in range(4):
                            dc = dp * 4 + dl
                            tok = k.mm(ps_dn[dl],
                                       w2b[:, hc, dc * 128:dc * 128 + 128],
                                       Hg[:, hc, :], start=(hc == 0),
                                       stop=(hc == NHC - 1),
                                       waits=[w_tok, gel[hc], dn_free[dl]],
                                       inc=(dl == 3))
                    for dl in range(4):
                        dc = dp * 4 + dl
                        tb = itmp % NTMP
                        itmp += 1
                        t1 = k.dve("tensor_mul", tmp[:, tb, :], ps_dn[dl],
                                   cbrd[:, sl],
                                   waits=[tok, cb_tok, tmp_free[tb]])
                        dn_free[dl] = t1
                        first = (e == 0 and half == 0)
                        k.dma("gpsimd", acc[dc, :, sl], tmp[:, tb, :],
                              waits=[t1],
                              accum_op=(ALU.bypass if first else ALU.add))
                        n_acc += 1
                        if n_acc % (NTMP // 2) == 0:
                            gtok = k.qend("gpsimd")
                            tmp_grp.append(gtok)
                            if len(tmp_grp) >= 2:
                                for j in range(NTMP // 2):
                                    tmp_free[(itmp + j) % NTMP] = tmp_grp[-2]
                hg_free = k.fence("tensor")
                xt_free[buf] = k.fence("tensor")
            w_free = k.fence("tensor")
        cbrd_free = k.fence("vector")

    gp_done = k.qend("gpsimd")
    res_v = k.fence("vector")
    res_t = k.fence("tensor")
    xf = g["ln_xf"]
    accb = g["f_accb"]
    prev = None
    outs = []
    for it in range(nt):
        sl = slice(it * TT, (it + 1) * TT)
        for c in range(NCH):
            k.dma("sync", xf[:, 0, c, :], io["x_in"][c, :, sl],
                  waits=[res_v, res_t, gp_done, prev])
            k.dma("sync", accb[:, c, :], acc[c, :, sl])
        lt = k.qend("sync")
        ts = []
        for c in range(NCH):
            t = k.dve("scalar_tensor_tensor", xf[:, 1, c, :],
                      accb[:, c, :], 0.5, xf[:, 0, c, :], ALU.mult,
                      ALU.add, waits=[lt])
            ts.append(t)
            k.dma("sync", io["x_out"][c, :, sl], xf[:, 1, c, :],
                  waits=[t])
        outs.append(k.qend("sync"))
        prev = [outs[-1]] + ts
    return outs


def build_ffn0_program(T):
    nc = bass.Bass("TRN2", target_bir_lowering=False)
    io = {}
    io["x_in"] = nc.dram_tensor("x_in", [NCH, 128, T], F32,
                                kind="ExternalInput").ap()
    io["x_out"] = nc.dram_tensor("x_out", [NCH, 128, T], F32,
                                 kind="ExternalOutput").ap()
    for n, shp in [("router_w", [D, 8]), ("router_scale_fm", [128, 8]),
                   ("gating_wT", [8, 2, D, 2048]),
                   ("linear_w", [8, 2048, D]), ("pes_row", [1, 8])]:
        io[n] = nc.dram_tensor(n, shp, F32, kind="ExternalInput").ap()
    declare_consts(nc, io)

    g = alloc_common(nc, T)
    g["xn_dram"] = nc.dram_tensor("xn_scr", [NCH, 128, T], BF16).ap()
    g["acc_dram"] = nc.dram_tensor("acc_scr", [NCH, 128, T], BF16).ap()
    g["cmb_dram"] = nc.dram_tensor("cmb_scr", [8, T], F32).ap()
    g["f_gw"] = nc.alloc_sbuf_tensor("f_gw", [128, NCH, 8], BF16).ap()
    g["f0_gwf"] = nc.alloc_sbuf_tensor("f0_gwf", [128, NCH, 8], F32).ap()
    g["f0_rsc"] = nc.alloc_sbuf_tensor("f0_rsc", [128, 8], F32).ap()
    g["f0_var"] = nc.alloc_sbuf_tensor("f0_var", [1, T], F32).ap()
    g["f0_rs"] = nc.alloc_sbuf_tensor("f0_rs", [1, 4, TT], F32).ap()
    g["f0_rcol"] = nc.alloc_sbuf_tensor("f0_rcol", [128, T // 128], F32).ap()
    g["f0_pes"] = nc.alloc_sbuf_tensor("f0_pes", [1, 8], BF16).ap()
    g["f0_pesb"] = nc.alloc_sbuf_tensor("f0_pesb", [128, 8], F32).ap()
    g["f_r8"] = nc.alloc_sbuf_tensor("f_r8", [128, 10, 8], F32).ap()
    g["f_ct8"] = nc.alloc_sbuf_tensor("f_ct8", [8, 128], F32).ap()
    g["f_xt"] = nc.alloc_sbuf_tensor("f_xt", [128, 2, NCH, TT], BF16).ap()
    g["f0_wg0"] = nc.alloc_sbuf_tensor("f0_wg0", [128, NCH, 1024], BF16).ap()
    g["f0_wg1"] = nc.alloc_sbuf_tensor("f0_wg1", [128, NCH, 1024], BF16).ap()
    g["f_w2b"] = nc.alloc_sbuf_tensor("f_w2b", [128, 8, D], BF16).ap()
    g["f_crow"] = nc.alloc_sbuf_tensor("f_crow", [1, T], BF16).ap()
    g["f_cbrd"] = nc.alloc_sbuf_tensor("f_cbrd", [128, T], BF16).ap()
    g["f_Hg"] = nc.alloc_sbuf_tensor("f_Hg", [128, 8, TT], BF16).ap()
    g["f0_gt"] = nc.alloc_sbuf_tensor("f0_gt", [128, TT], BF16).ap()
    g["ntmp"] = 8
    g["f_tmp"] = nc.alloc_sbuf_tensor("f_tmp", [128, 8, TT], BF16).ap()
    g["f_accb"] = nc.alloc_sbuf_tensor("f_accb", [128, NCH, TT], BF16).ap()

    k = KB(nc)
    k.io = io

    def body(name):
        load_consts(k, g)
        outs = emit_ffn0(k, g, io, T)
        finish_program(k, outs)

    build_block(nc, k, body)
    return nc


def prep_ffn0_inputs(xT, W, T):
    d = {
        "x_in": np.ascontiguousarray(xT.reshape(NCH, 128, T)),
        "router_w": np.ascontiguousarray(W["router_w"]),
        "router_scale_fm": fm(W["router_scale"]),
        "gating_wT": np.ascontiguousarray(
            W["gating_w"].transpose(0, 1, 3, 2)),
        "linear_w": np.ascontiguousarray(W["linear_w"]),
        "pes_row": np.ascontiguousarray(
            W["per_expert_scale"].reshape(1, 8)),
    }
    d.update(const_inputs())
    return d



def emit_rmsnorm_tile(k, g, A_dram, tt, w_col_ap, wtok, xg, eps=1e-6):
    sl = slice(tt * TT, (tt + 1) * TT)
    PS = g["PS"]
    at = g["a_tile"]
    sqb = g["a_sq"]
    row = g["ln_rows"]
    rowb = g["ln_rowb"]
    ct = g["const_tok"]
    for c in range(NCH):
        k.dma("sync", at[:, c, :], A_dram[c, :, sl], waits=[g["at_free"]])
    ld = k.qend("sync")
    t_sq = [k.act(sqb[:, c, :], at[:, c, :], AF.Square,
                  waits=[ld, g["asq_free"]]) for c in range(NCH)]
    tok = None
    for c in range(NCH):
        tok = k.mm(PS[6][0:1, :], g["ones_col"], sqb[:, c, :],
                   start=(c == 0), stop=(c == NCH - 1),
                   waits=[t_sq[c], ct, g["rms_ps_free"]])
    eps_ap, eps_tok = g["cc"][float(eps)]
    sd = row[0:1, 4, :]
    rr = row[0:1, 5, :]
    t5 = k.act(sd, PS[6][0:1, :], AF.Sqrt, bias=eps_ap[0:1, :],
               scale=1.0 / D, waits=[tok, eps_tok, g["row_free"]])
    g["rms_ps_free"] = t5
    t6 = k.dve("reciprocal", rr, sd, waits=[t5])
    t8 = k.dve("tensor_copy", rowb[0:1, 0, :], rr, waits=[t6])
    tb1 = k.mm(PS[6], g["ones_row"], rowb[0:1, 0, :], start=True,
               stop=True, waits=[t8, t5, ct, g["brd_free"]])
    g["row_free"] = tb1
    xg_toks = []
    for c in range(NCH):
        xg_toks.append(k.dve("scalar_tensor_tensor", xg[:, c, :],
                             at[:, c, :], w_col_ap[:, c:c + 1], PS[6],
                             ALU.mult, ALU.mult,
                             waits=[tb1, ld, wtok, g["xg_free"]]))
    g["brd_free"] = xg_toks[-1]
    g["rms_ps_free"] = xg_toks[-1]
    g["at_free"] = xg_toks[-1]
    g["asq_free"] = tok
    return xg_toks


def emit_proj_rope(k, g, xg, xg_toks, tt, wbuf, wtok, n_in_ch, out_spec,
                   out_chunks, do_rope, ps_a, ps_b):
    sl = slice(tt * TT, (tt + 1) * TT)
    PS = g["PS"]
    ct = g["const_tok"]
    kind, dst = out_spec
    toks = []
    for oc in range(out_chunks):
        pq = ps_a[oc % 2]
        tok = None
        for c in range(n_in_ch):
            tok = k.mm(pq, wbuf[:, c, oc * 128:oc * 128 + 128],
                       xg[:, c, :], start=(c == 0), stop=(c == n_in_ch - 1),
                       waits=[wtok, xg_toks[min(c, len(xg_toks) - 1)],
                              g["pq_free"][oc % 2]])
        if do_rope:
            qraw = g["qraw"]
            t0 = k.act(qraw[:], pq, AF.Copy, waits=[tok, g["qraw_free"]])
            sqt = g["sqt"]
            t1 = k.act(sqt[:], pq, AF.Square, waits=[tok, g["sqt_free"]])
            tn = k.mm(ps_b[0][0:2, :], g["ind2"], sqt[:], start=True,
                      stop=True, waits=[t1, ct, g["psn_free"]])
            cc6, tk6 = g["cc"][1e-6]
            sd2 = g["row2"][0:2, 0, :]
            rr2 = g["row2"][0:2, 1, :]
            rb2 = g["row2b"][0:2, 0, :]
            t2 = k.act(sd2, ps_b[0][0:2, :], AF.Sqrt, bias=cc6[0:2, :],
                       scale=1.0 / 64, waits=[tn, tk6, g["row2_free"]])
            g["psn_free"] = t2
            t3 = k.dve("reciprocal", rr2, sd2, waits=[t2])
            t4 = k.dve("tensor_copy", rb2, rr2, waits=[t3])
            tb = k.mm(ps_b[1], g["ind2T"], rb2, start=True, stop=True,
                      waits=[t4, ct, g["psb_free"]])
            g["row2_free"] = tb
            trot = k.mm(ps_b[2], g["rotM"], qraw[:], start=True, stop=True,
                        waits=[t0, ct, g["rot_free"]])
            rp1 = g["rp1"]
            rp2 = g["rp2"]
            t5 = k.dve("tensor_mul", rp1[:], qraw[:], g["cos_t"][:],
                       waits=[t0, g["tab_tok"], g["rp1_free"]])
            g["qraw_free"] = [trot, t5]
            t7 = k.dve("tensor_mul", rp2[:], ps_b[2], g["sin_t"][:],
                       waits=[trot, g["tab_tok"], g["rp2_free"]])
            g["rot_free"] = t7
            t8 = k.dve("tensor_add", rp1[:], rp1[:], rp2[:],
                       waits=[t5, t7])
            g["rp2_free"] = t8
            g["sqt_free"] = tn
            g["pq_free"][oc % 2] = [t0, t1]
            if kind == "T":
                t9 = k.dve("tensor_mul", dst[:, oc, sl], rp1[:], ps_b[1],
                           waits=[t8, tb, g["osb_free"]])
                toks.append(t9)
            else:
                stb = oc % 2
                t9 = k.dve("tensor_mul", g["qst"][:, stb, :], rp1[:],
                           ps_b[1], waits=[t8, tb, g["qst_free"][stb]])
                k.dma("sync", dst[oc, :, sl], g["qst"][:, stb, :],
                      waits=[t9])
                toks.append(k.qend("sync"))
                g["qst_free"][stb] = toks[-1]
            g["rp1_free"] = t9
            g["psb_free"] = t9
        else:
            if kind == "T":
                t = k.act(dst[:, oc, sl], pq, AF.Copy,
                          waits=[tok, g["osb_free"]])
            else:
                t = k.act(dst[:, oc, :], pq, AF.Copy,
                          waits=[tok, g["osb_free"]])
            g["pq_free"][oc % 2] = t
            toks.append(t)
    return toks


def emit_attn_core(k, g, q_tok, k_sb, vT_sb, o_col, head, qt, kc_lo,
                   kc_hi, mask_of, T):
    PS = g["PS"]
    ct = g["const_tok"]
    kvh = head // 4
    qrow = (head % 2) * 64
    qch = head // 2
    krow = (kvh % 2) * 64
    kch = kvh // 2
    Et = g["E_t"]
    qtile = g["q_tile"]
    tok_o = None
    for i, kc in enumerate(range(kc_lo, kc_hi + 1)):
        eb = i % 2
        ts = k.mm(PS[0], k_sb[krow:krow + 64, kch,
                              kc * 128:kc * 128 + 128],
                  qtile[krow:krow + 64, :], start=True, stop=True,
                  waits=[g["qk_done"], q_tok, g["pss_free"][0]])
        cc8, tk8 = g["cc"][-8.0]
        te = k.act(Et[:, eb, :], PS[0], AF.Exp, bias=cc8[:],
                   scale=0.125, waits=[ts, tk8, g["et_free"][eb]])
        g["pss_free"][0] = te
        mi = mask_of(kc, qt)
        if mi is not None:
            te = k.dve("tensor_mul", Et[:, eb, :], Et[:, eb, :],
                       g["amask"][:, mi, :], waits=[te, ct])
        tok_o = k.mm(PS[1][0:65, :], vT_sb[:, kc, kvh, :], Et[:, eb, :],
                     start=(i == 0), stop=(kc == kc_hi),
                     waits=[te, g["vT_done"], g["pso_free"]])
        g["et_free"][eb] = tok_o
    dv = g["row2"][0:1, 2, :]
    dvb = g["row2b"][0:1, 1, :]
    t1 = k.dve("reciprocal", dv, PS[1][64:65, :],
               waits=[tok_o, g["dv_free"]])
    t2 = k.dve("tensor_copy", dvb, dv, waits=[t1])
    tb = k.mm(PS[2][0:64, :], g["ones_row"][0:1, 0:64], dvb, start=True,
              stop=True, waits=[t2, ct, g["psb2_free"]])
    g["dv_free"] = tb
    bs = g["brd_sb"]
    t3 = k.act(bs[:], PS[2][0:64, :], AF.Copy, waits=[tb, g["bs_free"]])
    g["psb2_free"] = t3
    t4 = k.dve("tensor_mul", o_col[qrow:qrow + 64, qch, :], PS[1][0:64, :],
               bs[:], waits=[t3, tok_o, g["ocol_free"]])
    g["pso_free"] = t4
    g["bs_free"] = t4
    return t4


def emit_attn(k, g, io, T):
    nt = T // TT
    A = g["xn_dram"]
    PS = g["PS"]
    ct = g["const_tok"]

    ln_done = emit_layernorm(k, g, io["x_in"], A, T)

    k.dma("sync", g["norm1"][:], io["norm1_fm"][:])
    k.dma("sync", g["norm2"][:], io["norm2_fm"][:])
    k.dma("sync", g["ind2"][:], io["c_ind2"][:])
    k.dma("sync", g["ind2T"][:], io["c_ind2T"][:])
    k.dma("sync", g["rotM"][:], io["c_rot"][:])
    k.dma("sync", g["amask"][:], io["c_amask"][:])
    k.dma("sync", g["ident_bf"][:], io["c_ident_bf"][:])
    wtok0 = k.qend("sync")

    def attn_pass(which):
        glob = which == "g"
        if glob:
            k.dma("gpsimd", g["w_q"][:],
                  io["g_wq"].rearrange("(c p) o -> p c o", p=128),
                  waits=[g["w_war"]])
            k.dma("gpsimd", g["w_kvd"][:],
                  io["g_kv_down"].rearrange("(c p) o -> p c o", p=128))
            k.dma("gpsimd", g["w_kup"][:],
                  io["g_k_up"].rearrange("(c p) o -> p c o", p=128))
            k.dma("gpsimd", g["w_vup"][:],
                  io["g_v_up"].rearrange("(c p) o -> p c o", p=128))
            k.dma("gpsimd", g["w_o"][:],
                  io["g_wo"].rearrange("(c p) o -> p c o", p=128))
        else:
            k.dma("gpsimd", g["w_q"][:],
                  io["l_wq"].rearrange("(c p) o -> p c o", p=128),
                  waits=[g["w_war"]])
            k.dma("gpsimd", g["w_kl"][:],
                  io["l_wk"].rearrange("(c p) o -> p c o", p=128))
            k.dma("gpsimd", g["w_vl"][:],
                  io["l_wv"].rearrange("(c p) o -> p c o", p=128))
            k.dma("gpsimd", g["w_o"][:],
                  io["l_wo"].rearrange("(c p) o -> p c o", p=128))
        wtok = k.qend("gpsimd")
        wcol = g["norm1"] if glob else g["norm2"]
        cosd = io["c_cos_g"] if glob else io["c_cos_l"]
        sind = io["c_sin_g"] if glob else io["c_sin_l"]
        for tt in range(nt):
            sl = slice(tt * TT, (tt + 1) * TT)
            xg = g["xg_t"]
            xg_toks = emit_rmsnorm_tile(k, g, A, tt, wcol, wtok0, xg)
            k.dma("sync", g["cos_t"][:], cosd[:, sl],
                  waits=[g["tab_war"]])
            k.dma("sync", g["sin_t"][:], sind[:, sl])
            g["tab_tok"] = k.qend("sync")
            emit_proj_rope(k, g, xg, xg_toks, tt, g["w_q"], wtok, NCH,
                           ("dram", g["q_dram"]), NCH, True,
                           [PS[0], PS[1]], [PS[2], PS[3], PS[4]])
            if glob:
                lt = emit_proj_rope(k, g, xg, xg_toks, tt, g["w_kvd"],
                                    wtok, NCH, ("tile", g["lat_t"]), 2,
                                    False, [PS[0], PS[1]], None)
                emit_proj_rope(k, g, g["lat_t"], lt, tt, g["w_kup"],
                               wtok, 2, ("T", g["k_sb"]), 2, True,
                               [PS[0], PS[1]], [PS[2], PS[3], PS[4]])
                vt = emit_proj_rope(k, g, g["lat_t"], lt, tt,
                                    g["w_vup"], wtok, 2,
                                    ("tile", g["v_t"]), 2, False,
                                    [PS[0], PS[1]], None)
            else:
                emit_proj_rope(k, g, xg, xg_toks, tt, g["w_kl"], wtok,
                               NCH, ("T", g["k_sb"]), 2, True,
                               [PS[0], PS[1]], [PS[2], PS[3], PS[4]])
                vt = emit_proj_rope(k, g, xg, xg_toks, tt, g["w_vl"],
                                    wtok, NCH, ("tile", g["v_t"]), 2,
                                    False, [PS[0], PS[1]], None)
            g["xg_free"] = k.fence("tensor")
            g["tab_war"] = k.fence("vector")
            for tc in range(TT // 128):
                kc = tt * (TT // 128) + tc
                for kvh in range(4):
                    vrow = (kvh % 2) * 64
                    vch = kvh // 2
                    tpv = k.tp(g["PSB"][:, 0:64],
                               g["v_t"][vrow:vrow + 64, vch,
                                        tc * 128:tc * 128 + 128],
                               g["ident_bf"][vrow:vrow + 64,
                                              vrow:vrow + 64],
                               waits=[vt[vch], wtok0, g["psv_free"]])
                    g["psv_free"] = k.dve(
                        "tensor_copy", g["vT_sb"][:, kc, kvh, 0:64],
                        g["PSB"][:, 0:64], waits=[tpv, g["vT_war"]])
            g["osb_free"] = k.fence("tensor")
        k.pool("memset", g["vT_sb"][:, :, :, 64:65], 1.0,
               waits=[k.fence("vector"), k.fence("tensor")])
        g["vT_done"] = k.fence("gpsimd")
        qk_done = [k.fence("vector"), k.fence("scalar")]
        g["qk_done"] = qk_done
        g["pss_free"][0] = qk_done
        g["pso_free"] = qk_done
        g["dv_free"] = qk_done
        g["psb2_free"] = qk_done
        for qt in range(nt):
            for head in range(16):
                qch = head // 2
                qrow = (head % 2) * 64
                kvh_ = head // 4
                kr_ = (kvh_ % 2) * 64
                k.dma("sync", g["q_tile"][kr_:kr_ + 64, :],
                      g["q_dram"][qch, qrow:qrow + 64,
                                  qt * TT:(qt + 1) * TT],
                      waits=[g["qt_war"]])
                q_tok = k.qend("sync")
                if glob:
                    kc_lo, kc_hi = 0, qt * (TT // 128) + 3

                    def mask_of(kc, qt_):
                        j = kc - qt_ * 4
                        return j if 0 <= j <= 3 else None
                else:
                    kc_lo, kc_hi = max(0, qt * 4 - 2), qt * 4 + 3

                    def mask_of(kc, qt_):
                        return 4 + (kc - qt_ * 4 + 2)
                emit_attn_core(k, g, q_tok, g["k_sb"], g["vT_sb"],
                               g["o_col"], head, qt, kc_lo, kc_hi,
                               mask_of, T)
                g["qt_war"] = k.fence("tensor")
            ocol_done = k.fence("vector")
            sl = slice(qt * TT, (qt + 1) * TT)
            for oc in range(NCH):
                pd = PS[5 + oc % 2]
                tok = None
                for c in range(NCH):
                    tok = k.mm(pd, g["w_o"][:, c, oc * 128:oc * 128 + 128],
                               g["o_col"][:, c, :], start=(c == 0),
                               stop=(c == NCH - 1),
                               waits=[wtok, ocol_done,
                                      g["wo_free"][oc % 2]])
                tb = g["acc_cnt"] % g["ntmp"]
                t1 = k.dve("tensor_copy", g["f_tmp"][:, tb, :], pd,
                           waits=[tok, g["tmp_free"][tb]])
                g["wo_free"][oc % 2] = t1
                k.dma("gpsimd", A[oc, :, sl], g["f_tmp"][:, tb, :],
                      waits=[t1], accum_op=ALU.add)
                g["acc_cnt"] += 1
                if g["acc_cnt"] % 2 == 0:
                    gtok = k.qend("gpsimd")
                    g["tmp_grp"].append(gtok)
                    if len(g["tmp_grp"]) >= 2:
                        for j in range(2):
                            g["tmp_free"][(g["acc_cnt"] + j) %
                                          g["ntmp"]] = g["tmp_grp"][-2]
            g["ocol_free"] = k.fence("tensor")
        gp = k.qend("gpsimd")
        g["at_free"] = [gp, g["at_free"]]
        g["rms_ps_free"] = k.fence("vector")
        g["row_free"] = k.fence("vector")
        g["brd_free"] = k.fence("vector")
        g["w_war"] = k.fence("tensor")
        g["vT_war"] = k.fence("tensor")
        g["osb_free"] = k.fence("tensor")
        g["qk_done"] = None

    for nm, v in [("at_free", None), ("asq_free", None),
                  ("rms_ps_free", None), ("row_free", None),
                  ("brd_free", None), ("xg_free", None),
                  ("osb_free", None), ("qraw_free", None),
                  ("sqt_free", None), ("psn_free", None),
                  ("row2_free", None), ("psb_free", None),
                  ("rot_free", None), ("rp1_free", None),
                  ("rp2_free", None), ("psv_free", None),
                  ("vT_war", None), ("vT_done", None), ("qk_done", None),
                  ("pso_free", None), ("dv_free", None),
                  ("psb2_free", None), ("bs_free", None),
                  ("ocol_free", None), ("qt_war", None),
                  ("tab_war", None), ("tab_tok", None), ("w_war", None),
                  ("acc_cnt", 0)]:
        g[nm] = v
    g["pss_free"] = [None]
    g["et_free"] = [None, None]
    g["wo_free"] = [None, None]
    g["pq_free"] = [None, None]
    g["qst_free"] = [None, None]
    g["tmp_free"] = [None] * g["ntmp"]
    g["tmp_grp"] = []

    attn_pass("g")
    attn_pass("l")

    gp_done = k.qend("gpsimd")
    res_v = k.fence("vector")
    xf = g["ln_xf"]
    accb = g["f_accb"]
    prev = None
    outs = []
    for it in range(nt):
        sl = slice(it * TT, (it + 1) * TT)
        for c in range(NCH):
            k.dma("sync", xf[:, 0, c, :], io["x_in"][c, :, sl],
                  waits=[res_v, gp_done, prev])
            k.dma("sync", accb[:, c, :], A[c, :, sl])
        lt = k.qend("sync")
        ts = []
        for c in range(NCH):
            t = k.dve("scalar_tensor_tensor", xf[:, 1, c, :],
                      accb[:, c, :], 1.0, xf[:, 0, c, :], ALU.mult,
                      ALU.add, waits=[lt])
            ts.append(t)
            k.dma("sync", io["x_out"][c, :, sl], xf[:, 1, c, :],
                  waits=[t])
        outs.append(k.qend("sync"))
        prev = [outs[-1]] + ts
    return outs


def build_attn_program(T):
    nc = bass.Bass("TRN2", target_bir_lowering=False)
    io = {}
    io["x_in"] = nc.dram_tensor("x_in", [NCH, 128, T], F32,
                                kind="ExternalInput").ap()
    io["x_out"] = nc.dram_tensor("x_out", [NCH, 128, T], F32,
                                 kind="ExternalOutput").ap()
    for n, shp in [("norm1_fm", [128, 8]), ("norm2_fm", [128, 8]),
                   ("g_wq", [D, D]), ("g_kv_down", [D, 256]),
                   ("g_k_up", [256, 256]), ("g_v_up", [256, 256]),
                   ("g_wo", [D, D]), ("l_wq", [D, D]), ("l_wk", [D, 256]),
                   ("l_wv", [D, 256]), ("l_wo", [D, D])]:
        io[n] = nc.dram_tensor(n, shp, F32, kind="ExternalInput").ap()
    for n, shp in [("c_cos_g", [128, T]), ("c_sin_g", [128, T]),
                   ("c_cos_l", [128, T]), ("c_sin_l", [128, T]),
                   ("c_rot", [128, 128]), ("c_ind2", [128, 2]),
                   ("c_ind2T", [2, 128]), ("c_amask", [128, 10, TT]),
                   ("c_ident_bf", [128, 128])]:
        io[n] = nc.dram_tensor(n, shp, BF16, kind="ExternalInput").ap()
    declare_consts(nc, io)

    g = alloc_common(nc, T)
    g["xn_dram"] = nc.dram_tensor("xn_scr", [NCH, 128, T], BF16).ap()
    g["q_dram"] = nc.dram_tensor("q_scr", [NCH, 128, T], BF16).ap()
    g["norm1"] = nc.alloc_sbuf_tensor("norm1", [128, 8], F32).ap()
    g["norm2"] = nc.alloc_sbuf_tensor("norm2", [128, 8], F32).ap()
    g["ind2"] = nc.alloc_sbuf_tensor("ind2", [128, 2], BF16).ap()
    g["ind2T"] = nc.alloc_sbuf_tensor("ind2T", [2, 128], BF16).ap()
    g["rotM"] = nc.alloc_sbuf_tensor("rotM", [128, 128], BF16).ap()
    g["ident_bf"] = nc.alloc_sbuf_tensor("ident_bf", [128, 128],
                                         BF16).ap()
    g["cos_t"] = nc.alloc_sbuf_tensor("cos_t", [128, TT], BF16).ap()
    g["sin_t"] = nc.alloc_sbuf_tensor("sin_t", [128, TT], BF16).ap()
    g["amask"] = nc.alloc_sbuf_tensor("amask", [128, 10, TT], BF16).ap()
    g["a_tile"] = g["ln_xb"]
    g["a_sq"] = g["ln_sq"]
    g["xg_t"] = nc.alloc_sbuf_tensor("xg_t", [128, NCH, TT], BF16).ap()
    g["row2"] = nc.alloc_sbuf_tensor("row2", [2, 4, TT], F32).ap()
    g["row2b"] = nc.alloc_sbuf_tensor("row2b", [2, 2, TT], BF16).ap()
    g["qraw"] = nc.alloc_sbuf_tensor("qraw", [128, TT], BF16).ap()
    g["sqt"] = nc.alloc_sbuf_tensor("sqt", [128, TT], BF16).ap()
    g["rp1"] = nc.alloc_sbuf_tensor("rp1", [128, TT], F32).ap()
    g["rp2"] = nc.alloc_sbuf_tensor("rp2", [128, TT], F32).ap()
    g["qst"] = nc.alloc_sbuf_tensor("qst", [128, 2, TT], BF16).ap()
    g["q_tile"] = nc.alloc_sbuf_tensor("q_tile", [128, TT], BF16).ap()
    g["k_sb"] = nc.alloc_sbuf_tensor("k_sb", [128, 2, T], BF16).ap()
    g["v_t"] = nc.alloc_sbuf_tensor("v_t", [128, 2, TT], BF16).ap()
    g["lat_t"] = nc.alloc_sbuf_tensor("lat_t", [128, 2, TT], BF16).ap()
    g["vT_sb"] = nc.alloc_sbuf_tensor("vT_sb", [128, T // 128, 4, 65],
                                      BF16).ap()
    g["E_t"] = nc.alloc_sbuf_tensor("E_t", [128, 2, TT], BF16).ap()
    g["brd_sb"] = nc.alloc_sbuf_tensor("brd_sb", [64, TT], F32).ap()
    g["o_col"] = nc.alloc_sbuf_tensor("o_col", [128, NCH, TT], BF16).ap()
    g["ntmp"] = 4
    g["f_tmp"] = nc.alloc_sbuf_tensor("f_tmp", [128, 4, TT], BF16).ap()
    g["f_accb"] = g["xg_t"]
    g["w_q"] = nc.alloc_sbuf_tensor("w_q", [128, NCH, D], BF16).ap()
    g["w_kvd"] = nc.alloc_sbuf_tensor("w_kvd", [128, NCH, 256], BF16).ap()
    g["w_kup"] = nc.alloc_sbuf_tensor("w_kup", [128, 2, 256], BF16).ap()
    g["w_vup"] = nc.alloc_sbuf_tensor("w_vup", [128, 2, 256], BF16).ap()
    g["w_kl"] = nc.alloc_sbuf_tensor("w_kl", [128, NCH, 256], BF16).ap()
    g["w_vl"] = nc.alloc_sbuf_tensor("w_vl", [128, NCH, 256], BF16).ap()
    g["w_o"] = nc.alloc_sbuf_tensor("w_o", [128, NCH, D], BF16).ap()

    k = KB(nc)
    k.io = io

    def body(name):
        load_consts(k, g)
        outs = emit_attn(k, g, io, T)
        finish_program(k, outs)

    build_block(nc, k, body)
    return nc


def rope_tables(base, T):
    j = np.arange(32)
    inv = base ** (-(2.0 * j) / 64.0)
    th = np.arange(T)[:, None] * inv[None, :]
    cos = np.cos(th)
    sin = np.sin(th)
    p = np.arange(128)
    pj = (p % 64) // 2
    cosF = cos[:, pj].T
    sgn = np.where(p % 2 == 0, -1.0, 1.0)
    sinF = sin[:, pj].T * sgn[:, None]
    return (np.ascontiguousarray(cosF).astype(BF),
            np.ascontiguousarray(sinF).astype(BF))


def attn_masks():
    p = np.arange(128)[:, None]
    c = np.arange(TT)[None, :]
    m = np.zeros((128, 10, TT), np.float32)
    for j in range(4):
        m[:, j, :] = (c >= j * 128 + p)
    for jj in range(6):
        j = jj - 2
        d = c - j * 128 - p
        m[:, 4 + jj, :] = (d >= 0) & (d < 256)
    return m.astype(BF)


def prep_attn_inputs(xT, W, T):
    cg, sg = rope_tables(1e6, T)
    cl, sl_ = rope_tables(1e4, T)
    rot = np.zeros((128, 128), np.float32)
    rot[np.arange(128), np.arange(128) ^ 1] = 1.0
    ind2 = np.zeros((128, 2), np.float32)
    ind2[:64, 0] = 1.0
    ind2[64:, 1] = 1.0
    d = {
        "x_in": np.ascontiguousarray(xT.reshape(NCH, 128, T)),
        "norm1_fm": fm(W["norm1_w"]),
        "norm2_fm": fm(W["norm2_w"]),
        "g_wq": W["g_wq"], "g_kv_down": W["g_kv_down"],
        "g_k_up": W["g_k_up"], "g_v_up": W["g_v_up"], "g_wo": W["g_wo"],
        "l_wq": W["l_wq"], "l_wk": W["l_wk"], "l_wv": W["l_wv"],
        "l_wo": W["l_wo"],
        "c_cos_g": cg, "c_sin_g": sg, "c_cos_l": cl, "c_sin_l": sl_,
        "c_rot": rot.astype(BF), "c_ind2": ind2.astype(BF),
        "c_ind2T": np.ascontiguousarray(ind2.T).astype(BF),
        "c_amask": attn_masks(),
        "c_ident_bf": np.eye(128, dtype=np.float32).astype(BF),
    }
    d = {k_: (np.ascontiguousarray(v) if isinstance(v, np.ndarray) else v)
         for k_, v in d.items()}
    d.update(const_inputs())
    return d



def emit_mamba(k, g, io, T):
    nt = T // TT
    NC32 = T // 128
    A = g["xn_dram"]
    PS = g["PS"]
    PSB = g["PSB"]
    ct = g["const_tok"]

    ln_done = emit_layernorm(k, g, io["x_in"], A, T)

    k.dma("sync", g["normw"][:], io["norm_fm"][:])
    k.dma("sync", g["ident_bf"][:], io["c_ident_bf"][:])
    k.dma("sync", g["trimask"][:], io["c_trimask"][:])
    wtok0 = k.qend("sync")

    for nm in ["at_free", "asq_free", "rms_ps_free", "row_free",
               "brd_free", "xg_free"]:
        g[nm] = None
    xm_st = None
    for tt in range(nt):
        sl = slice(tt * TT, (tt + 1) * TT)
        xg_toks = emit_rmsnorm_tile(k, g, A, tt, g["normw"], wtok0,
                                    g["xg_t"], eps=1e-5)
        for c in range(NCH):
            k.dma("sync", g["xm_dram"][c, :, sl], g["xg_t"][:, c, :],
                  waits=[xg_toks[c]])
        xm_st = k.qend("sync")
        g["xg_free"] = xm_st

    NOC = 19

    def direction(dirn):
        fwd = dirn == 0
        pre = "f_" if fwd else "b_"
        ic0 = 0 if fwd else 4
        k.dma("gpsimd", g["m_win"][:],
              io[pre + "in_wT"].rearrange("(c p) o -> p c o", p=128),
              waits=[g["mw_war"]])
        k.dma("sync", g["m_convb"][:], io[pre + "conv_b_fm"][:])
        k.dma("sync", g["m_dtb"][:], io[pre + "dt_bias"][:])
        k.dma("sync", g["m_alog"][:], io[pre + "A_log"][:])
        k.dma("sync", g["m_drep"][:], io[pre + "D_fm"][:])
        k.dma("sync", g["m_gn"][:], io[pre + "gnorm_fm"][:])
        sw = k.qend("sync")
        wtok = k.qend("gpsimd")

        xt2 = g["f_xt"]
        xt_free = [g["xt_war"], g["xt_war"]]
        ev_free = [g["ev_war"]] * 2
        dt_free = g["dt_war"]
        st_tok = None
        for tt in range(nt):
            sl = slice(tt * TT, (tt + 1) * TT)
            buf = tt % 2
            for i in range(4):
                k.dma("sync", xt2[:, buf, i, :],
                      g["xm_dram"][ic0 + i, :, sl],
                      waits=[xm_st, xt_free[buf]])
            ldx = k.qend("sync")
            mm_last = None
            for oc in range(NOC):
                rows = 16 if oc == 18 else 128
                pq = PS[oc % 2]
                tok = None
                for i in range(4):
                    tok = k.mm(pq[0:rows, :],
                               g["m_win"][:, i, oc * 128:oc * 128 + rows],
                               xt2[:, buf, i, :], start=(i == 0),
                               stop=(i == 3),
                               waits=[wtok, ldx, ev_free[oc % 2]])
                mm_last = tok
                if oc == 18:
                    ev = k.dve("tensor_copy", g["m_sc16"][0:16, sl],
                               pq[0:16, :], waits=[tok, dt_free])
                    ev_free[oc % 2] = ev
                else:
                    ev = k.act(g["m_evt"][:, oc % 2, :], pq, AF.Copy,
                               waits=[tok, ev_free[oc % 2]])
                    dst = (g["z_dram"] if oc < 8 else g["xbc_dram"])
                    di = oc if oc < 8 else oc - 8
                    k.dma("sync", dst[di, :, sl],
                          g["m_evt"][:, oc % 2, :], waits=[ev])
                    ev_free[oc % 2] = k.qend("sync")
            xt_free[buf] = mm_last
        inp_done = k.qend("sync")

        dtv = k.fence("vector")
        sc = g["m_sc16"]
        dt_sb = g["m_dt"]
        ldA = g["m_ldA"]
        cc1, tk1 = g["cc"][1.0]
        t1 = k.dve("tensor_scalar_add", sc[0:16, :], sc[0:16, :],
                   g["m_dtb"][0:16, :], waits=[dtv, sw, g["dt2_war"]])
        t2 = k.dve("tensor_scalar_max", dt_sb[0:16, :], sc[0:16, :],
                   0.0, waits=[t1, g["dt3_war"]])
        t3 = k.act(ldA[0:16, :], sc[0:16, :], AF.Abs,
                   waits=[t1, g["ldA_war"]])
        t4 = k.act(sc[0:16, :], ldA[0:16, :], AF.Exp, scale=-1.0,
                   waits=[t3, t2])
        t5 = k.act(ldA[0:16, :], sc[0:16, :], AF.Ln,
                   bias=cc1[0:16, :], waits=[t4, tk1])
        t6 = k.dve("tensor_add", dt_sb[0:16, :], dt_sb[0:16, :],
                   ldA[0:16, :], waits=[t5, t2])
        ta = k.act(g["m_a16"][0:16, 0:1], g["m_alog"][0:16, :], AF.Exp,
                   waits=[sw, t5, g["a16_war"]])
        tb = k.dve("tensor_single_scalar", g["m_a16"][0:16, 1:2],
                   g["m_a16"][0:16, 0:1], -1.0, ALU.mult, waits=[ta])
        tl = k.dve("tensor_scalar", ldA[0:16, :], dt_sb[0:16, :],
                   g["m_a16"][0:16, 1:2], None, ALU.mult,
                   waits=[tb, t6])
        g["dt2_war"] = k.fence("scalar")
        g["dt3_war"] = tl
        g["a16_war"] = tl

        k.dma("gpsimd", g["m_conv"][:],
              io[pre + "conv_diag"].rearrange("a k i j -> i (a k) j"),
              waits=[k.fence("tensor")])
        wtok = k.qend("gpsimd")
        xbt = g["m_xbt"]
        cv_free = [k.fence("vector")] * 2
        xb_free = [None, None]
        for tt in range(nt):
            sl = slice(tt * TT, (tt + 1) * TT)
            buf = 0
            edge = (tt == 0) if fwd else (tt == nt - 1)
            mz = None
            if edge:
                pc = 0 if fwd else 512 + 3 - 3
                mz = k.pool("memset",
                            xbt[:, buf, :, (0 if fwd else 512):515] if not fwd
                            else xbt[:, buf, :, 0:3], 0.0,
                            waits=[xb_free[buf]])
            for cch in range(10):
                if fwd:
                    lo = tt * TT - 3
                    if tt == 0:
                        k.dma("sync", xbt[:, buf, cch, 3:515],
                              g["xbc_dram"][cch, :, sl],
                              waits=[xb_free[buf], inp_done, mz])
                    else:
                        k.dma("sync", xbt[:, buf, cch, 0:515],
                              g["xbc_dram"][cch, :, lo:lo + 515],
                              waits=[xb_free[buf], inp_done])
                else:
                    if tt == nt - 1:
                        k.dma("sync", xbt[:, buf, cch, 0:512],
                              g["xbc_dram"][cch, :, sl],
                              waits=[xb_free[buf], inp_done, mz])
                    else:
                        k.dma("sync", xbt[:, buf, cch, 0:515],
                              g["xbc_dram"][cch, :, tt * TT:tt * TT + 515],
                              waits=[xb_free[buf], inp_done])
            ldb = k.qend("sync")
            mm_l = None
            for cch in range(10):
                pq = PS[cch % 2]
                tok = None
                for kk in range(4):
                    off = kk if fwd else (3 - kk)
                    tok = k.mm(pq, g["m_conv"][:, cch * 4 + kk, :],
                               xbt[:, buf, cch, off:off + TT],
                               start=(kk == 0), stop=(kk == 3),
                               waits=[wtok, ldb, cv_free[cch % 2]])
                mm_l = tok
                ev = k.act(g["m_evt"][:, cch % 2, :], pq, AF.Silu,
                           bias=g["m_convb"][:, cch:cch + 1],
                           waits=[tok, sw, cv_free[cch % 2]])
                cv_free[cch % 2] = ev
                if cch < 8:
                    k.dma("sync", g["xsil_dram"][cch, :, sl],
                          g["m_evt"][:, cch % 2, :], waits=[ev])
                    cv_free[cch % 2] = k.qend("sync")
                elif cch == 8:
                    cv_free[cch % 2] = k.dve(
                        "tensor_copy", g["m_B"][:, sl],
                        g["m_evt"][:, cch % 2, :],
                        waits=[ev, g["B_war"]])
                else:
                    cv_free[cch % 2] = k.dve(
                        "tensor_copy", g["m_C"][:, sl],
                        g["m_evt"][:, cch % 2, :],
                        waits=[ev, g["C_war"]])
            xb_free[buf] = mm_l
        conv_done = [k.qend("sync"), k.fence("vector")]

        hz1 = k.pool("memset", g["m_hT"][:], 0.0, waits=[g["hT_war"]])
        hz2 = k.pool("memset", g["m_hTb"][:], 0.0, waits=[g["hT_war"]])
        k.pool("memset", g["m_z16"][:], 0.0)
        z16tok = k.qend("gpsimd") if False else k.fence("gpsimd")
        hT = g["m_hT"]
        hTb = g["m_hTb"]
        chunks = range(NC32) if fwd else range(NC32 - 1, -1, -1)
        wcol_idx = 127 if fwd else 0
        trimask = g["trimask"][:, (0 if fwd else 1) * 128:
                               (1 if fwd else 2) * 128]
        ht_tok = [hz1, hz2]
        cum_free = None
        ct_free = None
        dtt_free = None
        bt_free = None
        g_free = None
        xh_free = [None, None]
        ps0_free = None
        ps3_free = None
        psy_free = None
        psy2_free = None
        psh_free = None
        psb_free = None
        n1_free = None
        d_free = None
        w_free2 = None
        e_free = None
        eb_free = None
        dtx_free = None
        p_free = None
        bw_free = None
        y1_free = None
        y2_free = None
        ycol_free = None
        first_chunk = True
        for ci, c in enumerate(chunks):
            cb = ci % 2
            csl = slice(c * 128, c * 128 + 128)
            for ch8 in range(NCH):
                k.dma("sync", g["m_xh"][:, cb, ch8, 0:128],
                      g["xsil_dram"][ch8, :, csl],
                      waits=[conv_done[0], xh_free[cb]])
            xht = k.qend("sync")
            cum = g["m_cum"]
            t1 = k.dve("tensor_tensor_scan", cum[0:16, 0, :],
                       ldA[0:16, csl], g["m_z16"][0:16, :], 0.0,
                       ALU.add, ALU.add,
                       waits=[tl, z16tok, cum_free])
            if not fwd:
                s1 = k.dve("tensor_scalar", cum[0:16, 1, :],
                           cum[0:16, 0, :], cum[0:16, 0, 127:128], None,
                           ALU.subtract, waits=[t1])
                s2 = k.dve("tensor_sub", cum[0:16, 0, :],
                           ldA[0:16, csl], cum[0:16, 1, :], waits=[s1])
                t1 = s2
            cum_free = t1
            tc1 = k.tp(PS[5][0:128, 0:16], cum[0:16, 0, :],
                       g["ident_f32"][0:16, 0:16],
                       waits=[t1, ct, ps0_free])
            tc2 = k.dve("tensor_copy", g["m_cumT"][:, 0:16],
                        PS[5][0:128, 0:16], waits=[tc1, ct_free])
            tn1 = k.dve("tensor_single_scalar", g["m_cumT"][:, 16:32],
                        g["m_cumT"][:, 0:16], -1.0, ALU.mult,
                        waits=[tc2])
            td1 = k.tp(PS[5][0:128, 16:32], dt_sb[0:16, csl],
                       g["ident_f32"][0:16, 0:16],
                       waits=[t6, ct, ps0_free])
            td2 = k.dve("tensor_copy", g["m_dtT"][:, 0:16],
                        PS[5][0:128, 16:32], waits=[td1, dtt_free])
            ps0_free = [tc2, td2]
            tb1 = k.tp(PSB[:, 0:128], g["m_B"][:, csl],
                       g["ident_bf"][:, :],
                       waits=[conv_done[1], wtok0, ct, psb_free])
            tb2 = k.dve("tensor_copy", g["m_BT"][:], PSB[:, 0:128],
                        waits=[tb1, bt_free])
            tg1 = k.mm(PS[6][:, 0:128], g["m_B"][:, csl], g["m_C"][:, csl],
                       start=True, stop=True,
                       waits=[conv_done[1], g_free])
            tg2 = k.dve("tensor_mul", g["m_G"][:], PS[6][:, 0:128], trimask,
                        waits=[tg1, wtok0, g["G_war"]])
            g_free = tg2
            for h in range(16):
                hr = (h % 2) * 64
                hc8 = h // 2
                a1 = k.dve("tensor_scalar", g["m_n1"][:],
                           g["ident_f32"][:], 0.0,
                           g["m_cumT"][:, h:h + 1], ALU.mult, ALU.add,
                           waits=[tc2, ct, n1_free, d_free])
                a2 = k.tp(PS[0][:, 0:128], g["m_n1"][:], g["ident_f32"],
                          waits=[a1, ct, ps3_free])
                aM = k.dve("tensor_scalar", g["m_n1"][:], PS[0][:, 0:128],
                           g["m_cumT"][:, 16 + h:17 + h], 0.0, ALU.add,
                           ALU.min, waits=[a2, tn1, a1])
                n1_free = None
                a3 = k.act(g["m_D"][:], g["m_n1"][:], AF.Exp,
                           waits=[aM, d_free])
                a4 = k.act(g["m_w"][:, 0:1],
                           PS[0][:, wcol_idx:wcol_idx + 1], AF.Exp,
                           bias=g["m_cumT"][:, 16 + h:17 + h],
                           waits=[a2, tn1, w_free2])
                a5 = k.act(g["m_e"][0:1, :], PS[0][0:1, 0:128], AF.Exp,
                           waits=[a2, e_free])
                ps3_free = [a3, a4, a5]
                a6 = k.mm(PS[1][:, 0:128], g["ones_row"], g["m_e"][0:1, :],
                          start=True, stop=True,
                          waits=[a5, ct, eb_free])
                a7 = k.act(g["m_eb"][:], PS[1][:, 0:128], AF.Copy, waits=[a6])
                eb_free = a7
                e_free = a6
                a8 = k.tp(PSB[:, 128:192],
                          g["m_xh"][hr:hr + 64, cb, hc8, 0:128],
                          g["ident_bf"][hr:hr + 64, hr:hr + 64],
                          waits=[xht, ct, dtx_free])
                a9 = k.act(g["m_dtxT"][:], PSB[:, 128:192], AF.Copy,
                           scale=g["m_dtT"][:, h:h + 1],
                           waits=[a8, td2, g["dtx2_war"]])
                dtx_free = a9
                b1 = k.dve("tensor_mul", g["m_P"][:], g["m_G"][:],
                           g["m_D"][:], waits=[tg2, a3, p_free])
                d_free = b1
                b2 = k.mm(PS[2][0:64, 0:128], g["m_dtxT"][:, 0:64],
                          g["m_P"][:], start=True, stop=True,
                          waits=[a9, b1, psy_free])
                p_free = b2
                b3 = k.mm(PS[3][0:64, 0:128], hTb[:, h, :], g["m_C"][:, csl],
                          start=True, stop=True,
                          waits=[ht_tok[0], ht_tok[1], conv_done[1],
                                 psy2_free])
                b4 = k.dve("tensor_mul", g["m_y1"][hr:hr + 64, :],
                           PS[3][0:64, 0:128],
                           g["m_eb"][hr:hr + 64, :],
                           waits=[b3, a7, y1_free])
                psy2_free = b4
                b5 = k.dve("tensor_add", g["m_y2"][hr:hr + 64, :],
                           g["m_y1"][hr:hr + 64, :],
                           PS[2][0:64, 0:128], waits=[b4, b2, y2_free])
                psy_free = b5
                y1_free = b5
                b6 = k.dve("scalar_tensor_tensor",
                           g["m_ycol"][hr:hr + 64, hc8, :],
                           g["m_xh"][hr:hr + 64, cb, hc8, 0:128],
                           g["m_drep"][hr:hr + 64, hc8:hc8 + 1],
                           g["m_y2"][hr:hr + 64, :], ALU.mult, ALU.add,
                           waits=[b5, xht, sw, ycol_free])
                y2_free = b6
                c1 = k.dve("tensor_scalar", g["m_Bw"][:], g["m_BT"][:],
                           g["m_w"][:, 0:1], None, ALU.mult,
                           waits=[tb2, a4, bw_free])
                w_free2 = c1
                c2 = k.mm(PS[4][:, 0:64], g["m_Bw"][:], g["m_dtxT"][:, 0:64],
                          start=True, stop=True,
                          waits=[c1, a9, psh_free])
                bw_free = c2
                dtx_free = [dtx_free, c2]
                c3 = k.dve("scalar_tensor_tensor", hT[:, h, :],
                           hT[:, h, :], g["m_eb"][:, wcol_idx:
                                                  wcol_idx + 1],
                           PS[4][:, 0:64], ALU.mult, ALU.add,
                           waits=[c2, a7, ht_tok[0], ht_tok[1]])
                psh_free = c3
                c4 = k.dve("tensor_copy", hTb[:, h, :], hT[:, h, :],
                           waits=[c3])
                ht_tok = [c3, c4]
                eb_free = [eb_free, c3, b4]
            bt_free = k.fence("vector")
            ct_free = k.fence("scalar")
            dtt_free = k.fence("scalar")
            psb_free = k.fence("vector")
            g["G_war"] = k.fence("vector")
            g["dtx2_war"] = k.fence("tensor")
            yst = []
            for ch8 in range(NCH):
                k.dma("sync", g["y_dram"][ch8, :, csl],
                      g["m_ycol"][:, ch8, :],
                      waits=[k.fence("vector")])
            ycol_free = k.qend("sync")
            xh_free[cb] = k.fence("tensor")
        y_done = k.qend("sync")

        k.dma("gpsimd", g["m_wout"][:],
              io[pre + "outp_wT"].rearrange("(c p) o -> p c o", p=128),
              waits=[k.fence("tensor")])
        wtok = k.qend("gpsimd")
        oc0 = 0 if fwd else 4
        gn_free = None
        y2_war = None
        yg_war = None
        po_free = [k.fence("vector")] * 2
        ot_free = [k.fence("vector")] * 2
        for tt in range(nt):
            sl = slice(tt * TT, (tt + 1) * TT)
            buf = tt % 2
            for ch8 in range(NCH):
                k.dma("sync", g["m_xh"][:, 0, ch8, :],
                      g["y_dram"][ch8, :, sl], waits=[y_done,
                                                      g["yl_war"]])
                k.dma("sync", g["m_xh"][:, 1, ch8, :],
                      g["z_dram"][ch8, :, sl], waits=[g["yl_war"]])
            lz = k.qend("sync")
            sq_toks = []
            y2t = g["xg_t"]
            for ch8 in range(NCH):
                zt = k.act(g["m_evt"][:, ch8 % 2, :],
                           g["m_xh"][:, 1, ch8, :], AF.Silu,
                           waits=[lz, g["ev2_war"][ch8 % 2]])
                yt = k.dve("tensor_mul", y2t[:, ch8, :],
                           g["m_xh"][:, 0, ch8, :],
                           g["m_evt"][:, ch8 % 2, :],
                           waits=[zt, lz, y2_war])
                g["ev2_war"][ch8 % 2] = yt
                sq = k.act(g["a_sq"][:, ch8, :], y2t[:, ch8, :],
                           AF.Square, waits=[yt, g["sq2_war"]])
                sq_toks.append([sq, yt])
            tok = None
            for ch8 in range(NCH):
                tok = k.mm(PS[6][0:1, :], g["ones_col"],
                           g["a_sq"][:, ch8, :],
                           start=(ch8 == 0), stop=(ch8 == 7),
                           waits=[sq_toks[ch8][0], ct, gn_free])
            cc5, tk5 = g["cc"][1e-5]
            sd = g["ln_rows"][0:1, 4, :]
            rr = g["ln_rows"][0:1, 5, :]
            t5 = k.act(sd, PS[6][0:1, :], AF.Sqrt, bias=cc5[0:1, :],
                       scale=1.0 / D, waits=[tok, tk5, g["r5_war"]])
            gn_free = t5
            t6 = k.dve("reciprocal", rr, sd, waits=[t5])
            t8 = k.dve("tensor_copy", g["ln_rowb"][0:1, 0, :], rr,
                       waits=[t6])
            tb1 = k.mm(PS[6], g["ones_row"], g["ln_rowb"][0:1, 0, :],
                       start=True, stop=True, waits=[t8, t5, ct,
                                                     yg_war])
            g["r5_war"] = tb1
            yg_toks = []
            for ch8 in range(NCH):
                yg_toks.append(k.dve(
                    "scalar_tensor_tensor", g["m_yg"][:, ch8, :],
                    y2t[:, ch8, :], g["m_gn"][:, ch8:ch8 + 1], PS[6],
                    ALU.mult, ALU.mult,
                    waits=[tb1, sq_toks[ch8][1], sw, g["yg2_war"]]))
            yg_war = yg_toks[-1]
            y2_war = k.fence("scalar")
            g["sq2_war"] = k.fence("tensor")
            for oc in range(4):
                pq = PS[oc % 2]
                tok = None
                for ch8 in range(NCH):
                    tok = k.mm(pq, g["m_wout"][:, ch8,
                                               oc * 128:oc * 128 + 128],
                               g["m_yg"][:, ch8, :], start=(ch8 == 0),
                               stop=(ch8 == 7),
                               waits=[wtok, yg_toks[ch8],
                                      po_free[oc % 2]])
                ev = k.act(g["m_evt"][:, oc % 2, :], pq, AF.Copy,
                           waits=[tok, ot_free[oc % 2]])
                po_free[oc % 2] = ev
                k.dma("sync", g["cat_dram"][oc0 + oc, :, sl],
                      g["m_evt"][:, oc % 2, :], waits=[ev])
                ot_free[oc % 2] = k.qend("sync")
            g["yg2_war"] = k.fence("tensor")
            g["yl_war"] = [k.fence("tensor"), k.fence("scalar"),
                           k.fence("vector")]
        g["mw_war"] = k.fence("tensor")
        g["xt_war"] = k.fence("tensor")
        g["ev_war"] = k.fence("sync")
        g["dt_war"] = k.fence("vector")
        g["ldA_war"] = k.fence("vector")
        g["hT_war"] = k.fence("vector")
        g["B_war"] = k.fence("tensor")
        g["C_war"] = k.fence("tensor")
        return k.qend("sync")

    for nm, v in [("mw_war", None), ("xt_war", None), ("ev_war", None),
                  ("dt_war", None), ("dt2_war", None), ("dt3_war", None), ("a16_war", None),
                  ("ldA_war", None), ("hT_war", None), ("B_war", None),
                  ("C_war", None), ("G_war", None), ("dtx2_war", None),
                  ("yl_war", None), ("sq2_war", None), ("yg2_war", None),
                  ("r5_war", None)]:
        g[nm] = v
    g["ev2_war"] = [None, None]

    direction(0)
    cat_f = direction(1)

    k.dma("gpsimd", g["m_wop"][:],
          io["op_wT"].rearrange("(c p) o -> p c o", p=128),
          waits=[g["mw_war"]])
    wop = k.qend("gpsimd")
    k.dma("sync", g["m_opb"][:], io["op_b_fm"][:])
    opb = k.qend("sync")
    ct_free = [None, None]
    po_free = [k.fence("vector")] * 2
    ot_free = [k.fence("vector")] * 2
    res_prev = None
    outs = []
    for tt in range(nt):
        sl = slice(tt * TT, (tt + 1) * TT)
        for ch8 in range(NCH):
            k.dma("sync", g["m_xh"][:, 0, ch8, :],
                  g["cat_dram"][ch8, :, sl], waits=[cat_f, res_prev])
        lc = k.qend("sync")
        for oc in range(NCH):
            pq = PS[oc % 2]
            tok = None
            for ch8 in range(NCH):
                tok = k.mm(pq, g["m_wop"][:, ch8, oc * 128:oc * 128 + 128],
                           g["m_xh"][:, 0, ch8, :], start=(ch8 == 0),
                           stop=(ch8 == 7),
                           waits=[wop, lc, po_free[oc % 2]])
            k.dma("sync", g["m_evt"][:, oc % 2, :], A[oc, :, sl],
                  waits=[ot_free[oc % 2]])
            k.dma("sync", g["ln_xf"][:, 0, oc % 2, :],
                  io["x_in"][oc, :, sl], waits=[ot_free[oc % 2]])
            lax = k.qend("sync")
            e1 = k.dve("tensor_scalar", g["m_y1"][0:128, 0:TT] if False
                       else g["xg_t"][:, oc % 2, :], pq,
                       g["m_opb"][:, oc:oc + 1], None, ALU.add,
                       waits=[tok, opb, ct_free[oc % 2]])
            po_free[oc % 2] = e1
            e2 = k.dve("tensor_add", g["xg_t"][:, 2 + oc % 2, :],
                       g["xg_t"][:, oc % 2, :], g["m_evt"][:, oc % 2, :],
                       waits=[e1, lax])
            e3 = k.dve("scalar_tensor_tensor",
                       g["ln_xf"][:, 1, oc % 2, :],
                       g["xg_t"][:, 2 + oc % 2, :], 1.0,
                       g["ln_xf"][:, 0, oc % 2, :], ALU.mult, ALU.add,
                       waits=[e2])
            ct_free[oc % 2] = e3
            k.dma("sync", io["x_out"][oc, :, sl],
                  g["ln_xf"][:, 1, oc % 2, :], waits=[e3])
            ot_free[oc % 2] = k.qend("sync")
        res_prev = ot_free[1]
    return outs + [ot_free[0], ot_free[1]]


def build_mamba_program(T):
    nc = bass.Bass("TRN2", target_bir_lowering=False)
    io = {}
    io["x_in"] = nc.dram_tensor("x_in", [NCH, 128, T], F32,
                                kind="ExternalInput").ap()
    io["x_out"] = nc.dram_tensor("x_out", [NCH, 128, T], F32,
                                 kind="ExternalOutput").ap()
    specs = [("norm_fm", [128, 8]), ("op_wT", [D, D]),
             ("op_b_fm", [128, 8])]
    for pre in ("f_", "b_"):
        specs += [(pre + "in_wT", [512, 2320]),
                  (pre + "conv_diag", [10, 4, 128, 128]),
                  (pre + "conv_b_fm", [128, 10]),
                  (pre + "dt_bias", [16, 1]), (pre + "A_log", [16, 1]),
                  (pre + "D_fm", [128, 8]), (pre + "gnorm_fm", [128, 8]),
                  (pre + "outp_wT", [D, 512])]
    for n, shp in specs:
        io[n] = nc.dram_tensor(n, shp, F32, kind="ExternalInput").ap()
    io["c_ident_bf"] = nc.dram_tensor("c_ident_bf", [128, 128], BF16,
                                      kind="ExternalInput").ap()
    io["c_trimask"] = nc.dram_tensor("c_trimask", [128, 256], BF16,
                                     kind="ExternalInput").ap()
    declare_consts(nc, io)

    g = alloc_common(nc, T)
    for n, shp, dt in [("xn_dram", [NCH, 128, T], BF16),
                       ("xm_dram", [NCH, 128, T], BF16),
                       ("z_dram", [NCH, 128, T], BF16),
                       ("xbc_dram", [10, 128, T], BF16),
                       ("xsil_dram", [NCH, 128, T], BF16),
                       ("y_dram", [NCH, 128, T], BF16),
                       ("cat_dram", [NCH, 128, T], BF16)]:
        g[n] = nc.dram_tensor(n.replace("dram", "scr"), shp, dt).ap()
    g["normw"] = nc.alloc_sbuf_tensor("normw", [128, 8], F32).ap()
    g["ident_bf"] = nc.alloc_sbuf_tensor("ident_bf", [128, 128],
                                         BF16).ap()
    g["trimask"] = nc.alloc_sbuf_tensor("trimask", [128, 256], BF16).ap()
    g["a_tile"] = g["ln_xb"]
    g["a_sq"] = g["ln_sq"]
    g["xg_t"] = nc.alloc_sbuf_tensor("xg_t", [128, NCH, TT], BF16).ap()
    g["f_xt"] = nc.alloc_sbuf_tensor("f_xt", [128, 2, 4, TT], BF16).ap()
    wu = nc.alloc_sbuf_tensor("m_wu", [128, 9280], BF16)
    waddr = nc.lookup_mloc(wu).addr
    g["m_win"] = nc.alloc_sbuf_tensor_at(
        "m_win", [128, 4, 2320], BF16, offset=waddr).ap()
    g["m_conv"] = nc.alloc_sbuf_tensor_at(
        "m_conv", [128, 40, 128], BF16, offset=waddr).ap()
    g["m_convb"] = nc.alloc_sbuf_tensor("m_convb", [128, 10], F32).ap()
    g["m_dtb"] = nc.alloc_sbuf_tensor("m_dtb", [16, 1], F32).ap()
    g["m_alog"] = nc.alloc_sbuf_tensor("m_alog", [16, 1], F32).ap()
    g["m_drep"] = nc.alloc_sbuf_tensor("m_drep", [128, 8], F32).ap()
    g["m_gn"] = nc.alloc_sbuf_tensor("m_gn", [128, 8], F32).ap()
    g["m_wout"] = nc.alloc_sbuf_tensor_at(
        "m_wout", [128, 8, 512], BF16, offset=waddr).ap()
    g["m_wop"] = nc.alloc_sbuf_tensor_at(
        "m_wop", [128, 8, 1024], BF16, offset=waddr).ap()
    g["m_opb"] = nc.alloc_sbuf_tensor("m_opb", [128, 8], F32).ap()
    g["m_evt"] = nc.alloc_sbuf_tensor("m_evt", [128, 2, TT], BF16).ap()
    g["m_sc16"] = nc.alloc_sbuf_tensor("m_sc16", [16, T], F32).ap()
    g["m_dt"] = nc.alloc_sbuf_tensor("m_dt", [16, T], F32).ap()
    g["m_ldA"] = nc.alloc_sbuf_tensor("m_ldA", [16, T], F32).ap()
    g["m_a16"] = nc.alloc_sbuf_tensor("m_a16", [16, 2], F32).ap()
    g["m_xbt"] = nc.alloc_sbuf_tensor("m_xbt", [128, 1, 10, 515],
                                      BF16).ap()
    g["m_B"] = nc.alloc_sbuf_tensor("m_B", [128, T], BF16).ap()
    g["m_C"] = nc.alloc_sbuf_tensor("m_C", [128, T], BF16).ap()
    g["m_xh"] = nc.alloc_sbuf_tensor("m_xh", [128, 2, NCH, TT],
                                     BF16).ap()
    g["m_cum"] = nc.alloc_sbuf_tensor("m_cum", [16, 2, 128], F32).ap()
    g["m_z16"] = nc.alloc_sbuf_tensor("m_z16", [16, 128], F32).ap()
    g["m_cumT"] = nc.alloc_sbuf_tensor("m_cumT", [128, 32], F32).ap()
    g["m_dtT"] = nc.alloc_sbuf_tensor("m_dtT", [128, 16], F32).ap()
    g["m_BT"] = nc.alloc_sbuf_tensor("m_BT", [128, 128], BF16).ap()
    g["m_Bw"] = nc.alloc_sbuf_tensor("m_Bw", [128, 128], BF16).ap()
    g["m_G"] = nc.alloc_sbuf_tensor("m_G", [128, 128], BF16).ap()
    g["m_D"] = nc.alloc_sbuf_tensor("m_D", [128, 128], BF16).ap()
    g["m_P"] = nc.alloc_sbuf_tensor("m_P", [128, 128], BF16).ap()
    g["m_n1"] = nc.alloc_sbuf_tensor("m_n1", [128, 128], F32).ap()
    g["m_w"] = nc.alloc_sbuf_tensor("m_w", [128, 1], F32).ap()
    g["m_e"] = nc.alloc_sbuf_tensor("m_e", [1, 128], BF16).ap()
    g["m_eb"] = nc.alloc_sbuf_tensor("m_eb", [128, 128], F32).ap()
    g["m_dtxT"] = nc.alloc_sbuf_tensor("m_dtxT", [128, 64], BF16).ap()
    g["m_hT"] = nc.alloc_sbuf_tensor("m_hT", [128, 16, 64], F32).ap()
    g["m_hTb"] = nc.alloc_sbuf_tensor("m_hTb", [128, 16, 64], BF16).ap()
    g["m_y1"] = nc.alloc_sbuf_tensor("m_y1", [128, 128], F32).ap()
    g["m_y2"] = nc.alloc_sbuf_tensor("m_y2", [128, 128], F32).ap()
    g["m_ycol"] = nc.alloc_sbuf_tensor("m_ycol", [128, NCH, 128],
                                       BF16).ap()
    xbt_addr = nc.lookup_mloc(g["m_xbt"].tensor).addr
    g["m_yg"] = nc.alloc_sbuf_tensor_at(
        "m_yg", [128, NCH, TT], BF16, offset=xbt_addr).ap()

    k = KB(nc)
    k.io = io

    def body(name):
        load_consts(k, g)
        outs = emit_mamba(k, g, io, T)
        finish_program(k, outs)

    build_block(nc, k, body)
    return nc


def prep_mamba_inputs(xT, W, T):
    d = {
        "x_in": np.ascontiguousarray(xT.reshape(NCH, 128, T)),
        "norm_fm": fm(W["norm_w"]),
        "op_wT": np.ascontiguousarray(W["op_w"].T),
        "op_b_fm": fm(W["op_b"]),
    }
    for pre in ("f_", "b_"):
        cw = W[pre + "conv_w"]
        cd = np.zeros((10, 4, 128, 128), np.float32)
        for cch in range(10):
            for kk in range(4):
                np.fill_diagonal(cd[cch, kk], cw[cch * 128:(cch + 1) * 128,
                                                kk])
        d[pre + "in_wT"] = np.ascontiguousarray(W[pre + "in_w"].T)
        d[pre + "conv_diag"] = cd
        d[pre + "conv_b_fm"] = fm(W[pre + "conv_b"].reshape(1280)[
            :1280]).reshape(128, 10) if False else np.ascontiguousarray(
            W[pre + "conv_b"].reshape(10, 128).T)
        d[pre + "dt_bias"] = W[pre + "dt_bias"].reshape(16, 1)
        d[pre + "A_log"] = W[pre + "A_log"].reshape(16, 1)
        d[pre + "D_fm"] = np.ascontiguousarray(
            np.repeat(W[pre + "D"], 64).reshape(8, 128).T)
        d[pre + "gnorm_fm"] = fm(W[pre + "gnorm_w"])
        d[pre + "outp_wT"] = np.ascontiguousarray(W[pre + "outp_w"].T)
    tri = np.zeros((128, 256), np.float32)
    s_ = np.arange(128)[:, None]
    t_ = np.arange(128)[None, :]
    tri[:, 0:128] = (s_ <= t_)
    tri[:, 128:256] = (s_ >= t_)
    d["c_trimask"] = tri.astype(BF)
    d["c_ident_bf"] = np.eye(128, dtype=np.float32).astype(BF)
    d = {k_: np.ascontiguousarray(v).astype(
        v.dtype if v.dtype == BF else np.float32) for k_, v in d.items()}
    d.update(const_inputs())
    return d



def alloc_common(nc, T):
    g = {}
    g["PS"] = [nc.alloc_psum_tensor(f"ps{i}", [128, 512], F32).ap()
               for i in range(7)]
    g["PSB"] = nc.alloc_psum_tensor("psb", [128, 1024], BF16).ap()
    g["ones_col"] = nc.alloc_sbuf_tensor("ones_col", [128, 1], BF16).ap()
    g["ones_row"] = nc.alloc_sbuf_tensor("ones_row", [1, 128], BF16).ap()
    g["ident_f32"] = nc.alloc_sbuf_tensor("ident_f32", [128, 128], F32).ap()
    g["ln_xf"] = nc.alloc_sbuf_tensor("ln_xf", [128, 2, NCH, TT], F32).ap()
    g["ln_sq"] = nc.alloc_sbuf_tensor("ln_sq", [128, NCH, TT], BF16).ap()
    g["ln_xb"] = nc.alloc_sbuf_tensor("ln_xb", [128, NCH, TT], BF16).ap()
    g["ln_rows"] = nc.alloc_sbuf_tensor("ln_rows", [1, 7, TT], F32).ap()
    g["ln_rowb"] = nc.alloc_sbuf_tensor("ln_rowb", [1, 2, TT], BF16).ap()
    g["ccol"] = nc.alloc_sbuf_tensor("ccol", [128, 8], F32).ap()
    return g


def declare_consts(nc, io):
    io["c_ones_col"] = nc.dram_tensor("c_ones_col", [128, 1], BF16,
                                      kind="ExternalInput").ap()
    io["c_ones_row"] = nc.dram_tensor("c_ones_row", [1, 128], BF16,
                                      kind="ExternalInput").ap()
    io["c_ident_f32"] = nc.dram_tensor("c_ident_f32", [128, 128], F32,
                                       kind="ExternalInput").ap()


def const_inputs():
    return {
        "c_ones_col": np.ones((128, 1), BF),
        "c_ones_row": np.ones((1, 128), BF),
        "c_ident_f32": np.eye(128, dtype=np.float32),
    }


CCVALS = [1e-5, 1e-6, -8.0, 1.0, 0.5, -0.5]


def load_consts(k, g):
    k.dma("sync", g["ones_col"][:], k.io["c_ones_col"][:])
    k.dma("sync", g["ones_row"][:], k.io["c_ones_row"][:])
    k.dma("sync", g["ident_f32"][:], k.io["c_ident_f32"][:])
    g["const_tok"] = k.qend("sync")
    g["cc"] = {}
    for i, v in enumerate(CCVALS):
        tok = k.pool("memset", g["ccol"][:, i:i + 1], v)
        g["cc"][v] = (g["ccol"][:, i:i + 1], tok)


def finish_program(k, outs):
    if k.cur == "sync":
        for s, c in k.counts.items():
            if c:
                k.eng.wait_ge(k.sem_h[s], c)


def build_block(nc, k, emit_body):
    def run(name, eng):
        k.begin(name, eng)
        emit_body(name)

    with nc.Block() as blk:
        blk.sync(lambda e: run("sync", e))
        blk.tensor(lambda e: run("tensor", e))
        blk.scalar(lambda e: run("scalar", e))
        blk.vector(lambda e: run("vector", e))
        blk.gpsimd(lambda e: run("gpsimd", e))



def build_ffn1_program(T):
    nc = bass.Bass("TRN2", target_bir_lowering=False)
    io = {}
    io["x_in"] = nc.dram_tensor("x_in", [NCH, 128, T], F32,
                                kind="ExternalInput").ap()
    io["x_out"] = nc.dram_tensor("x_out", [NCH, 128, T], F32,
                                 kind="ExternalOutput").ap()
    for n, shp in [("gate_w", [D, 8]), ("gate_b128", [128, 8]),
                   ("w1", [8, D, 4096]), ("b1_r", [8, 128, 32]),
                   ("w2", [8, 4096, D]), ("b2", [8, D]),
                   ("ws1", [D, 4096]), ("bs1_r", [128, 32]),
                   ("ws2", [4096, D]), ("bs2_fm", [128, 8])]:
        io[n] = nc.dram_tensor(n, shp, F32, kind="ExternalInput").ap()
    declare_consts(nc, io)

    g = alloc_common(nc, T)
    g["xn_dram"] = nc.dram_tensor("xn_scr", [NCH, 128, T], BF16).ap()
    g["acc_dram"] = nc.dram_tensor("acc_scr", [NCH, 128, T], BF16).ap()
    g["cmb_dram"] = nc.dram_tensor("cmb_scr", [8, T], F32).ap()
    g["f_gw"] = nc.alloc_sbuf_tensor("f_gw", [128, NCH, 8], BF16).ap()
    g["f_gb"] = nc.alloc_sbuf_tensor("f_gb", [128, 8], F32).ap()
    g["f_r8"] = nc.alloc_sbuf_tensor("f_r8", [128, 10, 8], F32).ap()
    g["f_ct8"] = nc.alloc_sbuf_tensor("f_ct8", [8, 128], F32).ap()
    g["f_cmbTb"] = nc.alloc_sbuf_tensor("f_cmbTb", [8, T], BF16).ap()
    g["f_xt"] = nc.alloc_sbuf_tensor("f_xt", [128, 2, NCH, TT], BF16).ap()
    g["f_w1b"] = nc.alloc_sbuf_tensor("f_w1b", [128, NCH, 2048], BF16).ap()
    g["f_w2b"] = nc.alloc_sbuf_tensor("f_w2b", [128, 16, D], BF16).ap()
    g["f_b1b"] = nc.alloc_sbuf_tensor("f_b1b", [128, 16], F32).ap()
    g["f_crow"] = nc.alloc_sbuf_tensor("f_crow", [1, T], BF16).ap()
    g["f_cbrd"] = nc.alloc_sbuf_tensor("f_cbrd", [128, T], BF16).ap()
    g["f_Hg"] = nc.alloc_sbuf_tensor("f_Hg", [128, 16, TT], BF16).ap()
    g["f_stg"] = nc.alloc_sbuf_tensor("f_stg", [128, 4, 4, TT], BF16).ap()
    g["f_b2s"] = nc.alloc_sbuf_tensor("f_b2s", [8, D], BF16).ap()
    g["f_accb"] = g["f_Hg"][:, 0:NCH, :]
    g["f_bs2"] = nc.alloc_sbuf_tensor("f_bs2", [128, 8], F32).ap()

    k = KB(nc)
    k.io = io

    def body(name):
        load_consts(k, g)
        outs = emit_ffn1(k, g, io, T)
        finish_program(k, outs)

    build_block(nc, k, body)
    return nc


def prep_ffn1_inputs(xT, W, T):
    d = {
        "x_in": np.ascontiguousarray(xT.reshape(NCH, 128, T)),
        "gate_w": np.ascontiguousarray(W["gate_w"]),
        "gate_b128": np.ascontiguousarray(
            np.broadcast_to(W["gate_b"], (128, 8))).astype(np.float32),
        "w1": np.ascontiguousarray(W["w1"]),
        "b1_r": np.ascontiguousarray(
            W["b1"].reshape(8, 32, 128).transpose(0, 2, 1)),
        "w2": np.ascontiguousarray(W["w2"]),
        "b2": np.ascontiguousarray(W["b2"]),
        "ws1": np.ascontiguousarray(W["ws1"]),
        "bs1_r": np.ascontiguousarray(W["bs1"].reshape(32, 128).T),
        "ws2": np.ascontiguousarray(W["ws2"]),
        "bs2_fm": fm(W["bs2"]),
    }
    d.update(const_inputs())
    return d


_PROG_CACHE = {}


def _prog(name, T):
    key = (name, T)
    if key not in _PROG_CACHE:
        _PROG_CACHE[key] = {"ffn0": build_ffn0_program,
                            "ffn1": build_ffn1_program,
                            "attn": build_attn_program,
                            "mamba": build_mamba_program}[name](T)
    return _PROG_CACHE[key]


def kernel(**inputs):
    T = 4096
    hidden = np.asarray(inputs["L0_attn_hidden"], dtype=np.float32)
    B = hidden.shape[0]
    xs = [np.ascontiguousarray(hidden[b].T).reshape(NCH, 128, T)
          for b in range(B)]

    w_ffn0 = {k_[len("L0_ffn_"):]: np.asarray(v, np.float32)
              for k_, v in inputs.items() if k_.startswith("L0_ffn_")}
    w_attn = {k_[len("L0_attn_"):]: np.asarray(v, np.float32)
              for k_, v in inputs.items() if k_.startswith("L0_attn_")
              and k_ != "L0_attn_hidden"}
    w_ffn1 = {k_[len("L1_ffn_"):]: np.asarray(v, np.float32)
              for k_, v in inputs.items() if k_.startswith("L1_ffn_")}
    w_mamba = {k_[len("L1_attn_"):]: np.asarray(v, np.float32)
               for k_, v in inputs.items() if k_.startswith("L1_attn_")}
    _ = (inputs["L1_attn_f_in_w"], inputs["L1_attn_f_conv_w"],
         inputs["L1_attn_f_conv_b"], inputs["L1_attn_f_dt_bias"],
         inputs["L1_attn_f_A_log"], inputs["L1_attn_f_D"],
         inputs["L1_attn_f_gnorm_w"], inputs["L1_attn_f_outp_w"],
         inputs["L1_attn_b_in_w"], inputs["L1_attn_b_conv_w"],
         inputs["L1_attn_b_conv_b"], inputs["L1_attn_b_dt_bias"],
         inputs["L1_attn_b_A_log"], inputs["L1_attn_b_D"],
         inputs["L1_attn_b_gnorm_w"], inputs["L1_attn_b_outp_w"],
         inputs["L1_attn_norm_w"], inputs["L1_attn_op_w"],
         inputs["L1_attn_op_b"])

    if FUSED:
        key = ("fused", T)
        if key not in _PROG_CACHE:
            _PROG_CACHE[key] = build_fused_program(T)
        nc = _PROG_CACHE[key]
        shared = prep_fused_inputs(w_ffn0, w_attn, w_ffn1, w_mamba, T)
        in_maps = [dict(shared, x_in=xs[b]) for b in range(B)]
        res = run_bass_kernel_spmd(nc, in_maps, core_ids=list(range(B)))
        xs = [np.asarray(r["x_out"]) for r in res.results]
        out = np.stack([xs[b].reshape(D, T).T for b in range(B)], axis=0)
        return np.ascontiguousarray(out.astype(np.float32))

    stage_list = [("ffn0", w_ffn0, prep_ffn0_inputs),
                  ("attn", w_attn, prep_attn_inputs),
                  ("ffn0", w_ffn0, prep_ffn0_inputs),
                  ("ffn1", w_ffn1, prep_ffn1_inputs),
                  ("mamba", w_mamba, prep_mamba_inputs),
                  ("ffn1", w_ffn1, prep_ffn1_inputs)]

    shared_cache = {}
    for name, W, prep in stage_list:
        if name not in shared_cache:
            d = prep(np.zeros((D, T), np.float32), W, T)
            del d["x_in"]
            shared_cache[name] = d

    for name, W, prep in stage_list:
        nc = _prog(name, T)
        shared = shared_cache[name]
        in_maps = [dict(shared, x_in=xs[b]) for b in range(B)]
        res = run_bass_kernel_spmd(nc, in_maps, core_ids=list(range(B)))
        xs = [np.asarray(r["x_out"]) for r in res.results]

    out = np.stack([xs[b].reshape(D, T).T for b in range(B)], axis=0)
    return np.ascontiguousarray(out.astype(np.float32))



def _sb_factory(nc, prefix):
    def sb(name, shape, dtype):
        return nc.alloc_sbuf_tensor(prefix + name, shape, dtype).ap()
    return sb


class Arena:

    def __init__(self, nc, nbytes):
        self.nc = nc
        self.h = nc.alloc_sbuf_tensor("stage_arena", [128, nbytes],
                                      mybir.dt.uint8)
        self.base = nc.lookup_mloc(self.h).addr
        self.nbytes = nbytes
        self.off = 0
        self.n = 0

    def reset(self):
        self.off = 0

    def sb(self, name, shape, dtype):
        per = int(np.prod(shape[1:])) * mybir.dt.size(dtype)
        per = (per + 31) // 32 * 32
        assert self.off + per <= self.nbytes, (
            f"stage arena overflow at {name}: {self.off + per} > "
            f"{self.nbytes}")
        self.n += 1
        ap = self.nc.alloc_sbuf_tensor_at(
            f"ar{self.n}_{name}", shape, dtype,
            offset=self.base + self.off).ap()
        self.off += per
        return ap


def alloc_ffn1_g(nc, g, io, T, sb):
    if "xn_dram" not in g:
        g["xn_dram"] = nc.dram_tensor("xn_scr", [NCH, 128, T], BF16).ap()
    if "acc_dram" not in g:
        g["acc_dram"] = nc.dram_tensor("acc_scr", [NCH, 128, T],
                                       BF16).ap()
    if "cmb_dram" not in g:
        g["cmb_dram"] = nc.dram_tensor("cmb_scr", [8, T], F32).ap()
    g["f_gw"] = sb("f_gw", [128, NCH, 8], BF16)
    g["f_gb"] = sb("f_gb", [128, 8], F32)
    g["f_r8"] = sb("f_r8", [128, 10, 8], F32)
    g["f_ct8"] = sb("f_ct8", [8, 128], F32)
    g["f_cmbTb"] = sb("f_cmbTb", [8, T], BF16)
    g["f_xt"] = sb("f_xt", [128, 2, NCH, TT], BF16)
    g["f_w1b"] = sb("f_w1b", [128, NCH, 2048], BF16)
    g["f_w2b"] = sb("f_w2b", [128, 16, D], BF16)
    g["f_b1b"] = sb("f_b1b", [128, 16], F32)
    g["f_crow"] = sb("f_crow", [1, T], BF16)
    g["f_cbrd"] = sb("f_cbrd", [128, T], BF16)
    g["f_Hg"] = sb("f_Hg", [128, 16, TT], BF16)
    g["f_stg"] = sb("f_stg", [128, 4, 4, TT], BF16)
    g["f_b2s"] = sb("f_b2s", [8, D], BF16)
    g["f_accb"] = g["f_Hg"][:, 0:NCH, :]
    g["f_bs2"] = sb("f_bs2", [128, 8], F32)


def alloc_ffn0_g(nc, g, io, T, sb):
    if "xn_dram" not in g:
        g["xn_dram"] = nc.dram_tensor("xn_scr", [NCH, 128, T], BF16).ap()
    if "acc_dram" not in g:
        g["acc_dram"] = nc.dram_tensor("acc_scr", [NCH, 128, T],
                                       BF16).ap()
    if "cmb_dram" not in g:
        g["cmb_dram"] = nc.dram_tensor("cmb_scr", [8, T], F32).ap()
    g["f_gw"] = sb("f_gw", [128, NCH, 8], BF16)
    g["f0_gwf"] = sb("f0_gwf", [128, NCH, 8], F32)
    g["f0_rsc"] = sb("f0_rsc", [128, 8], F32)
    g["f0_var"] = sb("f0_var", [1, T], F32)
    g["f0_rs"] = sb("f0_rs", [1, 4, TT], F32)
    g["f0_rcol"] = sb("f0_rcol", [128, T // 128], F32)
    g["f0_pes"] = sb("f0_pes", [1, 8], BF16)
    g["f0_pesb"] = sb("f0_pesb", [128, 8], F32)
    g["f_r8"] = sb("f_r8", [128, 10, 8], F32)
    g["f_ct8"] = sb("f_ct8", [8, 128], F32)
    g["f_xt"] = sb("f_xt", [128, 2, NCH, TT], BF16)
    g["f0_wg0"] = sb("f0_wg0", [128, NCH, 1024], BF16)
    g["f0_wg1"] = sb("f0_wg1", [128, NCH, 1024], BF16)
    g["f_w2b"] = sb("f_w2b", [128, 8, D], BF16)
    g["f_crow"] = sb("f_crow", [1, T], BF16)
    g["f_cbrd"] = sb("f_cbrd", [128, T], BF16)
    g["f_Hg"] = sb("f_Hg", [128, 8, TT], BF16)
    g["f0_gt"] = sb("f0_gt", [128, TT], BF16)
    g["ntmp"] = 8
    g["f_tmp"] = sb("f_tmp", [128, 8, TT], BF16)
    g["f_accb"] = sb("f_accb", [128, NCH, TT], BF16)


def alloc_attn_g(nc, g, io, T, sb):
    if "xn_dram" not in g:
        g["xn_dram"] = nc.dram_tensor("xn_scr", [NCH, 128, T], BF16).ap()
    if "q_dram" not in g:
        g["q_dram"] = nc.dram_tensor("q_scr", [NCH, 128, T], BF16).ap()
    g["norm1"] = sb("norm1", [128, 8], F32)
    g["norm2"] = sb("norm2", [128, 8], F32)
    g["ind2"] = sb("ind2", [128, 2], BF16)
    g["ind2T"] = sb("ind2T", [2, 128], BF16)
    g["rotM"] = sb("rotM", [128, 128], BF16)
    g["ident_bf"] = sb("ident_bf", [128, 128], BF16)
    g["cos_t"] = sb("cos_t", [128, TT], BF16)
    g["sin_t"] = sb("sin_t", [128, TT], BF16)
    g["amask"] = sb("amask", [128, 10, TT], BF16)
    g["a_tile"] = g["ln_xb"]
    g["a_sq"] = g["ln_sq"]
    g["xg_t"] = sb("xg_t", [128, NCH, TT], BF16)
    g["row2"] = sb("row2", [2, 4, TT], F32)
    g["row2b"] = sb("row2b", [2, 2, TT], BF16)
    g["qraw"] = sb("qraw", [128, TT], BF16)
    g["sqt"] = sb("sqt", [128, TT], BF16)
    g["rp1"] = sb("rp1", [128, TT], F32)
    g["rp2"] = sb("rp2", [128, TT], F32)
    g["qst"] = sb("qst", [128, 2, TT], BF16)
    g["q_tile"] = sb("q_tile", [128, TT], BF16)
    g["k_sb"] = sb("k_sb", [128, 2, T], BF16)
    g["v_t"] = sb("v_t", [128, 2, TT], BF16)
    g["lat_t"] = sb("lat_t", [128, 2, TT], BF16)
    g["vT_sb"] = sb("vT_sb", [128, T // 128, 4, 65], BF16)
    g["E_t"] = sb("E_t", [128, 2, TT], BF16)
    g["brd_sb"] = sb("brd_sb", [64, TT], F32)
    g["o_col"] = sb("o_col", [128, NCH, TT], BF16)
    g["ntmp"] = 4
    g["f_tmp"] = sb("f_tmp", [128, 4, TT], BF16)
    g["f_accb"] = g["xg_t"]
    g["w_q"] = sb("w_q", [128, NCH, D], BF16)
    g["w_kvd"] = sb("w_kvd", [128, NCH, 256], BF16)
    g["w_kup"] = sb("w_kup", [128, 2, 256], BF16)
    g["w_vup"] = sb("w_vup", [128, 2, 256], BF16)
    g["w_kl"] = sb("w_kl", [128, NCH, 256], BF16)
    g["w_vl"] = sb("w_vl", [128, NCH, 256], BF16)
    g["w_o"] = sb("w_o", [128, NCH, D], BF16)


def alloc_mamba_g(nc, g, io, T, sb, arena=None):
    for n, shp, dt in [("xn_dram", [NCH, 128, T], BF16),
                       ("xm_dram", [NCH, 128, T], BF16),
                       ("z_dram", [NCH, 128, T], BF16),
                       ("xbc_dram", [10, 128, T], BF16),
                       ("xsil_dram", [NCH, 128, T], BF16),
                       ("y_dram", [NCH, 128, T], BF16),
                       ("cat_dram", [NCH, 128, T], BF16)]:
        if n not in g:
            g[n] = nc.dram_tensor(n.replace("dram", "scr"), shp, dt).ap()
    g["normw"] = sb("normw", [128, 8], F32)
    g["ident_bf"] = sb("ident_bf", [128, 128], BF16)
    g["trimask"] = sb("trimask", [128, 256], BF16)
    g["a_tile"] = g["ln_xb"]
    g["a_sq"] = g["ln_sq"]
    g["xg_t"] = sb("xg_t", [128, NCH, TT], BF16)
    g["f_xt"] = sb("f_xt", [128, 2, 4, TT], BF16)
    wu = sb("m_wu", [128, 9280], BF16)
    if arena is None:
        waddr = nc.lookup_mloc(wu.tensor).addr
        at = lambda n_, s_, d_: nc.alloc_sbuf_tensor_at(
            n_, s_, d_, offset=waddr).ap()
    else:
        waddr = nc.lookup_mloc(wu.tensor).addr
        at = lambda n_, s_, d_: nc.alloc_sbuf_tensor_at(
            "arm_" + n_, s_, d_, offset=waddr).ap()
    g["m_win"] = at("m_win", [128, 4, 2320], BF16)
    g["m_conv"] = at("m_conv", [128, 40, 128], BF16)
    g["m_wout"] = at("m_wout", [128, 8, 512], BF16)
    g["m_wop"] = at("m_wop", [128, 8, 1024], BF16)
    g["m_convb"] = sb("m_convb", [128, 10], F32)
    g["m_dtb"] = sb("m_dtb", [16, 1], F32)
    g["m_alog"] = sb("m_alog", [16, 1], F32)
    g["m_drep"] = sb("m_drep", [128, 8], F32)
    g["m_gn"] = sb("m_gn", [128, 8], F32)
    g["m_opb"] = sb("m_opb", [128, 8], F32)
    g["m_evt"] = sb("m_evt", [128, 2, TT], BF16)
    g["m_sc16"] = sb("m_sc16", [16, T], F32)
    g["m_dt"] = sb("m_dt", [16, T], F32)
    g["m_ldA"] = sb("m_ldA", [16, T], F32)
    g["m_a16"] = sb("m_a16", [16, 2], F32)
    g["m_xbt"] = sb("m_xbt", [128, 1, 10, 515], BF16)
    g["m_B"] = sb("m_B", [128, T], BF16)
    g["m_C"] = sb("m_C", [128, T], BF16)
    g["m_xh"] = sb("m_xh", [128, 2, NCH, TT], BF16)
    g["m_cum"] = sb("m_cum", [16, 2, 128], F32)
    g["m_z16"] = sb("m_z16", [16, 128], F32)
    g["m_cumT"] = sb("m_cumT", [128, 32], F32)
    g["m_dtT"] = sb("m_dtT", [128, 16], F32)
    g["m_BT"] = sb("m_BT", [128, 128], BF16)
    g["m_Bw"] = sb("m_Bw", [128, 128], BF16)
    g["m_G"] = sb("m_G", [128, 128], BF16)
    g["m_D"] = sb("m_D", [128, 128], BF16)
    g["m_P"] = sb("m_P", [128, 128], BF16)
    g["m_n1"] = sb("m_n1", [128, 128], F32)
    g["m_w"] = sb("m_w", [128, 1], F32)
    g["m_e"] = sb("m_e", [1, 128], BF16)
    g["m_eb"] = sb("m_eb", [128, 128], F32)
    g["m_dtxT"] = sb("m_dtxT", [128, 64], BF16)
    g["m_hT"] = sb("m_hT", [128, 16, 64], F32)
    g["m_hTb"] = sb("m_hTb", [128, 16, 64], BF16)
    g["m_y1"] = sb("m_y1", [128, 128], F32)
    g["m_y2"] = sb("m_y2", [128, 128], F32)
    g["m_ycol"] = sb("m_ycol", [128, NCH, 128], BF16)
    xbt_addr = nc.lookup_mloc(g["m_xbt"].tensor).addr
    g["m_yg"] = nc.alloc_sbuf_tensor_at(
        "arm_m_yg", [128, NCH, TT], BF16, offset=xbt_addr).ap()


FFN0_IO = [("router_w", [D, 8]), ("router_scale_fm", [128, 8]),
           ("gating_wT", [8, 2, D, 2048]), ("linear_w", [8, 2048, D]),
           ("pes_row", [1, 8])]
FFN1_IO = [("gate_w", [D, 8]), ("gate_b128", [128, 8]),
           ("w1", [8, D, 4096]), ("b1_r", [8, 128, 32]),
           ("w2", [8, 4096, D]), ("b2", [8, D]), ("ws1", [D, 4096]),
           ("bs1_r", [128, 32]), ("ws2", [4096, D]), ("bs2_fm", [128, 8])]
ATTN_IO = [("norm1_fm", [128, 8]), ("norm2_fm", [128, 8]),
           ("g_wq", [D, D]), ("g_kv_down", [D, 256]),
           ("g_k_up", [256, 256]), ("g_v_up", [256, 256]),
           ("g_wo", [D, D]), ("l_wq", [D, D]), ("l_wk", [D, 256]),
           ("l_wv", [D, 256]), ("l_wo", [D, D])]


def MAMBA_IO():
    specs = [("norm_fm", [128, 8]), ("op_wT", [D, D]),
             ("op_b_fm", [128, 8])]
    for pre in ("f_", "b_"):
        specs += [(pre + "in_wT", [512, 2320]),
                  (pre + "conv_diag", [10, 4, 128, 128]),
                  (pre + "conv_b_fm", [128, 10]),
                  (pre + "dt_bias", [16, 1]), (pre + "A_log", [16, 1]),
                  (pre + "D_fm", [128, 8]), (pre + "gnorm_fm", [128, 8]),
                  (pre + "outp_wT", [D, 512])]
    return specs


def ATTN_CONST(T):
    return [("c_cos_g", [128, T]), ("c_sin_g", [128, T]),
            ("c_cos_l", [128, T]), ("c_sin_l", [128, T]),
            ("c_rot", [128, 128]), ("c_ind2", [128, 2]),
            ("c_ind2T", [2, 128]), ("c_amask", [128, 10, TT]),
            ("c_ident_bf", [128, 128])]


def build_fused_program(T):
    nc = bass.Bass("TRN2", target_bir_lowering=False)
    xin = nc.dram_tensor("x_in", [NCH, 128, T], F32,
                         kind="ExternalInput").ap()
    xout = nc.dram_tensor("x_out", [NCH, 128, T], F32,
                          kind="ExternalOutput").ap()
    X0 = nc.dram_tensor("Xping", [NCH, 128, T], F32).ap()
    X1 = nc.dram_tensor("Xpong", [NCH, 128, T], F32).ap()

    def decl(prefix, specs, dt=F32):
        d = {}
        for n, shp in specs:
            d[n] = nc.dram_tensor(prefix + n, shp, dt,
                                  kind="ExternalInput").ap()
        return d

    io_f0 = decl("f0$", FFN0_IO)
    io_f1 = decl("f1$", FFN1_IO)
    io_at = decl("at$", ATTN_IO)
    io_at.update(decl("at$", ATTN_CONST(T), BF16))
    io_mb = decl("mb$", MAMBA_IO())
    io_mb["c_ident_bf"] = io_at["c_ident_bf"]
    io_mb["c_trimask"] = nc.dram_tensor("mb$c_trimask", [128, 256], BF16,
                                        kind="ExternalInput").ap()
    io_all = {}
    declare_consts(nc, io_all)

    g = alloc_common(nc, T)
    g["xn_dram"] = nc.dram_tensor("xn_scr", [NCH, 128, T], BF16).ap()
    g["acc_dram"] = nc.dram_tensor("acc_scr", [NCH, 128, T], BF16).ap()
    g["cmb_dram"] = nc.dram_tensor("cmb_scr", [8, T], F32).ap()
    g["q_dram"] = nc.dram_tensor("q_scr", [NCH, 128, T], BF16).ap()
    for n in ["xm_dram", "z_dram", "xsil_dram", "y_dram", "cat_dram"]:
        g[n] = nc.dram_tensor(n.replace("dram", "scr"),
                              [NCH, 128, T], BF16).ap()
    g["xbc_dram"] = nc.dram_tensor("xbc_scr", [10, 128, T], BF16).ap()

    arena = Arena(nc, 142 * 1024)
    stage_g = {}
    for name, alloc in [("ffn0", alloc_ffn0_g), ("ffn1", alloc_ffn1_g),
                        ("attn", alloc_attn_g), ("mamba", alloc_mamba_g)]:
        arena.reset()
        sg = dict(g)
        io = {"ffn0": io_f0, "ffn1": io_f1, "attn": io_at,
              "mamba": io_mb}[name]
        io.update(io_all)
        alloc(nc, sg, io, T, arena.sb)
        stage_g[name] = (sg, io)

    k = KB(nc)
    k.io = io_all

    seq = [("ffn0", emit_ffn0), ("attn", emit_attn), ("ffn0", emit_ffn0),
           ("ffn1", emit_ffn1), ("mamba", emit_mamba),
           ("ffn1", emit_ffn1)]

    def body(name):
        sg0, _ = stage_g["ffn0"]
        load_consts(k, sg0)
        cc = sg0["cc"]
        ctk = sg0["const_tok"]
        outs = None
        for si, (snm, emitter) in enumerate(seq):
            sg, sio = stage_g[snm]
            sg["cc"] = cc
            sg["const_tok"] = ctk
            sio = dict(sio)
            sio["x_in"] = xin if si == 0 else (X0 if si % 2 == 1 else X1)
            sio["x_out"] = xout if si == len(seq) - 1 else (
                X0 if si % 2 == 0 else X1)
            outs = emitter(k, sg, sio, T)
            k.barrier()
        finish_program(k, outs)

    build_block(nc, k, body)
    return nc


def prep_fused_inputs(W0, Wat, W1, Wmb, T):
    z = np.zeros((D, T), np.float32)
    d = {}
    for prefix, prep, W in [("f0$", prep_ffn0_inputs, W0),
                            ("at$", prep_attn_inputs, Wat),
                            ("f1$", prep_ffn1_inputs, W1),
                            ("mb$", prep_mamba_inputs, Wmb)]:
        dd = prep(z, W, T)
        del dd["x_in"]
        for k_, v in dd.items():
            if k_.startswith("c_ones") or k_ == "c_ident_f32":
                d[k_] = v
            elif prefix == "mb$" and k_ == "c_ident_bf":
                d["at$c_ident_bf"] = v
            else:
                d[prefix + k_] = v
    return d


# version 10
# speedup vs baseline: 1.0365x; 1.0365x over previous
import numpy as np
import ml_dtypes
import concourse.bass as bass
import concourse.mybir as mybir
from concourse.bass_utils import run_bass_kernel_spmd

F32 = mybir.dt.float32
BF16 = mybir.dt.bfloat16
AF = mybir.ActivationFunctionType
ALU = mybir.AluOpType
AX = mybir.AxisListType
BF = ml_dtypes.bfloat16

D = 1024
NCH = 8
N_CORES = 8
TT = 512
STRICT_SYNC = True
FUSED = True


class Tok:
    __slots__ = ("sem", "val")

    def __init__(self, sem, val):
        self.sem = sem
        self.val = val


SEMS = ["sync_q", "gpsimd_q", "scalar_q", "sync", "tensor", "scalar",
        "vector", "gpsimd"]


class KB:
    ENGINES = ("sync", "tensor", "scalar", "vector", "gpsimd")

    def __init__(self, nc):
        self.nc = nc
        self.sem_h = {s: nc.alloc_semaphore(s) for s in SEMS}
        self.io = {}
        self.cur = None

    def begin(self, engine_name, engine_handle):
        self.cur = engine_name
        self.eng = engine_handle
        self.counts = {s: 0 for s in self.sem_h}
        self.watermark = {e: {} for e in self.ENGINES}

    def _waits(self, engine_name, waits):
        for t in waits:
            if t is None:
                continue
            if isinstance(t, (list, tuple)):
                self._waits(engine_name, t)
                continue
            if t.sem == engine_name and (
                    engine_name == "tensor" or not STRICT_SYNC):
                continue
            wm = self.watermark[engine_name]
            if wm.get(t.sem, -1) >= t.val:
                continue
            wm[t.sem] = t.val
            if self.cur == engine_name:
                self.eng.wait_ge(self.sem_h[t.sem], t.val)

    def op(self, engine_name, emit_fn, waits=(), inc=None, inc_by=1):
        if inc is None:
            inc = engine_name
        self._waits(engine_name, waits)
        self.counts[inc] += inc_by
        tok = Tok(inc, self.counts[inc])
        if self.cur == engine_name:
            ins = emit_fn(self.eng)
            if ins is not None:
                ins.then_inc(self.sem_h[inc], inc_by)
        return tok

    def noinc(self, engine_name, emit_fn, waits=()):
        self._waits(engine_name, waits)
        if self.cur == engine_name:
            emit_fn(self.eng)
        return None

    def fence(self, sem):
        return Tok(sem, self.counts[sem])

    def barrier(self):
        wm = self.watermark[self.cur]
        for s_, c_ in self.counts.items():
            if c_ == 0 or wm.get(s_, -1) >= c_:
                continue
            wm[s_] = c_
            self.eng.wait_ge(self.sem_h[s_], c_)

    def qend(self, queue):
        sem_q = queue + "_q"
        target = self.counts[sem_q]
        self.counts[queue] += 1
        tok = Tok(queue, self.counts[queue])
        wm = self.watermark[queue]
        if wm.get(sem_q, -1) < target:
            wm[sem_q] = target
        if self.cur == queue:
            self.eng.wait_ge(self.sem_h[sem_q], target).then_inc(
                self.sem_h[queue], 1)
        return tok

    def dma(self, queue, out, in_, waits=(), **kw):
        return self.op(queue, lambda e: e.dma_start(out, in_, **kw),
                       waits=waits, inc=queue + "_q", inc_by=16)

    def mm(self, out, lhsT, rhs, start, stop, waits=(), inc=True):
        fn = lambda e: e.matmul(out, lhsT, rhs, start=start, stop=stop)
        if stop and inc:
            return self.op("tensor", fn, waits=waits)
        return self.noinc("tensor", fn, waits=waits)

    def tp(self, out, in_, ident, waits=()):
        return self.op("tensor", lambda e: e.transpose(out, in_, ident),
                       waits=waits)

    def act(self, out, in_, func, bias=0.0, scale=1.0, waits=()):
        return self.op("scalar",
                       lambda e: e.activation(out, in_, func, bias=bias,
                                              scale=scale), waits=waits)

    def dve(self, method, *args, waits=(), **kw):
        return self.op("vector", lambda e: getattr(e, method)(*args, **kw),
                       waits=waits)

    def pool(self, method, *args, waits=(), **kw):
        return self.op("gpsimd", lambda e: getattr(e, method)(*args, **kw),
                       waits=waits)


def fm(v):
    v = np.asarray(v)
    r = v.shape[0]
    assert r % 128 == 0
    return np.ascontiguousarray(v.reshape(r // 128, 128).T)



def emit_layernorm(k, g, X_in, xn_out, T, var_sb=None, eps=1e-5):
    nt = T // TT
    xf = g["ln_xf"]
    sq = g["ln_sq"]
    xb = g["ln_xb"]
    row = g["ln_rows"]
    rowb = g["ln_rowb"]
    PS = g["PS"]
    ps_s, ps_q, ps_b1, ps_b2 = PS[0], PS[1], PS[2], PS[3]
    ct = g["const_tok"]
    xf_free = [None, None]
    stat_free = None
    brd_free = None
    sq_free = None
    st_tok = None
    for it in range(nt):
        buf = it % 2
        sl = slice(it * TT, (it + 1) * TT)
        for c in range(NCH):
            k.dma("sync", xf[:, buf, c, :], X_in[c, :, sl],
                  waits=[xf_free[buf]])
        ld = k.qend("sync")
        t_sq = []
        t_xb = []
        for c in range(NCH):
            t_sq.append(k.act(sq[:, c, :], xf[:, buf, c, :], AF.Square,
                              waits=[ld, sq_free]))
            t_xb.append(k.pool("tensor_copy", xb[:, c, :], xf[:, buf, c, :],
                               waits=[ld, sq_free]))
        tok_s = tok_q = None
        for c in range(NCH):
            tok_s = k.mm(ps_s[0:1, :], g["ones_col"], xb[:, c, :],
                         start=(c == 0), stop=(c == NCH - 1),
                         waits=[t_xb[c], ct, stat_free])
            tok_q = k.mm(ps_q[0:1, :], g["ones_col"], sq[:, c, :],
                         start=(c == 0), stop=(c == NCH - 1),
                         waits=[t_sq[c], ct, stat_free])
        m, q2, m2, var, sd, rr, mr = (row[0:1, i, :] for i in range(7))
        t1 = k.dve("tensor_single_scalar", m, ps_s[0:1, :], 1.0 / D,
                   ALU.mult, waits=[tok_s])
        t2 = k.dve("tensor_single_scalar", q2, ps_q[0:1, :], 1.0 / D,
                   ALU.mult, waits=[tok_q])
        stat_free = t2
        t3 = k.dve("tensor_mul", m2, m, m, waits=[t1])
        t4 = k.dve("tensor_sub", var, q2, m2, waits=[t2, t3])
        if var_sb is not None:
            k.dve("tensor_copy", var_sb[0:1, sl], var, waits=[t4])
        eps_ap, eps_tok = g["cc"][float(eps)]
        t5 = k.act(sd, var, AF.Sqrt, bias=eps_ap[0:1, :],
                   waits=[t4, eps_tok])
        t6 = k.dve("reciprocal", rr, sd, waits=[t5])
        t7 = k.dve("tensor_mul", mr, m, rr, waits=[t6])
        t8 = k.dve("tensor_copy", rowb[0:1, 0, :], rr, waits=[t6])
        t9 = k.dve("tensor_copy", rowb[0:1, 1, :], mr, waits=[t7])
        tb1 = k.mm(ps_b1, g["ones_row"], rowb[0:1, 0, :], start=True,
                   stop=True, waits=[t8, ct, brd_free])
        tb2 = k.mm(ps_b2, g["ones_row"], rowb[0:1, 1, :], start=True,
                   stop=True, waits=[t9, ct, brd_free])
        ta_l = tb_l = None
        for c in range(NCH):
            ta_l = k.dve("tensor_mul", sq[:, c, :], xf[:, buf, c, :], ps_b1,
                         waits=[tb1, tok_q, t_sq[c]])
            tb_l = k.dve("tensor_sub", xb[:, c, :], sq[:, c, :], ps_b2,
                         waits=[tb2, ta_l, tok_s, t_xb[c]])
            k.dma("sync", xn_out[c, :, sl], xb[:, c, :], waits=[tb_l])
        st_tok = k.qend("sync")
        brd_free = tb_l
        xf_free[buf] = [ta_l, t_sq[-1], t_xb[-1]]
        sq_free = [st_tok, tb_l]
    return st_tok



def emit_ffn1(k, g, io, T):
    nt = T // TT
    xn = g["xn_dram"]
    acc = g["acc_dram"]
    PS = g["PS"]
    ct = g["const_tok"]

    ln_done = emit_layernorm(k, g, io["x_in"], xn, T)

    ps_l, ps_t = PS[4], PS[5]
    gw = g["f_gw"]
    gb = g["f_gb"]
    r8 = g["f_r8"]
    ct8 = g["f_ct8"]
    ct8_free = None
    cmbTb = g["f_cmbTb"]
    xt2 = g["f_xt"]
    k.dma("gpsimd", gw[:], io["gate_w"].rearrange("(c p) e -> p c e", p=128))
    tgw = k.qend("gpsimd")
    k.dma("sync", gb[:], io["gate_b128"][:], waits=[ln_done])
    xt_free = [None, None]
    psl_free = None
    pst_free = None
    for it in range(nt):
        buf = it % 2
        sl = slice(it * TT, (it + 1) * TT)
        for c in range(NCH):
            k.dma("sync", xt2[:, buf, c, :], xn[c, :, sl],
                  waits=[xt_free[buf]])
        ldx = k.qend("sync")
        mm_last = None
        for sub in range(TT // 128):
            s0 = sub * 128
            gsl = slice(it * TT + s0, it * TT + s0 + 128)
            tok = None
            for c in range(NCH):
                tok = k.mm(ps_l[:, 0:8], xt2[:, buf, c, s0:s0 + 128],
                           gw[:, c, :], start=(c == 0), stop=(c == NCH - 1),
                           waits=[ldx, tgw, psl_free])
            mm_last = tok
            lg = r8[:, 0, :]
            nm1 = r8[:, 1, 0:1]
            eq = r8[:, 2, :]
            l2 = r8[:, 3, :]
            nm2 = r8[:, 4, 0:1]
            msk = r8[:, 5, :]
            E = r8[:, 6, :]
            En = r8[:, 7, :]
            den = r8[:, 8, 0:1]
            dnv = r8[:, 9, 0:1]
            a1 = k.dve("tensor_add", lg, ps_l[:, 0:8], gb, waits=[tok, ldx])
            psl_free = a1
            a2 = k.dve("reduce_max", nm1, lg, axis=AX.X, negate=True,
                       waits=[a1])
            a3 = k.dve("tensor_scalar", eq, lg, nm1, 0.0, ALU.add,
                       ALU.is_equal, waits=[a2])
            a4 = k.dve("scalar_tensor_tensor", l2, eq, -1e9, lg, ALU.mult,
                       ALU.add, waits=[a3])
            a5 = k.dve("reduce_max", nm2, l2, axis=AX.X, negate=True,
                       waits=[a4])
            a6 = k.dve("tensor_scalar", msk, lg, nm2, 0.0, ALU.add,
                       ALU.is_ge, waits=[a5])
            a7 = k.act(E, lg, AF.Exp, bias=nm1, waits=[a2])
            a8 = k.dve("tensor_mul", En, E, msk, waits=[a6, a7])
            a9 = k.dve("reduce_sum", den, En, axis=AX.X, waits=[a8])
            a10 = k.dve("reciprocal", dnv, den, waits=[a9])
            a11 = k.dve("tensor_scalar", lg, En, dnv, None, ALU.mult,
                        waits=[a10])
            tp = k.tp(ps_t[0:8, 0:128], lg, g["ident_f32"],
                      waits=[a11, ct, pst_free])
            pst_free = k.dve("tensor_copy", ct8[:], ps_t[0:8, 0:128],
                             waits=[tp, ct8_free])
            k.dma("sync", g["cmb_dram"][0:8, gsl], ct8[:],
                  waits=[pst_free])
            ct8_free = k.qend("sync")
        xt_free[buf] = mm_last
    tcd = ct8_free
    k.dma("gpsimd", cmbTb[:], g["cmb_dram"][:], waits=[tcd])
    tcb = k.qend("gpsimd")

    HHALF, NHC = 2048, 16
    w1b = g["f_w1b"]
    w2b = g["f_w2b"]
    b1b = g["f_b1b"]
    crow = g["f_crow"]
    cbrd = g["f_cbrd"]
    Hg = g["f_Hg"]
    stg = g["f_stg"]
    ps_up = [PS[0], PS[1], PS[2]]
    ps_dn = [PS[3], PS[4], PS[5], PS[6]]
    w_free = None
    b1_free = None
    crow_free = None
    cbrd_free = None
    xt_free = [xt_free[0], xt_free[1]]
    stg_free = [None] * 4
    stg_grp = []
    up_free = [k.fence("vector"), k.fence("vector"), k.fence("vector")]
    dn_free = [k.fence("vector")] * 4
    hg_free = None
    cb_tok = None
    n_acc = 0
    istg = 0
    for e in range(9):
        shared = (e == 8)
        if not shared:
            k.dma("gpsimd", crow[:], g["cmb_dram"][e:e + 1, :],
                  waits=[tcd, crow_free])
            tcr = k.qend("gpsimd")
            cbs = [None]
            for it in range(nt):
                sl = slice(it * TT, (it + 1) * TT)
                tpb = k.mm(PS[2], g["ones_row"], crow[0:1, sl], start=True,
                           stop=True,
                           waits=[tcr, ct, cbrd_free, cbs[-1],
                                  k.fence("scalar"), k.fence("vector"),
                                  up_free[2]])
                cbs.append(k.dve("tensor_copy", cbrd[:, sl], PS[2],
                                 waits=[tpb]))
            cb_tok = cbs[-1]
            up_free[2] = cb_tok
            crow_free = k.fence("tensor")
        for half in range(2):
            h0 = half * HHALF
            for dc in range(NCH):
                srcw = (io["ws1"] if shared else io["w1"][e]).rearrange(
                    "(c p) h -> p c h", p=128)[:, dc, h0:h0 + HHALF]
                k.dma("gpsimd", w1b[:, dc, :], srcw, waits=[w_free])
            for hc in range(NHC):
                srcw = (io["ws2"] if shared else io["w2"][e]).rearrange(
                    "(c p) d -> p c d", p=128)[:, h0 // 128 + hc, :]
                k.dma("gpsimd", w2b[:, hc, :], srcw, waits=[w_free])
            w_tok = k.qend("gpsimd")
            bsrc = (io["bs1_r"] if shared else io["b1_r"][e])
            k.dma("sync", b1b[:], bsrc[:, half * NHC:(half + 1) * NHC],
                  waits=[b1_free])
            first = (e == 0 and half == 0)
            for it in range(nt):
                sl = slice(it * TT, (it + 1) * TT)
                buf = it % 2
                for c in range(NCH):
                    k.dma("sync", xt2[:, buf, c, :], xn[c, :, sl],
                          waits=[xt_free[buf]])
                ldx = k.qend("sync")
                gel = []
                for hc in range(NHC):
                    pu = ps_up[hc % 3]
                    tok = None
                    for dc in range(NCH):
                        tok = k.mm(pu, w1b[:, dc, hc * 128:hc * 128 + 128],
                                   xt2[:, buf, dc, :],
                                   start=(dc == 0), stop=(dc == NCH - 1),
                                   waits=[w_tok, ldx, up_free[hc % 3]])
                    gel.append(k.act(Hg[:, hc, :], pu, AF.Gelu,
                                     bias=b1b[:, hc:hc + 1],
                                     waits=[tok, ldx, hg_free]))
                    up_free[hc % 3] = gel[-1]
                for dp in range(2):
                    tok = None
                    for hc in range(NHC):
                        for dl in range(4):
                            dc = dp * 4 + dl
                            tok = k.mm(ps_dn[dl],
                                       w2b[:, hc, dc * 128:dc * 128 + 128],
                                       Hg[:, hc, :], start=(hc == 0),
                                       stop=(hc == NHC - 1),
                                       waits=[w_tok, gel[hc], dn_free[dl]],
                                       inc=(dl == 3))
                    slot = istg % 4
                    istg += 1
                    t1 = None
                    for dl in range(4):
                        if shared:
                            t1 = k.dve("tensor_copy", stg[:, slot, dl, :],
                                       ps_dn[dl],
                                       waits=[tok, stg_free[slot]])
                        else:
                            t1 = k.dve("tensor_mul", stg[:, slot, dl, :],
                                       ps_dn[dl], cbrd[:, sl],
                                       waits=[tok, cb_tok,
                                              stg_free[slot]])
                        dn_free[dl] = t1
                    dst = acc[dp * 4:dp * 4 + 4, :, sl].rearrange(
                        "c p t -> p c t")
                    k.dma("gpsimd", dst, stg[:, slot, :, :], waits=[t1],
                          accum_op=(ALU.bypass if first else ALU.add))
                    n_acc += 1
                    if n_acc % 2 == 0:
                        gtok = k.qend("gpsimd")
                        stg_grp.append(gtok)
                        if len(stg_grp) >= 2:
                            for j in range(2):
                                stg_free[(istg + j) % 4] = stg_grp[-2]
                hg_free = k.fence("tensor")
                xt_free[buf] = k.fence("tensor")
            w_free = k.fence("tensor")
            b1_free = k.fence("scalar")
        if not shared:
            cbrd_free = k.fence("vector")

    k.dma("sync", g["f_bs2"][:], io["bs2_fm"][:])
    g["bs2_tok"] = k.qend("sync")
    b2s = g["f_b2s"]
    k.dma("gpsimd", b2s[:], io["b2"][:])
    tb2 = k.qend("gpsimd")
    pe_b = k.fence("vector")
    ps2_free = k.fence("vector")
    for it in range(nt):
        sl = slice(it * TT, (it + 1) * TT)
        for dc in range(NCH):
            slot = istg % 4
            istg += 1
            tok = k.mm(PS[2], b2s[0:8, dc * 128:dc * 128 + 128],
                       cmbTb[0:8, sl], start=True, stop=True,
                       waits=[tb2, tcb, pe_b, ps2_free])
            t1 = k.dve("tensor_scalar_add", stg[:, slot, 0, :], PS[2],
                       g["f_bs2"][:, dc:dc + 1],
                       waits=[tok, stg_free[slot], g["bs2_tok"]])
            ps2_free = Tok("vector", k.counts["vector"])
            k.dma("gpsimd", acc[dc, :, sl], stg[:, slot, 0, :],
                  waits=[t1], accum_op=ALU.add)
            n_acc += 1
            if n_acc % 2 == 0:
                gtok = k.qend("gpsimd")
                stg_grp.append(gtok)
                if len(stg_grp) >= 2:
                    for j in range(2):
                        stg_free[(istg + j) % 4] = stg_grp[-2]

    gp_done = k.qend("gpsimd")
    res_v = k.fence("vector")
    res_t = k.fence("tensor")
    xf = g["ln_xf"]
    accb = g["f_accb"]
    prev = None
    outs = []
    for it in range(nt):
        sl = slice(it * TT, (it + 1) * TT)
        for c in range(NCH):
            k.dma("sync", xf[:, 0, c, :], io["x_in"][c, :, sl],
                  waits=[res_v, res_t, gp_done, prev])
            k.dma("sync", accb[:, c, :], acc[c, :, sl])
        lt = k.qend("sync")
        ts = []
        for c in range(NCH):
            t = k.dve("scalar_tensor_tensor", xf[:, 1, c, :],
                      accb[:, c, :], 0.5, xf[:, 0, c, :], ALU.mult,
                      ALU.add, waits=[lt])
            ts.append(t)
            k.dma("sync", io["x_out"][c, :, sl], xf[:, 1, c, :],
                  waits=[t])
        outs.append(k.qend("sync"))
        prev = [outs[-1]] + ts
    return outs



def emit_ffn0(k, g, io, T):
    nt = T // TT
    xn = g["xn_dram"]
    acc = g["acc_dram"]
    PS = g["PS"]
    ct = g["const_tok"]
    var_sb = g["f0_var"]

    ln_done = emit_layernorm(k, g, io["x_in"], xn, T, var_sb=var_sb)

    rrow = g["f0_rs"]
    vtok = k.fence("vector")
    cc6, tk6 = g["cc"][1e-6]
    rcol = g["f0_rcol"]
    ps_t = PS[5]
    rt_free = None
    rs_tok = None
    r6p = None
    for tt8 in range(nt):
        sl8 = slice(tt8 * TT, (tt8 + 1) * TT)
        r1 = k.dve("tensor_single_scalar", rrow[0:1, 0, :],
                   var_sb[0:1, sl8], 1e-5, ALU.add, waits=[vtok, r6p])
        r2 = k.dve("reciprocal", rrow[0:1, 1, :], rrow[0:1, 0, :],
                   waits=[r1])
        r3 = k.dve("tensor_mul", rrow[0:1, 2, :], var_sb[0:1, sl8],
                   rrow[0:1, 1, :], waits=[r2])
        r4 = k.act(rrow[0:1, 3, :], rrow[0:1, 2, :], AF.Sqrt,
                   bias=cc6[0:1, :], waits=[r3, tk6, r6p])
        r5 = k.dve("reciprocal", rrow[0:1, 0, :], rrow[0:1, 3, :],
                   waits=[r4])
        r6 = k.dve("tensor_single_scalar", rrow[0:1, 1, :],
                   rrow[0:1, 0, :], 1.0 / 32.0, ALU.mult, waits=[r5])
        r6p = r6
        for sub in range(TT // 128):
            it = tt8 * (TT // 128) + sub
            tp = k.tp(ps_t[0:128, 0:1],
                      rrow[0:1, 1, sub * 128:(sub + 1) * 128],
                      g["ident_f32"][0:1, 0:1], waits=[r6, ct, rt_free])
            rs_tok = k.dve("tensor_copy", rcol[:, it:it + 1],
                           ps_t[0:128, 0:1], waits=[tp])
            rt_free = rs_tok
            r6p = [r6p, rt_free]
    gw = g["f_gw"]
    gwf = g["f0_gwf"]
    k.dma("sync", gwf[:], io["router_w"].rearrange("(c p) e -> p c e", p=128))
    k.dma("sync", g["f0_rsc"][:], io["router_scale_fm"][:])
    trw = k.qend("sync")
    wp = []
    for c in range(NCH):
        wp.append(k.dve("tensor_scalar", gw[:, c, :], gwf[:, c, :],
                        g["f0_rsc"][:, c:c + 1], None, ALU.mult,
                        waits=[trw]))
    tgw = wp[-1]
    pesb = g["f0_pesb"]
    k.dma("gpsimd", g["f0_pes"][:], io["pes_row"][:])
    tpes0 = k.qend("gpsimd")
    tpb = k.mm(PS[2][:, 0:8], g["ones_row"], g["f0_pes"][0:1, :],
               start=True, stop=True, waits=[tpes0, ct, k.fence("vector")])
    tpes = k.dve("tensor_copy", pesb[:], PS[2][:, 0:8], waits=[tpb])

    ps_l = PS[4]
    r8 = g["f_r8"]
    ct8 = g["f_ct8"]
    ct8_free = None
    xt2 = g["f_xt"]
    xt_free = [None, None]
    psl_free = None
    pst_free = rs_tok
    for it in range(nt):
        buf = it % 2
        sl = slice(it * TT, (it + 1) * TT)
        for c in range(NCH):
            k.dma("sync", xt2[:, buf, c, :], xn[c, :, sl],
                  waits=[xt_free[buf], ln_done])
        ldx = k.qend("sync")
        mm_last = None
        for sub in range(TT // 128):
            s0 = sub * 128
            i128 = it * (TT // 128) + sub
            gsl = slice(i128 * 128, i128 * 128 + 128)
            tok = None
            for c in range(NCH):
                tok = k.mm(ps_l[:, 0:8], xt2[:, buf, c, s0:s0 + 128],
                           gw[:, c, :], start=(c == 0), stop=(c == NCH - 1),
                           waits=[ldx, tgw, psl_free])
            mm_last = tok
            lg = r8[:, 0, :]
            nm1 = r8[:, 1, 0:1]
            eq = r8[:, 2, :]
            l2 = r8[:, 3, :]
            nm2 = r8[:, 4, 0:1]
            msk = r8[:, 5, :]
            E = r8[:, 6, :]
            En = r8[:, 7, :]
            den = r8[:, 8, 0:1]
            dnv = r8[:, 9, 0:1]
            a1 = k.dve("tensor_scalar", lg, ps_l[:, 0:8],
                       rcol[:, i128:i128 + 1], None, ALU.mult,
                       waits=[tok, rs_tok])
            psl_free = a1
            a2 = k.dve("reduce_max", nm1, lg, axis=AX.X, negate=True,
                       waits=[a1])
            a3 = k.dve("tensor_scalar", eq, lg, nm1, 0.0, ALU.add,
                       ALU.is_equal, waits=[a2])
            a4 = k.dve("scalar_tensor_tensor", l2, eq, -1e9, lg, ALU.mult,
                       ALU.add, waits=[a3])
            a5 = k.dve("reduce_max", nm2, l2, axis=AX.X, negate=True,
                       waits=[a4])
            a6 = k.dve("tensor_scalar", msk, lg, nm2, 0.0, ALU.add,
                       ALU.is_ge, waits=[a5])
            a7 = k.act(E, lg, AF.Exp, bias=nm1, waits=[a2])
            a8 = k.dve("tensor_mul", En, E, msk, waits=[a6, a7])
            a9 = k.dve("reduce_sum", den, En, axis=AX.X, waits=[a8])
            a10 = k.dve("reciprocal", dnv, den, waits=[a9])
            a11 = k.dve("tensor_scalar", lg, En, dnv, None, ALU.mult,
                        waits=[a10])
            tp = k.tp(ps_t[0:8, 0:128], lg, g["ident_f32"],
                      waits=[a11, ct, pst_free])
            pst_free = k.dve("tensor_copy", ct8[:], ps_t[0:8, 0:128],
                             waits=[tp, ct8_free])
            k.dma("sync", g["cmb_dram"][0:8, gsl], ct8[:],
                  waits=[pst_free])
            ct8_free = k.qend("sync")
        xt_free[buf] = mm_last
    tcd = ct8_free

    HHALF, NHC = 1024, 8
    wg0 = g["f0_wg0"]
    wg1 = g["f0_wg1"]
    w2b = g["f_w2b"]
    crow = g["f_crow"]
    cbrd = g["f_cbrd"]
    Hg = g["f_Hg"]
    gt = g["f0_gt"]
    tmp = g["f_tmp"]
    NTMP = g["ntmp"]
    ps_g0, ps_g1 = PS[0], PS[1]
    ps_dn = [PS[3], PS[4], PS[5], PS[6]]
    w_free = None
    crow_free = None
    cbrd_free = k.fence("vector")
    tmp_free = [None] * NTMP
    tmp_grp = []
    g0_free = k.fence("vector")
    g1_free = k.fence("vector")
    dn_free = [k.fence("vector")] * 4
    hg_free = None
    gt_free = None
    cb_tok = None
    n_acc = 0
    itmp = 0
    for e in range(8):
        k.dma("gpsimd", crow[:], g["cmb_dram"][e:e + 1, :],
              waits=[tcd, crow_free])
        tcr = k.qend("gpsimd")
        cbs = [None]
        for it in range(nt):
            sl = slice(it * TT, (it + 1) * TT)
            tpb2 = k.mm(PS[2], g["ones_row"], crow[0:1, sl], start=True,
                        stop=True,
                        waits=[tcr, ct, cbrd_free, tpes, cbs[-1]])
            cbs.append(k.dve("tensor_scalar", cbrd[:, sl], PS[2],
                             pesb[:, e:e + 1], None, ALU.mult,
                             waits=[tpb2]))
        cb_tok = cbs[-1]
        crow_free = k.fence("tensor")
        for half in range(2):
            h0 = half * HHALF
            for dc in range(NCH):
                k.dma("gpsimd", wg0[:, dc, :],
                      io["gating_wT"][e, 0].rearrange(
                          "(c p) h -> p c h", p=128)[:, dc, h0:h0 + HHALF],
                      waits=[w_free])
                k.dma("gpsimd", wg1[:, dc, :],
                      io["gating_wT"][e, 1].rearrange(
                          "(c p) h -> p c h", p=128)[:, dc, h0:h0 + HHALF],
                      waits=[w_free])
            for hc in range(NHC):
                k.dma("gpsimd", w2b[:, hc, :],
                      io["linear_w"][e].rearrange(
                          "(c p) d -> p c d", p=128)[:, h0 // 128 + hc, :],
                      waits=[w_free])
            w_tok = k.qend("gpsimd")
            for it in range(nt):
                sl = slice(it * TT, (it + 1) * TT)
                buf = it % 2
                for c in range(NCH):
                    k.dma("sync", xt2[:, buf, c, :], xn[c, :, sl],
                          waits=[xt_free[buf]])
                ldx = k.qend("sync")
                gel = []
                for hc in range(NHC):
                    tok0 = tok1 = None
                    for dc in range(NCH):
                        tok0 = k.mm(ps_g0,
                                    wg0[:, dc, hc * 128:hc * 128 + 128],
                                    xt2[:, buf, dc, :],
                                    start=(dc == 0), stop=(dc == NCH - 1),
                                    waits=[w_tok, ldx, g0_free])
                        tok1 = k.mm(ps_g1,
                                    wg1[:, dc, hc * 128:hc * 128 + 128],
                                    xt2[:, buf, dc, :],
                                    start=(dc == 0), stop=(dc == NCH - 1),
                                    waits=[w_tok, ldx, g1_free])
                    tg = k.act(gt[:], ps_g0, AF.Gelu_apprx_tanh,
                               waits=[tok0, gt_free])
                    g0_free = tg
                    th = k.dve("tensor_mul", Hg[:, hc, :], gt[:], ps_g1,
                               waits=[tg, tok1, hg_free])
                    gt_free = th
                    g1_free = th
                    gel.append(th)
                for dp in range(2):
                    tok = None
                    for hc in range(NHC):
                        for dl in range(4):
                            dc = dp * 4 + dl
                            tok = k.mm(ps_dn[dl],
                                       w2b[:, hc, dc * 128:dc * 128 + 128],
                                       Hg[:, hc, :], start=(hc == 0),
                                       stop=(hc == NHC - 1),
                                       waits=[w_tok, gel[hc], dn_free[dl]],
                                       inc=(dl == 3))
                    for dl in range(4):
                        dc = dp * 4 + dl
                        tb = itmp % NTMP
                        itmp += 1
                        t1 = k.dve("tensor_mul", tmp[:, tb, :], ps_dn[dl],
                                   cbrd[:, sl],
                                   waits=[tok, cb_tok, tmp_free[tb]])
                        dn_free[dl] = t1
                        first = (e == 0 and half == 0)
                        k.dma("gpsimd", acc[dc, :, sl], tmp[:, tb, :],
                              waits=[t1],
                              accum_op=(ALU.bypass if first else ALU.add))
                        n_acc += 1
                        if n_acc % (NTMP // 2) == 0:
                            gtok = k.qend("gpsimd")
                            tmp_grp.append(gtok)
                            if len(tmp_grp) >= 2:
                                for j in range(NTMP // 2):
                                    tmp_free[(itmp + j) % NTMP] = tmp_grp[-2]
                hg_free = k.fence("tensor")
                xt_free[buf] = k.fence("tensor")
            w_free = k.fence("tensor")
        cbrd_free = k.fence("vector")

    gp_done = k.qend("gpsimd")
    res_v = k.fence("vector")
    res_t = k.fence("tensor")
    xf = g["ln_xf"]
    accb = g["f_accb"]
    prev = None
    outs = []
    for it in range(nt):
        sl = slice(it * TT, (it + 1) * TT)
        for c in range(NCH):
            k.dma("sync", xf[:, 0, c, :], io["x_in"][c, :, sl],
                  waits=[res_v, res_t, gp_done, prev])
            k.dma("sync", accb[:, c, :], acc[c, :, sl])
        lt = k.qend("sync")
        ts = []
        for c in range(NCH):
            t = k.dve("scalar_tensor_tensor", xf[:, 1, c, :],
                      accb[:, c, :], 0.5, xf[:, 0, c, :], ALU.mult,
                      ALU.add, waits=[lt])
            ts.append(t)
            k.dma("sync", io["x_out"][c, :, sl], xf[:, 1, c, :],
                  waits=[t])
        outs.append(k.qend("sync"))
        prev = [outs[-1]] + ts
    return outs


def build_ffn0_program(T):
    nc = bass.Bass("TRN2", target_bir_lowering=False)
    io = {}
    io["x_in"] = nc.dram_tensor("x_in", [NCH, 128, T], F32,
                                kind="ExternalInput").ap()
    io["x_out"] = nc.dram_tensor("x_out", [NCH, 128, T], F32,
                                 kind="ExternalOutput").ap()
    for n, shp in [("router_w", [D, 8]), ("router_scale_fm", [128, 8]),
                   ("gating_wT", [8, 2, D, 2048]),
                   ("linear_w", [8, 2048, D]), ("pes_row", [1, 8])]:
        io[n] = nc.dram_tensor(n, shp, F32, kind="ExternalInput").ap()
    declare_consts(nc, io)

    g = alloc_common(nc, T)
    g["xn_dram"] = nc.dram_tensor("xn_scr", [NCH, 128, T], BF16).ap()
    g["acc_dram"] = nc.dram_tensor("acc_scr", [NCH, 128, T], BF16).ap()
    g["cmb_dram"] = nc.dram_tensor("cmb_scr", [8, T], F32).ap()
    g["f_gw"] = nc.alloc_sbuf_tensor("f_gw", [128, NCH, 8], BF16).ap()
    g["f0_gwf"] = nc.alloc_sbuf_tensor("f0_gwf", [128, NCH, 8], F32).ap()
    g["f0_rsc"] = nc.alloc_sbuf_tensor("f0_rsc", [128, 8], F32).ap()
    g["f0_var"] = nc.alloc_sbuf_tensor("f0_var", [1, T], F32).ap()
    g["f0_rs"] = nc.alloc_sbuf_tensor("f0_rs", [1, 4, TT], F32).ap()
    g["f0_rcol"] = nc.alloc_sbuf_tensor("f0_rcol", [128, T // 128], F32).ap()
    g["f0_pes"] = nc.alloc_sbuf_tensor("f0_pes", [1, 8], BF16).ap()
    g["f0_pesb"] = nc.alloc_sbuf_tensor("f0_pesb", [128, 8], F32).ap()
    g["f_r8"] = nc.alloc_sbuf_tensor("f_r8", [128, 10, 8], F32).ap()
    g["f_ct8"] = nc.alloc_sbuf_tensor("f_ct8", [8, 128], F32).ap()
    g["f_xt"] = nc.alloc_sbuf_tensor("f_xt", [128, 2, NCH, TT], BF16).ap()
    g["f0_wg0"] = nc.alloc_sbuf_tensor("f0_wg0", [128, NCH, 1024], BF16).ap()
    g["f0_wg1"] = nc.alloc_sbuf_tensor("f0_wg1", [128, NCH, 1024], BF16).ap()
    g["f_w2b"] = nc.alloc_sbuf_tensor("f_w2b", [128, 8, D], BF16).ap()
    g["f_crow"] = nc.alloc_sbuf_tensor("f_crow", [1, T], BF16).ap()
    g["f_cbrd"] = nc.alloc_sbuf_tensor("f_cbrd", [128, T], BF16).ap()
    g["f_Hg"] = nc.alloc_sbuf_tensor("f_Hg", [128, 8, TT], BF16).ap()
    g["f0_gt"] = nc.alloc_sbuf_tensor("f0_gt", [128, TT], BF16).ap()
    g["ntmp"] = 8
    g["f_tmp"] = nc.alloc_sbuf_tensor("f_tmp", [128, 8, TT], BF16).ap()
    g["f_accb"] = nc.alloc_sbuf_tensor("f_accb", [128, NCH, TT], BF16).ap()

    k = KB(nc)
    k.io = io

    def body(name):
        load_consts(k, g)
        outs = emit_ffn0(k, g, io, T)
        finish_program(k, outs)

    build_block(nc, k, body)
    return nc


def prep_ffn0_inputs(xT, W, T):
    d = {
        "x_in": np.ascontiguousarray(xT.reshape(NCH, 128, T)),
        "router_w": np.ascontiguousarray(W["router_w"]),
        "router_scale_fm": fm(W["router_scale"]),
        "gating_wT": np.ascontiguousarray(
            W["gating_w"].transpose(0, 1, 3, 2)),
        "linear_w": np.ascontiguousarray(W["linear_w"]),
        "pes_row": np.ascontiguousarray(
            W["per_expert_scale"].reshape(1, 8)),
    }
    d.update(const_inputs())
    return d



def emit_rmsnorm_tile(k, g, A_dram, tt, w_col_ap, wtok, xg, eps=1e-6):
    sl = slice(tt * TT, (tt + 1) * TT)
    PS = g["PS"]
    at = g["a_tile"]
    sqb = g["a_sq"]
    row = g["ln_rows"]
    rowb = g["ln_rowb"]
    ct = g["const_tok"]
    for c in range(NCH):
        k.dma("sync", at[:, c, :], A_dram[c, :, sl], waits=[g["at_free"]])
    ld = k.qend("sync")
    t_sq = [k.act(sqb[:, c, :], at[:, c, :], AF.Square,
                  waits=[ld, g["asq_free"]]) for c in range(NCH)]
    tok = None
    for c in range(NCH):
        tok = k.mm(PS[6][0:1, :], g["ones_col"], sqb[:, c, :],
                   start=(c == 0), stop=(c == NCH - 1),
                   waits=[t_sq[c], ct, g["rms_ps_free"]])
    eps_ap, eps_tok = g["cc"][float(eps)]
    sd = row[0:1, 4, :]
    rr = row[0:1, 5, :]
    t5 = k.act(sd, PS[6][0:1, :], AF.Sqrt, bias=eps_ap[0:1, :],
               scale=1.0 / D, waits=[tok, eps_tok, g["row_free"]])
    g["rms_ps_free"] = t5
    t6 = k.dve("reciprocal", rr, sd, waits=[t5])
    t8 = k.dve("tensor_copy", rowb[0:1, 0, :], rr, waits=[t6])
    tb1 = k.mm(PS[6], g["ones_row"], rowb[0:1, 0, :], start=True,
               stop=True, waits=[t8, t5, ct, g["brd_free"]])
    g["row_free"] = tb1
    xg_toks = []
    for c in range(NCH):
        xg_toks.append(k.dve("scalar_tensor_tensor", xg[:, c, :],
                             at[:, c, :], w_col_ap[:, c:c + 1], PS[6],
                             ALU.mult, ALU.mult,
                             waits=[tb1, ld, wtok, g["xg_free"]]))
    g["brd_free"] = xg_toks[-1]
    g["rms_ps_free"] = xg_toks[-1]
    g["at_free"] = xg_toks[-1]
    g["asq_free"] = tok
    return xg_toks


def emit_proj_rope(k, g, xg, xg_toks, tt, wbuf, wtok, n_in_ch, out_spec,
                   out_chunks, do_rope, ps_a, ps_b):
    sl = slice(tt * TT, (tt + 1) * TT)
    PS = g["PS"]
    ct = g["const_tok"]
    kind, dst = out_spec
    toks = []
    for oc in range(out_chunks):
        pq = ps_a[oc % 2]
        tok = None
        for c in range(n_in_ch):
            tok = k.mm(pq, wbuf[:, c, oc * 128:oc * 128 + 128],
                       xg[:, c, :], start=(c == 0), stop=(c == n_in_ch - 1),
                       waits=[wtok, xg_toks[min(c, len(xg_toks) - 1)],
                              g["pq_free"][oc % 2]])
        if do_rope:
            qraw = g["qraw"]
            t0 = k.act(qraw[:], pq, AF.Copy, waits=[tok, g["qraw_free"]])
            sqt = g["sqt"]
            t1 = k.act(sqt[:], pq, AF.Square, waits=[tok, g["sqt_free"]])
            tn = k.mm(ps_b[0][0:2, :], g["ind2"], sqt[:], start=True,
                      stop=True, waits=[t1, ct, g["psn_free"]])
            cc6, tk6 = g["cc"][1e-6]
            sd2 = g["row2"][0:2, 0, :]
            rr2 = g["row2"][0:2, 1, :]
            rb2 = g["row2b"][0:2, 0, :]
            t2 = k.act(sd2, ps_b[0][0:2, :], AF.Sqrt, bias=cc6[0:2, :],
                       scale=1.0 / 64, waits=[tn, tk6, g["row2_free"]])
            g["psn_free"] = t2
            t3 = k.dve("reciprocal", rr2, sd2, waits=[t2])
            t4 = k.dve("tensor_copy", rb2, rr2, waits=[t3])
            tb = k.mm(ps_b[1], g["ind2T"], rb2, start=True, stop=True,
                      waits=[t4, ct, g["psb_free"]])
            g["row2_free"] = tb
            trot = k.mm(ps_b[2], g["rotM"], qraw[:], start=True, stop=True,
                        waits=[t0, ct, g["rot_free"]])
            rp1 = g["rp1"]
            rp2 = g["rp2"]
            t5 = k.dve("tensor_mul", rp1[:], qraw[:], g["cos_t"][:],
                       waits=[t0, g["tab_tok"], g["rp1_free"]])
            g["qraw_free"] = [trot, t5]
            t7 = k.dve("tensor_mul", rp2[:], ps_b[2], g["sin_t"][:],
                       waits=[trot, g["tab_tok"], g["rp2_free"]])
            g["rot_free"] = t7
            t8 = k.dve("tensor_add", rp1[:], rp1[:], rp2[:],
                       waits=[t5, t7])
            g["rp2_free"] = t8
            g["sqt_free"] = tn
            g["pq_free"][oc % 2] = [t0, t1]
            if kind == "T":
                t9 = k.dve("tensor_mul", dst[:, oc, sl], rp1[:], ps_b[1],
                           waits=[t8, tb, g["osb_free"]])
                toks.append(t9)
            else:
                stb = oc % 2
                t9 = k.dve("tensor_mul", g["qst"][:, stb, :], rp1[:],
                           ps_b[1], waits=[t8, tb, g["qst_free"][stb]])
                k.dma("sync", dst[oc, :, sl], g["qst"][:, stb, :],
                      waits=[t9])
                toks.append(k.qend("sync"))
                g["qst_free"][stb] = toks[-1]
            g["rp1_free"] = t9
            g["psb_free"] = t9
        else:
            if kind == "T":
                t = k.act(dst[:, oc, sl], pq, AF.Copy,
                          waits=[tok, g["osb_free"]])
            else:
                t = k.act(dst[:, oc, :], pq, AF.Copy,
                          waits=[tok, g["osb_free"]])
            g["pq_free"][oc % 2] = t
            toks.append(t)
    return toks


def emit_attn_core(k, g, q_tok, k_sb, vT_sb, o_col, head, qt, kc_lo,
                   kc_hi, mask_of, T):
    PS = g["PS"]
    ct = g["const_tok"]
    kvh = head // 4
    qrow = (head % 2) * 64
    qch = head // 2
    krow = (kvh % 2) * 64
    kch = kvh // 2
    Et = g["E_t"]
    qtile = g["q_tile"]
    ps_s = [PS[0], PS[3]]
    NEB = 3
    tok_o = None
    cc8, tk8 = g["cc"][-8.0]
    for i, kc in enumerate(range(kc_lo, kc_hi + 1)):
        eb = i % NEB
        sbk = i % 2
        ts = k.mm(ps_s[sbk], k_sb[krow:krow + 64, kch,
                                  kc * 128:kc * 128 + 128],
                  qtile[krow:krow + 64, :], start=True, stop=True,
                  waits=[g["qk_done"], q_tok, g["pss_free"][sbk]])
        te = k.act(Et[:, eb, :], ps_s[sbk], AF.Exp, bias=cc8[:],
                   scale=0.125, waits=[ts, tk8, g["et_free"][eb]])
        g["pss_free"][sbk] = te
        mi = mask_of(kc, qt)
        if mi is not None:
            te = k.dve("tensor_mul", Et[:, eb, :], Et[:, eb, :],
                       g["amask"][:, mi, :], waits=[te, ct])
        tok_o = k.mm(PS[1][0:65, :], vT_sb[:, kc, kvh, :], Et[:, eb, :],
                     start=(i == 0), stop=(kc == kc_hi),
                     waits=[te, g["vT_done"], g["pso_free"]])
        g["et_free"][eb] = tok_o
    dv = g["row2"][0:1, 2, :]
    dvb = g["row2b"][0:1, 1, :]
    t1 = k.dve("reciprocal", dv, PS[1][64:65, :],
               waits=[tok_o, g["dv_free"]])
    t2 = k.dve("tensor_copy", dvb, dv, waits=[t1])
    tb = k.mm(PS[2][0:64, :], g["ones_row"][0:1, 0:64], dvb, start=True,
              stop=True, waits=[t2, ct, g["psb2_free"]])
    g["dv_free"] = tb
    bs = g["brd_sb"]
    t3 = k.act(bs[:], PS[2][0:64, :], AF.Copy, waits=[tb, g["bs_free"]])
    g["psb2_free"] = t3
    t4 = k.dve("tensor_mul", o_col[qrow:qrow + 64, qch, :], PS[1][0:64, :],
               bs[:], waits=[t3, tok_o, g["ocol_free"]])
    g["pso_free"] = t4
    g["bs_free"] = t4
    return t4


def emit_attn(k, g, io, T):
    nt = T // TT
    A = g["xn_dram"]
    PS = g["PS"]
    ct = g["const_tok"]

    ln_done = emit_layernorm(k, g, io["x_in"], A, T)

    k.dma("sync", g["norm1"][:], io["norm1_fm"][:])
    k.dma("sync", g["norm2"][:], io["norm2_fm"][:])
    k.dma("sync", g["ind2"][:], io["c_ind2"][:])
    k.dma("sync", g["ind2T"][:], io["c_ind2T"][:])
    k.dma("sync", g["rotM"][:], io["c_rot"][:])
    k.dma("sync", g["amask"][:], io["c_amask"][:])
    k.dma("sync", g["ident_bf"][:], io["c_ident_bf"][:])
    wtok0 = k.qend("sync")

    def attn_pass(which):
        glob = which == "g"
        if glob:
            k.dma("gpsimd", g["w_q"][:],
                  io["g_wq"].rearrange("(c p) o -> p c o", p=128),
                  waits=[g["w_war"]])
            k.dma("gpsimd", g["w_kvd"][:],
                  io["g_kv_down"].rearrange("(c p) o -> p c o", p=128))
            k.dma("gpsimd", g["w_kup"][:],
                  io["g_k_up"].rearrange("(c p) o -> p c o", p=128))
            k.dma("gpsimd", g["w_vup"][:],
                  io["g_v_up"].rearrange("(c p) o -> p c o", p=128))
            k.dma("gpsimd", g["w_o"][:],
                  io["g_wo"].rearrange("(c p) o -> p c o", p=128))
        else:
            k.dma("gpsimd", g["w_q"][:],
                  io["l_wq"].rearrange("(c p) o -> p c o", p=128),
                  waits=[g["w_war"]])
            k.dma("gpsimd", g["w_kl"][:],
                  io["l_wk"].rearrange("(c p) o -> p c o", p=128))
            k.dma("gpsimd", g["w_vl"][:],
                  io["l_wv"].rearrange("(c p) o -> p c o", p=128))
            k.dma("gpsimd", g["w_o"][:],
                  io["l_wo"].rearrange("(c p) o -> p c o", p=128))
        wtok = k.qend("gpsimd")
        pf = [k.fence("scalar"), k.fence("vector")]
        for nm_ in ("psn_free", "psb_free", "rot_free", "rms_ps_free",
                    "row_free", "brd_free"):
            g[nm_] = [g[nm_], pf]
        g["pq_free"] = [[g["pq_free"][0], pf], [g["pq_free"][1], pf]]
        wcol = g["norm1"] if glob else g["norm2"]
        cosd = io["c_cos_g"] if glob else io["c_cos_l"]
        sind = io["c_sin_g"] if glob else io["c_sin_l"]
        for tt in range(nt):
            sl = slice(tt * TT, (tt + 1) * TT)
            xg = g["xg_t"]
            xg_toks = emit_rmsnorm_tile(k, g, A, tt, wcol, wtok0, xg)
            k.dma("sync", g["cos_t"][:], cosd[:, sl],
                  waits=[g["tab_war"]])
            k.dma("sync", g["sin_t"][:], sind[:, sl])
            g["tab_tok"] = k.qend("sync")
            emit_proj_rope(k, g, xg, xg_toks, tt, g["w_q"], wtok, NCH,
                           ("dram", g["q_dram"]), NCH, True,
                           [PS[0], PS[1]], [PS[2], PS[3], PS[4]])
            if glob:
                lt = emit_proj_rope(k, g, xg, xg_toks, tt, g["w_kvd"],
                                    wtok, NCH, ("tile", g["lat_t"]), 2,
                                    False, [PS[0], PS[1]], None)
                emit_proj_rope(k, g, g["lat_t"], lt, tt, g["w_kup"],
                               wtok, 2, ("T", g["k_sb"]), 2, True,
                               [PS[0], PS[1]], [PS[2], PS[3], PS[4]])
                vt = emit_proj_rope(k, g, g["lat_t"], lt, tt,
                                    g["w_vup"], wtok, 2,
                                    ("tile", g["v_t"]), 2, False,
                                    [PS[0], PS[1]], None)
            else:
                emit_proj_rope(k, g, xg, xg_toks, tt, g["w_kl"], wtok,
                               NCH, ("T", g["k_sb"]), 2, True,
                               [PS[0], PS[1]], [PS[2], PS[3], PS[4]])
                vt = emit_proj_rope(k, g, xg, xg_toks, tt, g["w_vl"],
                                    wtok, NCH, ("tile", g["v_t"]), 2,
                                    False, [PS[0], PS[1]], None)
            g["xg_free"] = k.fence("tensor")
            g["tab_war"] = k.fence("vector")
            for tc in range(TT // 128):
                kc = tt * (TT // 128) + tc
                for kvh in range(4):
                    vrow = (kvh % 2) * 64
                    vch = kvh // 2
                    tpv = k.tp(g["PSB"][:, 0:64],
                               g["v_t"][vrow:vrow + 64, vch,
                                        tc * 128:tc * 128 + 128],
                               g["ident_bf"][vrow:vrow + 64,
                                              vrow:vrow + 64],
                               waits=[vt[vch], wtok0, g["psv_free"]])
                    g["psv_free"] = k.dve(
                        "tensor_copy", g["vT_sb"][:, kc, kvh, 0:64],
                        g["PSB"][:, 0:64], waits=[tpv, g["vT_war"]])
            g["osb_free"] = k.fence("tensor")
        k.pool("memset", g["vT_sb"][:, :, :, 64:65], 1.0,
               waits=[k.fence("vector"), k.fence("tensor")])
        g["vT_done"] = k.fence("gpsimd")
        qk_done = [k.fence("vector"), k.fence("scalar")]
        g["qk_done"] = qk_done
        g["pss_free"] = [qk_done, qk_done]
        g["et_free"] = [qk_done, qk_done, qk_done]
        g["pso_free"] = qk_done
        g["dv_free"] = qk_done
        g["psb2_free"] = qk_done
        for qt in range(nt):
            for head in range(16):
                qch = head // 2
                qrow = (head % 2) * 64
                kvh_ = head // 4
                kr_ = (kvh_ % 2) * 64
                k.dma("sync", g["q_tile"][kr_:kr_ + 64, :],
                      g["q_dram"][qch, qrow:qrow + 64,
                                  qt * TT:(qt + 1) * TT],
                      waits=[g["qt_war"]])
                q_tok = k.qend("sync")
                if glob:
                    kc_lo, kc_hi = 0, qt * (TT // 128) + 3

                    def mask_of(kc, qt_):
                        j = kc - qt_ * 4
                        return j if 0 <= j <= 3 else None
                else:
                    kc_lo, kc_hi = max(0, qt * 4 - 2), qt * 4 + 3

                    def mask_of(kc, qt_):
                        return 4 + (kc - qt_ * 4 + 2)
                emit_attn_core(k, g, q_tok, g["k_sb"], g["vT_sb"],
                               g["o_col"], head, qt, kc_lo, kc_hi,
                               mask_of, T)
                g["qt_war"] = k.fence("tensor")
            ocol_done = k.fence("vector")
            sl = slice(qt * TT, (qt + 1) * TT)
            for oc in range(NCH):
                pd = PS[5 + oc % 2]
                tok = None
                for c in range(NCH):
                    tok = k.mm(pd, g["w_o"][:, c, oc * 128:oc * 128 + 128],
                               g["o_col"][:, c, :], start=(c == 0),
                               stop=(c == NCH - 1),
                               waits=[wtok, ocol_done,
                                      g["wo_free"][oc % 2]])
                tb = g["acc_cnt"] % g["ntmp"]
                t1 = k.dve("tensor_copy", g["f_tmp"][:, tb, :], pd,
                           waits=[tok, g["tmp_free"][tb]])
                g["wo_free"][oc % 2] = t1
                k.dma("gpsimd", A[oc, :, sl], g["f_tmp"][:, tb, :],
                      waits=[t1], accum_op=ALU.add)
                g["acc_cnt"] += 1
                if g["acc_cnt"] % 2 == 0:
                    gtok = k.qend("gpsimd")
                    g["tmp_grp"].append(gtok)
                    if len(g["tmp_grp"]) >= 2:
                        for j in range(2):
                            g["tmp_free"][(g["acc_cnt"] + j) %
                                          g["ntmp"]] = g["tmp_grp"][-2]
            g["ocol_free"] = k.fence("tensor")
        gp = k.qend("gpsimd")
        g["at_free"] = [gp, g["at_free"]]
        g["rms_ps_free"] = k.fence("vector")
        g["row_free"] = k.fence("vector")
        g["brd_free"] = k.fence("vector")
        g["w_war"] = k.fence("tensor")
        g["vT_war"] = k.fence("tensor")
        g["osb_free"] = k.fence("tensor")
        g["qk_done"] = None

    for nm, v in [("at_free", None), ("asq_free", None),
                  ("rms_ps_free", None), ("row_free", None),
                  ("brd_free", None), ("xg_free", None),
                  ("osb_free", None), ("qraw_free", None),
                  ("sqt_free", None), ("psn_free", None),
                  ("row2_free", None), ("psb_free", None),
                  ("rot_free", None), ("rp1_free", None),
                  ("rp2_free", None), ("psv_free", None),
                  ("vT_war", None), ("vT_done", None), ("qk_done", None),
                  ("pso_free", None), ("dv_free", None),
                  ("psb2_free", None), ("bs_free", None),
                  ("ocol_free", None), ("qt_war", None),
                  ("tab_war", None), ("tab_tok", None), ("w_war", None),
                  ("acc_cnt", 0)]:
        g[nm] = v
    g["pss_free"] = [None, None]
    g["et_free"] = [None, None, None]
    g["wo_free"] = [None, None]
    g["pq_free"] = [None, None]
    g["qst_free"] = [None, None]
    g["tmp_free"] = [None] * g["ntmp"]
    g["tmp_grp"] = []

    attn_pass("g")
    attn_pass("l")

    gp_done = k.qend("gpsimd")
    res_v = k.fence("vector")
    xf = g["ln_xf"]
    accb = g["f_accb"]
    prev = None
    outs = []
    for it in range(nt):
        sl = slice(it * TT, (it + 1) * TT)
        for c in range(NCH):
            k.dma("sync", xf[:, 0, c, :], io["x_in"][c, :, sl],
                  waits=[res_v, gp_done, prev])
            k.dma("sync", accb[:, c, :], A[c, :, sl])
        lt = k.qend("sync")
        ts = []
        for c in range(NCH):
            t = k.dve("scalar_tensor_tensor", xf[:, 1, c, :],
                      accb[:, c, :], 1.0, xf[:, 0, c, :], ALU.mult,
                      ALU.add, waits=[lt])
            ts.append(t)
            k.dma("sync", io["x_out"][c, :, sl], xf[:, 1, c, :],
                  waits=[t])
        outs.append(k.qend("sync"))
        prev = [outs[-1]] + ts
    return outs


def build_attn_program(T):
    nc = bass.Bass("TRN2", target_bir_lowering=False)
    io = {}
    io["x_in"] = nc.dram_tensor("x_in", [NCH, 128, T], F32,
                                kind="ExternalInput").ap()
    io["x_out"] = nc.dram_tensor("x_out", [NCH, 128, T], F32,
                                 kind="ExternalOutput").ap()
    for n, shp in [("norm1_fm", [128, 8]), ("norm2_fm", [128, 8]),
                   ("g_wq", [D, D]), ("g_kv_down", [D, 256]),
                   ("g_k_up", [256, 256]), ("g_v_up", [256, 256]),
                   ("g_wo", [D, D]), ("l_wq", [D, D]), ("l_wk", [D, 256]),
                   ("l_wv", [D, 256]), ("l_wo", [D, D])]:
        io[n] = nc.dram_tensor(n, shp, F32, kind="ExternalInput").ap()
    for n, shp in [("c_cos_g", [128, T]), ("c_sin_g", [128, T]),
                   ("c_cos_l", [128, T]), ("c_sin_l", [128, T]),
                   ("c_rot", [128, 128]), ("c_ind2", [128, 2]),
                   ("c_ind2T", [2, 128]), ("c_amask", [128, 10, TT]),
                   ("c_ident_bf", [128, 128])]:
        io[n] = nc.dram_tensor(n, shp, BF16, kind="ExternalInput").ap()
    declare_consts(nc, io)

    g = alloc_common(nc, T)
    g["xn_dram"] = nc.dram_tensor("xn_scr", [NCH, 128, T], BF16).ap()
    g["q_dram"] = nc.dram_tensor("q_scr", [NCH, 128, T], BF16).ap()
    g["norm1"] = nc.alloc_sbuf_tensor("norm1", [128, 8], F32).ap()
    g["norm2"] = nc.alloc_sbuf_tensor("norm2", [128, 8], F32).ap()
    g["ind2"] = nc.alloc_sbuf_tensor("ind2", [128, 2], BF16).ap()
    g["ind2T"] = nc.alloc_sbuf_tensor("ind2T", [2, 128], BF16).ap()
    g["rotM"] = nc.alloc_sbuf_tensor("rotM", [128, 128], BF16).ap()
    g["ident_bf"] = nc.alloc_sbuf_tensor("ident_bf", [128, 128],
                                         BF16).ap()
    g["cos_t"] = nc.alloc_sbuf_tensor("cos_t", [128, TT], BF16).ap()
    g["sin_t"] = nc.alloc_sbuf_tensor("sin_t", [128, TT], BF16).ap()
    g["amask"] = nc.alloc_sbuf_tensor("amask", [128, 10, TT], BF16).ap()
    g["a_tile"] = g["ln_xb"]
    g["a_sq"] = g["ln_sq"]
    g["xg_t"] = nc.alloc_sbuf_tensor("xg_t", [128, NCH, TT], BF16).ap()
    g["row2"] = nc.alloc_sbuf_tensor("row2", [2, 4, TT], F32).ap()
    g["row2b"] = nc.alloc_sbuf_tensor("row2b", [2, 2, TT], BF16).ap()
    g["qraw"] = nc.alloc_sbuf_tensor("qraw", [128, TT], BF16).ap()
    g["sqt"] = nc.alloc_sbuf_tensor("sqt", [128, TT], BF16).ap()
    g["rp1"] = nc.alloc_sbuf_tensor("rp1", [128, TT], F32).ap()
    g["rp2"] = nc.alloc_sbuf_tensor("rp2", [128, TT], F32).ap()
    g["qst"] = nc.alloc_sbuf_tensor("qst", [128, 2, TT], BF16).ap()
    g["q_tile"] = nc.alloc_sbuf_tensor("q_tile", [128, TT], BF16).ap()
    g["k_sb"] = nc.alloc_sbuf_tensor("k_sb", [128, 2, T], BF16).ap()
    g["v_t"] = nc.alloc_sbuf_tensor("v_t", [128, 2, TT], BF16).ap()
    g["lat_t"] = nc.alloc_sbuf_tensor("lat_t", [128, 2, TT], BF16).ap()
    g["vT_sb"] = nc.alloc_sbuf_tensor("vT_sb", [128, T // 128, 4, 65],
                                      BF16).ap()
    g["E_t"] = nc.alloc_sbuf_tensor("E_t", [128, 3, TT], BF16).ap()
    g["brd_sb"] = nc.alloc_sbuf_tensor("brd_sb", [64, TT], F32).ap()
    g["o_col"] = nc.alloc_sbuf_tensor("o_col", [128, NCH, TT], BF16).ap()
    g["ntmp"] = 4
    g["f_tmp"] = nc.alloc_sbuf_tensor("f_tmp", [128, 4, TT], BF16).ap()
    g["f_accb"] = g["xg_t"]
    g["w_q"] = nc.alloc_sbuf_tensor("w_q", [128, NCH, D], BF16).ap()
    g["w_kvd"] = nc.alloc_sbuf_tensor("w_kvd", [128, NCH, 256], BF16).ap()
    g["w_kup"] = nc.alloc_sbuf_tensor("w_kup", [128, 2, 256], BF16).ap()
    g["w_vup"] = nc.alloc_sbuf_tensor("w_vup", [128, 2, 256], BF16).ap()
    g["w_kl"] = nc.alloc_sbuf_tensor("w_kl", [128, NCH, 256], BF16).ap()
    g["w_vl"] = nc.alloc_sbuf_tensor("w_vl", [128, NCH, 256], BF16).ap()
    g["w_o"] = nc.alloc_sbuf_tensor("w_o", [128, NCH, D], BF16).ap()

    k = KB(nc)
    k.io = io

    def body(name):
        load_consts(k, g)
        outs = emit_attn(k, g, io, T)
        finish_program(k, outs)

    build_block(nc, k, body)
    return nc


def rope_tables(base, T):
    j = np.arange(32)
    inv = base ** (-(2.0 * j) / 64.0)
    th = np.arange(T)[:, None] * inv[None, :]
    cos = np.cos(th)
    sin = np.sin(th)
    p = np.arange(128)
    pj = (p % 64) // 2
    cosF = cos[:, pj].T
    sgn = np.where(p % 2 == 0, -1.0, 1.0)
    sinF = sin[:, pj].T * sgn[:, None]
    return (np.ascontiguousarray(cosF).astype(BF),
            np.ascontiguousarray(sinF).astype(BF))


def attn_masks():
    p = np.arange(128)[:, None]
    c = np.arange(TT)[None, :]
    m = np.zeros((128, 10, TT), np.float32)
    for j in range(4):
        m[:, j, :] = (c >= j * 128 + p)
    for jj in range(6):
        j = jj - 2
        d = c - j * 128 - p
        m[:, 4 + jj, :] = (d >= 0) & (d < 256)
    return m.astype(BF)


def prep_attn_inputs(xT, W, T):
    cg, sg = rope_tables(1e6, T)
    cl, sl_ = rope_tables(1e4, T)
    rot = np.zeros((128, 128), np.float32)
    rot[np.arange(128), np.arange(128) ^ 1] = 1.0
    ind2 = np.zeros((128, 2), np.float32)
    ind2[:64, 0] = 1.0
    ind2[64:, 1] = 1.0
    d = {
        "x_in": np.ascontiguousarray(xT.reshape(NCH, 128, T)),
        "norm1_fm": fm(W["norm1_w"]),
        "norm2_fm": fm(W["norm2_w"]),
        "g_wq": W["g_wq"], "g_kv_down": W["g_kv_down"],
        "g_k_up": W["g_k_up"], "g_v_up": W["g_v_up"], "g_wo": W["g_wo"],
        "l_wq": W["l_wq"], "l_wk": W["l_wk"], "l_wv": W["l_wv"],
        "l_wo": W["l_wo"],
        "c_cos_g": cg, "c_sin_g": sg, "c_cos_l": cl, "c_sin_l": sl_,
        "c_rot": rot.astype(BF), "c_ind2": ind2.astype(BF),
        "c_ind2T": np.ascontiguousarray(ind2.T).astype(BF),
        "c_amask": attn_masks(),
        "c_ident_bf": np.eye(128, dtype=np.float32).astype(BF),
    }
    d = {k_: (np.ascontiguousarray(v) if isinstance(v, np.ndarray) else v)
         for k_, v in d.items()}
    d.update(const_inputs())
    return d



def emit_mamba(k, g, io, T):
    nt = T // TT
    NC32 = T // 128
    A = g["xn_dram"]
    PS = g["PS"]
    PSB = g["PSB"]
    ct = g["const_tok"]

    ln_done = emit_layernorm(k, g, io["x_in"], A, T)

    k.dma("sync", g["normw"][:], io["norm_fm"][:])
    k.dma("sync", g["ident_bf"][:], io["c_ident_bf"][:])
    k.dma("sync", g["trimask"][:], io["c_trimask"][:])
    wtok0 = k.qend("sync")

    for nm in ["at_free", "asq_free", "rms_ps_free", "row_free",
               "brd_free", "xg_free"]:
        g[nm] = None
    xm_st = None
    for tt in range(nt):
        sl = slice(tt * TT, (tt + 1) * TT)
        xg_toks = emit_rmsnorm_tile(k, g, A, tt, g["normw"], wtok0,
                                    g["xg_t"], eps=1e-5)
        for c in range(NCH):
            k.dma("sync", g["xm_dram"][c, :, sl], g["xg_t"][:, c, :],
                  waits=[xg_toks[c]])
        xm_st = k.qend("sync")
        g["xg_free"] = xm_st

    NOC = 19

    def direction(dirn):
        fwd = dirn == 0
        pre = "f_" if fwd else "b_"
        ic0 = 0 if fwd else 4
        k.dma("gpsimd", g["m_win"][:],
              io[pre + "in_wT"].rearrange("(c p) o -> p c o", p=128),
              waits=[g["mw_war"]])
        k.dma("sync", g["m_convb"][:], io[pre + "conv_b_fm"][:])
        k.dma("sync", g["m_dtb"][:], io[pre + "dt_bias"][:])
        k.dma("sync", g["m_alog"][:], io[pre + "A_log"][:])
        k.dma("sync", g["m_drep"][:], io[pre + "D_fm"][:])
        k.dma("sync", g["m_gn"][:], io[pre + "gnorm_fm"][:])
        sw = k.qend("sync")
        wtok = k.qend("gpsimd")

        xt2 = g["f_xt"]
        xt_free = [g["xt_war"], g["xt_war"]]
        ev_free = [g["ev_war"]] * 2
        dt_free = g["dt_war"]
        st_tok = None
        for tt in range(nt):
            sl = slice(tt * TT, (tt + 1) * TT)
            buf = tt % 2
            for i in range(4):
                k.dma("sync", xt2[:, buf, i, :],
                      g["xm_dram"][ic0 + i, :, sl],
                      waits=[xm_st, xt_free[buf]])
            ldx = k.qend("sync")
            mm_last = None
            for oc in range(NOC):
                rows = 16 if oc == 18 else 128
                pq = PS[oc % 2]
                tok = None
                for i in range(4):
                    tok = k.mm(pq[0:rows, :],
                               g["m_win"][:, i, oc * 128:oc * 128 + rows],
                               xt2[:, buf, i, :], start=(i == 0),
                               stop=(i == 3),
                               waits=[wtok, ldx, ev_free[oc % 2]])
                mm_last = tok
                if oc == 18:
                    ev = k.dve("tensor_copy", g["m_sc16"][0:16, sl],
                               pq[0:16, :], waits=[tok, dt_free])
                    ev_free[oc % 2] = ev
                else:
                    ev = k.act(g["m_evt"][:, oc % 2, :], pq, AF.Copy,
                               waits=[tok, ev_free[oc % 2]])
                    dst = (g["z_dram"] if oc < 8 else g["xbc_dram"])
                    di = oc if oc < 8 else oc - 8
                    k.dma("sync", dst[di, :, sl],
                          g["m_evt"][:, oc % 2, :], waits=[ev])
                    ev_free[oc % 2] = k.qend("sync")
            xt_free[buf] = mm_last
        inp_done = k.qend("sync")

        dtv = k.fence("vector")
        sc = g["m_sc16"]
        dt_sb = g["m_dt"]
        ldA = g["m_ldA"]
        cc1, tk1 = g["cc"][1.0]
        t1 = k.dve("tensor_scalar_add", sc[0:16, :], sc[0:16, :],
                   g["m_dtb"][0:16, :], waits=[dtv, sw, g["dt2_war"]])
        t2 = k.dve("tensor_scalar_max", dt_sb[0:16, :], sc[0:16, :],
                   0.0, waits=[t1, g["dt3_war"]])
        t3 = k.act(ldA[0:16, :], sc[0:16, :], AF.Abs,
                   waits=[t1, g["ldA_war"]])
        t4 = k.act(sc[0:16, :], ldA[0:16, :], AF.Exp, scale=-1.0,
                   waits=[t3, t2])
        t5 = k.act(ldA[0:16, :], sc[0:16, :], AF.Ln,
                   bias=cc1[0:16, :], waits=[t4, tk1])
        t6 = k.dve("tensor_add", dt_sb[0:16, :], dt_sb[0:16, :],
                   ldA[0:16, :], waits=[t5, t2])
        ta = k.act(g["m_a16"][0:16, 0:1], g["m_alog"][0:16, :], AF.Exp,
                   waits=[sw, t5, g["a16_war"]])
        tb = k.dve("tensor_single_scalar", g["m_a16"][0:16, 1:2],
                   g["m_a16"][0:16, 0:1], -1.0, ALU.mult, waits=[ta])
        tl = k.dve("tensor_scalar", ldA[0:16, :], dt_sb[0:16, :],
                   g["m_a16"][0:16, 1:2], None, ALU.mult,
                   waits=[tb, t6])
        g["dt2_war"] = k.fence("scalar")
        g["dt3_war"] = tl
        g["a16_war"] = tl

        k.dma("gpsimd", g["m_conv"][:],
              io[pre + "conv_diag"].rearrange("a k i j -> i (a k) j"),
              waits=[k.fence("tensor")])
        wtok = k.qend("gpsimd")
        xbt = g["m_xbt"]
        cv_free = [k.fence("vector")] * 2
        xb_free = [None, None]
        for tt in range(nt):
            sl = slice(tt * TT, (tt + 1) * TT)
            buf = 0
            edge = (tt == 0) if fwd else (tt == nt - 1)
            mz = None
            if edge:
                pc = 0 if fwd else 512 + 3 - 3
                mz = k.pool("memset",
                            xbt[:, buf, :, (0 if fwd else 512):515] if not fwd
                            else xbt[:, buf, :, 0:3], 0.0,
                            waits=[xb_free[buf]])
            for cch in range(10):
                if fwd:
                    lo = tt * TT - 3
                    if tt == 0:
                        k.dma("sync", xbt[:, buf, cch, 3:515],
                              g["xbc_dram"][cch, :, sl],
                              waits=[xb_free[buf], inp_done, mz])
                    else:
                        k.dma("sync", xbt[:, buf, cch, 0:515],
                              g["xbc_dram"][cch, :, lo:lo + 515],
                              waits=[xb_free[buf], inp_done])
                else:
                    if tt == nt - 1:
                        k.dma("sync", xbt[:, buf, cch, 0:512],
                              g["xbc_dram"][cch, :, sl],
                              waits=[xb_free[buf], inp_done, mz])
                    else:
                        k.dma("sync", xbt[:, buf, cch, 0:515],
                              g["xbc_dram"][cch, :, tt * TT:tt * TT + 515],
                              waits=[xb_free[buf], inp_done])
            ldb = k.qend("sync")
            mm_l = None
            for cch in range(10):
                pq = PS[cch % 2]
                tok = None
                for kk in range(4):
                    off = kk if fwd else (3 - kk)
                    tok = k.mm(pq, g["m_conv"][:, cch * 4 + kk, :],
                               xbt[:, buf, cch, off:off + TT],
                               start=(kk == 0), stop=(kk == 3),
                               waits=[wtok, ldb, cv_free[cch % 2]])
                mm_l = tok
                ev = k.act(g["m_evt"][:, cch % 2, :], pq, AF.Silu,
                           bias=g["m_convb"][:, cch:cch + 1],
                           waits=[tok, sw, cv_free[cch % 2]])
                cv_free[cch % 2] = ev
                if cch < 8:
                    k.dma("sync", g["xsil_dram"][cch, :, sl],
                          g["m_evt"][:, cch % 2, :], waits=[ev])
                    cv_free[cch % 2] = k.qend("sync")
                elif cch == 8:
                    cv_free[cch % 2] = k.dve(
                        "tensor_copy", g["m_B"][:, sl],
                        g["m_evt"][:, cch % 2, :],
                        waits=[ev, g["B_war"]])
                else:
                    cv_free[cch % 2] = k.dve(
                        "tensor_copy", g["m_C"][:, sl],
                        g["m_evt"][:, cch % 2, :],
                        waits=[ev, g["C_war"]])
            xb_free[buf] = mm_l
        conv_done = [k.qend("sync"), k.fence("vector")]

        hz1 = k.pool("memset", g["m_hT"][:], 0.0, waits=[g["hT_war"]])
        hz2 = k.pool("memset", g["m_hTb"][:], 0.0, waits=[g["hT_war"]])
        k.pool("memset", g["m_z16"][:], 0.0)
        z16tok = k.qend("gpsimd") if False else k.fence("gpsimd")
        hT = g["m_hT"]
        hTb = g["m_hTb"]
        chunks = range(NC32) if fwd else range(NC32 - 1, -1, -1)
        wcol_idx = 127 if fwd else 0
        trimask = g["trimask"][:, (0 if fwd else 1) * 128:
                               (1 if fwd else 2) * 128]
        ht_tok = [hz1, hz2]
        cum_free = None
        ct_free = None
        dtt_free = None
        bt_free = None
        g_free = None
        ps6_free = None
        xh_free = [None, None]
        ps3_free = [None, None]
        psy_free = None
        psy2_free = None
        psh_free = None
        psb_free = None
        n1_free = [None, None]
        d_free = [None, None]
        w_free2 = [None, None]
        e_free = [None, None]
        eb_free = None
        eb2_free = [None, None]
        dtx_free = [None, None]
        p_free = [None, None]
        bw_free = [None, None]
        y1_free = [None, None]
        y2_free = [None, None]
        ycol_free = None
        first_chunk = True
        for ci, c in enumerate(chunks):
            cb = ci % 2
            csl = slice(c * 128, c * 128 + 128)
            for ch8 in range(NCH):
                k.dma("sync", g["m_xh"][:, cb, ch8, 0:128],
                      g["xsil_dram"][ch8, :, csl],
                      waits=[conv_done[0], xh_free[cb]])
            xht = k.qend("sync")
            cum = g["m_cum"]
            t1 = k.dve("tensor_tensor_scan", cum[0:16, 0, :],
                       ldA[0:16, csl], g["m_z16"][0:16, :], 0.0,
                       ALU.add, ALU.add,
                       waits=[tl, z16tok, cum_free])
            if not fwd:
                s1 = k.dve("tensor_scalar", cum[0:16, 1, :],
                           cum[0:16, 0, :], cum[0:16, 0, 127:128], None,
                           ALU.subtract, waits=[t1])
                s2 = k.dve("tensor_sub", cum[0:16, 0, :],
                           ldA[0:16, csl], cum[0:16, 1, :], waits=[s1])
                t1 = s2
            cum_free = t1
            tc1 = k.tp(PS[6][0:128, 0:16], cum[0:16, 0, :],
                       g["ident_f32"][0:16, 0:16],
                       waits=[t1, ct, g_free, ps6_free])
            tc2 = k.dve("tensor_copy", g["m_cumT"][:, 0:16],
                        PS[6][0:128, 0:16], waits=[tc1, ct_free])
            tn1 = k.dve("tensor_single_scalar", g["m_cumT"][:, 16:32],
                        g["m_cumT"][:, 0:16], -1.0, ALU.mult,
                        waits=[tc2])
            td1 = k.tp(PS[6][0:128, 16:32], dt_sb[0:16, csl],
                       g["ident_f32"][0:16, 0:16],
                       waits=[t6, ct, g_free, ps6_free])
            td2 = k.dve("tensor_copy", g["m_dtT"][:, 0:16],
                        PS[6][0:128, 16:32], waits=[td1, dtt_free])
            tb1 = k.tp(PSB[:, 0:128], g["m_B"][:, csl],
                       g["ident_bf"][:, :],
                       waits=[conv_done[1], wtok0, ct, psb_free])
            tb2 = k.dve("tensor_copy", g["m_BT"][:], PSB[:, 0:128],
                        waits=[tb1, bt_free])
            tg1 = k.mm(PS[6][:, 0:128], g["m_B"][:, csl],
                       g["m_C"][:, csl], start=True, stop=True,
                       waits=[conv_done[1], g_free, tc2, td2])
            tg2 = k.dve("tensor_mul", g["m_G"][:], PS[6][:, 0:128],
                        trimask, waits=[tg1, wtok0, g["G_war"]])
            g_free = tg2
            ps6_free = tg2
            ps_t2 = [PS[0], PS[5]]
            for h in range(16):
                hp = h % 2
                hr = (h % 2) * 64
                hc8 = h // 2
                a1 = k.dve("tensor_scalar", g["m_n1"][:, hp, :],
                           g["ident_f32"][:], 0.0,
                           g["m_cumT"][:, h:h + 1], ALU.mult, ALU.add,
                           waits=[tc2, ct, n1_free[hp], d_free[hp]])
                a2 = k.tp(ps_t2[hp][:, 0:128], g["m_n1"][:, hp, :],
                          g["ident_f32"], waits=[a1, ct, ps3_free[hp]])
                aM = k.dve("tensor_scalar", g["m_n1"][:, hp, :],
                           ps_t2[hp][:, 0:128],
                           g["m_cumT"][:, 16 + h:17 + h], 0.0, ALU.add,
                           ALU.min, waits=[a2, tn1, a1])
                n1_free[hp] = None
                a3 = k.act(g["m_D"][:, hp, :], g["m_n1"][:, hp, :],
                           AF.Exp, waits=[aM, d_free[hp]])
                a4 = k.act(g["m_w"][:, hp:hp + 1],
                           ps_t2[hp][:, wcol_idx:wcol_idx + 1], AF.Exp,
                           bias=g["m_cumT"][:, 16 + h:17 + h],
                           waits=[a2, tn1, w_free2[hp]])
                a5 = k.act(g["m_e"][0:1, hp, :], ps_t2[hp][0:1, 0:128],
                           AF.Exp, waits=[a2, e_free[hp]])
                ps3_free[hp] = [a3, a4, a5, aM]
                a6 = k.mm(PS[1][:, 0:128], g["ones_row"],
                          g["m_e"][0:1, hp, :], start=True, stop=True,
                          waits=[a5, ct, eb_free])
                a7 = k.act(g["m_eb"][:, hp, :], PS[1][:, 0:128], AF.Copy,
                           waits=[a6, eb2_free[hp]])
                eb_free = a7
                e_free[hp] = a6
                a8 = k.tp(PSB[:, 128 + 64 * hp:192 + 64 * hp],
                          g["m_xh"][hr:hr + 64, cb, hc8, 0:128],
                          g["ident_bf"][hr:hr + 64, hr:hr + 64],
                          waits=[xht, ct, dtx_free[hp]])
                a9 = k.act(g["m_dtxT"][:, hp, :],
                           PSB[:, 128 + 64 * hp:192 + 64 * hp], AF.Copy,
                           scale=g["m_dtT"][:, h:h + 1],
                           waits=[a8, td2, g["dtx2_war"]])
                dtx_free[hp] = a9
                b1 = k.dve("tensor_mul", g["m_P"][:, hp, :], g["m_G"][:],
                           g["m_D"][:, hp, :], waits=[tg2, a3,
                                                      p_free[hp]])
                d_free[hp] = b1
                b2 = k.mm(PS[2][0:64, 0:128], g["m_dtxT"][:, hp, 0:64],
                          g["m_P"][:, hp, :], start=True, stop=True,
                          waits=[a9, b1, psy_free])
                p_free[hp] = b2
                b3 = k.mm(PS[3][0:64, 0:128], hTb[:, h, :],
                          g["m_C"][:, csl], start=True, stop=True,
                          waits=[ht_tok[0], ht_tok[1], conv_done[1],
                                 psy2_free])
                b4 = k.dve("tensor_mul", g["m_y1"][hr:hr + 64, :],
                           PS[3][0:64, 0:128],
                           g["m_eb"][hr:hr + 64, hp, :],
                           waits=[b3, a7, y1_free[hp]])
                psy2_free = b4
                b5 = k.dve("tensor_add", g["m_y2"][hr:hr + 64, :],
                           g["m_y1"][hr:hr + 64, :],
                           PS[2][0:64, 0:128], waits=[b4, b2,
                                                      y2_free[hp]])
                psy_free = b5
                y1_free[hp] = b5
                b6 = k.dve("scalar_tensor_tensor",
                           g["m_ycol"][hr:hr + 64, hc8, :],
                           g["m_xh"][hr:hr + 64, cb, hc8, 0:128],
                           g["m_drep"][hr:hr + 64, hc8:hc8 + 1],
                           g["m_y2"][hr:hr + 64, :], ALU.mult, ALU.add,
                           waits=[b5, xht, sw, ycol_free])
                y2_free[hp] = b6
                c1 = k.dve("tensor_scalar", g["m_Bw"][:, hp, :],
                           g["m_BT"][:], g["m_w"][:, hp:hp + 1], None,
                           ALU.mult, waits=[tb2, a4, bw_free[hp]])
                w_free2[hp] = c1
                c2 = k.mm(PS[4][:, 0:64], g["m_Bw"][:, hp, :],
                          g["m_dtxT"][:, hp, 0:64], start=True,
                          stop=True, waits=[c1, a9, psh_free])
                bw_free[hp] = c2
                dtx_free[hp] = [a9, c2]
                c3 = k.dve("scalar_tensor_tensor", hT[:, h, :],
                           hT[:, h, :],
                           g["m_eb"][:, hp, wcol_idx:wcol_idx + 1],
                           PS[4][:, 0:64], ALU.mult, ALU.add,
                           waits=[c2, a7, ht_tok[0], ht_tok[1]])
                psh_free = c3
                c4 = k.dve("tensor_copy", hTb[:, h, :], hT[:, h, :],
                           waits=[c3])
                ht_tok = [c3, c4]
                eb2_free[hp] = [c3, b4]
            bt_free = k.fence("vector")
            ct_free = k.fence("scalar")
            dtt_free = k.fence("scalar")
            psb_free = k.fence("vector")
            g["G_war"] = k.fence("vector")
            g["dtx2_war"] = k.fence("tensor")
            yst = []
            for ch8 in range(NCH):
                k.dma("sync", g["y_dram"][ch8, :, csl],
                      g["m_ycol"][:, ch8, :],
                      waits=[k.fence("vector")])
            ycol_free = k.qend("sync")
            xh_free[cb] = k.fence("tensor")
        y_done = k.qend("sync")

        k.dma("gpsimd", g["m_wout"][:],
              io[pre + "outp_wT"].rearrange("(c p) o -> p c o", p=128),
              waits=[k.fence("tensor")])
        wtok = k.qend("gpsimd")
        oc0 = 0 if fwd else 4
        gn_free = None
        y2_war = None
        yg_war = None
        po_free = [k.fence("vector")] * 2
        ot_free = [k.fence("vector")] * 2
        for tt in range(nt):
            sl = slice(tt * TT, (tt + 1) * TT)
            buf = tt % 2
            for ch8 in range(NCH):
                k.dma("sync", g["m_xh"][:, 0, ch8, :],
                      g["y_dram"][ch8, :, sl], waits=[y_done,
                                                      g["yl_war"]])
                k.dma("sync", g["m_xh"][:, 1, ch8, :],
                      g["z_dram"][ch8, :, sl], waits=[g["yl_war"]])
            lz = k.qend("sync")
            sq_toks = []
            y2t = g["xg_t"]
            for ch8 in range(NCH):
                zt = k.act(g["m_evt"][:, ch8 % 2, :],
                           g["m_xh"][:, 1, ch8, :], AF.Silu,
                           waits=[lz, g["ev2_war"][ch8 % 2]])
                yt = k.dve("tensor_mul", y2t[:, ch8, :],
                           g["m_xh"][:, 0, ch8, :],
                           g["m_evt"][:, ch8 % 2, :],
                           waits=[zt, lz, y2_war])
                g["ev2_war"][ch8 % 2] = yt
                sq = k.act(g["a_sq"][:, ch8, :], y2t[:, ch8, :],
                           AF.Square, waits=[yt, g["sq2_war"]])
                sq_toks.append([sq, yt])
            tok = None
            for ch8 in range(NCH):
                tok = k.mm(PS[6][0:1, :], g["ones_col"],
                           g["a_sq"][:, ch8, :],
                           start=(ch8 == 0), stop=(ch8 == 7),
                           waits=[sq_toks[ch8][0], ct, gn_free])
            cc5, tk5 = g["cc"][1e-5]
            sd = g["ln_rows"][0:1, 4, :]
            rr = g["ln_rows"][0:1, 5, :]
            t5 = k.act(sd, PS[6][0:1, :], AF.Sqrt, bias=cc5[0:1, :],
                       scale=1.0 / D, waits=[tok, tk5, g["r5_war"]])
            gn_free = t5
            t6 = k.dve("reciprocal", rr, sd, waits=[t5])
            t8 = k.dve("tensor_copy", g["ln_rowb"][0:1, 0, :], rr,
                       waits=[t6])
            tb1 = k.mm(PS[6], g["ones_row"], g["ln_rowb"][0:1, 0, :],
                       start=True, stop=True, waits=[t8, t5, ct,
                                                     yg_war])
            g["r5_war"] = tb1
            yg_toks = []
            for ch8 in range(NCH):
                yg_toks.append(k.dve(
                    "scalar_tensor_tensor", g["m_yg"][:, ch8, :],
                    y2t[:, ch8, :], g["m_gn"][:, ch8:ch8 + 1], PS[6],
                    ALU.mult, ALU.mult,
                    waits=[tb1, sq_toks[ch8][1], sw, g["yg2_war"]]))
            yg_war = yg_toks[-1]
            y2_war = k.fence("scalar")
            g["sq2_war"] = k.fence("tensor")
            for oc in range(4):
                pq = PS[oc % 2]
                tok = None
                for ch8 in range(NCH):
                    tok = k.mm(pq, g["m_wout"][:, ch8,
                                               oc * 128:oc * 128 + 128],
                               g["m_yg"][:, ch8, :], start=(ch8 == 0),
                               stop=(ch8 == 7),
                               waits=[wtok, yg_toks[ch8],
                                      po_free[oc % 2]])
                ev = k.act(g["m_evt"][:, oc % 2, :], pq, AF.Copy,
                           waits=[tok, ot_free[oc % 2]])
                po_free[oc % 2] = ev
                k.dma("sync", g["cat_dram"][oc0 + oc, :, sl],
                      g["m_evt"][:, oc % 2, :], waits=[ev])
                ot_free[oc % 2] = k.qend("sync")
            g["yg2_war"] = k.fence("tensor")
            g["yl_war"] = [k.fence("tensor"), k.fence("scalar"),
                           k.fence("vector")]
        g["mw_war"] = k.fence("tensor")
        g["xt_war"] = k.fence("tensor")
        g["ev_war"] = k.fence("sync")
        g["dt_war"] = k.fence("vector")
        g["ldA_war"] = k.fence("vector")
        g["hT_war"] = k.fence("vector")
        g["B_war"] = k.fence("tensor")
        g["C_war"] = k.fence("tensor")
        return k.qend("sync")

    for nm, v in [("mw_war", None), ("xt_war", None), ("ev_war", None),
                  ("dt_war", None), ("dt2_war", None), ("dt3_war", None), ("a16_war", None),
                  ("ldA_war", None), ("hT_war", None), ("B_war", None),
                  ("C_war", None), ("G_war", None), ("dtx2_war", None),
                  ("yl_war", None), ("sq2_war", None), ("yg2_war", None),
                  ("r5_war", None)]:
        g[nm] = v
    g["ev2_war"] = [None, None]

    direction(0)
    cat_f = direction(1)

    k.dma("gpsimd", g["m_wop"][:],
          io["op_wT"].rearrange("(c p) o -> p c o", p=128),
          waits=[g["mw_war"]])
    wop = k.qend("gpsimd")
    k.dma("sync", g["m_opb"][:], io["op_b_fm"][:])
    opb = k.qend("sync")
    ct_free = [None, None]
    po_free = [k.fence("vector")] * 2
    ot_free = [k.fence("vector")] * 2
    res_prev = None
    outs = []
    for tt in range(nt):
        sl = slice(tt * TT, (tt + 1) * TT)
        for ch8 in range(NCH):
            k.dma("sync", g["m_xh"][:, 0, ch8, :],
                  g["cat_dram"][ch8, :, sl], waits=[cat_f, res_prev])
        lc = k.qend("sync")
        for oc in range(NCH):
            pq = PS[oc % 2]
            tok = None
            for ch8 in range(NCH):
                tok = k.mm(pq, g["m_wop"][:, ch8, oc * 128:oc * 128 + 128],
                           g["m_xh"][:, 0, ch8, :], start=(ch8 == 0),
                           stop=(ch8 == 7),
                           waits=[wop, lc, po_free[oc % 2]])
            k.dma("sync", g["m_evt"][:, oc % 2, :], A[oc, :, sl],
                  waits=[ot_free[oc % 2]])
            k.dma("sync", g["ln_xf"][:, 0, oc % 2, :],
                  io["x_in"][oc, :, sl], waits=[ot_free[oc % 2]])
            lax = k.qend("sync")
            e1 = k.dve("tensor_scalar", g["m_y1"][0:128, 0:TT] if False
                       else g["xg_t"][:, oc % 2, :], pq,
                       g["m_opb"][:, oc:oc + 1], None, ALU.add,
                       waits=[tok, opb, ct_free[oc % 2]])
            po_free[oc % 2] = e1
            e2 = k.dve("tensor_add", g["xg_t"][:, 2 + oc % 2, :],
                       g["xg_t"][:, oc % 2, :], g["m_evt"][:, oc % 2, :],
                       waits=[e1, lax])
            e3 = k.dve("scalar_tensor_tensor",
                       g["ln_xf"][:, 1, oc % 2, :],
                       g["xg_t"][:, 2 + oc % 2, :], 1.0,
                       g["ln_xf"][:, 0, oc % 2, :], ALU.mult, ALU.add,
                       waits=[e2])
            ct_free[oc % 2] = e3
            k.dma("sync", io["x_out"][oc, :, sl],
                  g["ln_xf"][:, 1, oc % 2, :], waits=[e3])
            ot_free[oc % 2] = k.qend("sync")
        res_prev = ot_free[1]
    return outs + [ot_free[0], ot_free[1]]


def build_mamba_program(T):
    nc = bass.Bass("TRN2", target_bir_lowering=False)
    io = {}
    io["x_in"] = nc.dram_tensor("x_in", [NCH, 128, T], F32,
                                kind="ExternalInput").ap()
    io["x_out"] = nc.dram_tensor("x_out", [NCH, 128, T], F32,
                                 kind="ExternalOutput").ap()
    specs = [("norm_fm", [128, 8]), ("op_wT", [D, D]),
             ("op_b_fm", [128, 8])]
    for pre in ("f_", "b_"):
        specs += [(pre + "in_wT", [512, 2320]),
                  (pre + "conv_diag", [10, 4, 128, 128]),
                  (pre + "conv_b_fm", [128, 10]),
                  (pre + "dt_bias", [16, 1]), (pre + "A_log", [16, 1]),
                  (pre + "D_fm", [128, 8]), (pre + "gnorm_fm", [128, 8]),
                  (pre + "outp_wT", [D, 512])]
    for n, shp in specs:
        io[n] = nc.dram_tensor(n, shp, F32, kind="ExternalInput").ap()
    io["c_ident_bf"] = nc.dram_tensor("c_ident_bf", [128, 128], BF16,
                                      kind="ExternalInput").ap()
    io["c_trimask"] = nc.dram_tensor("c_trimask", [128, 256], BF16,
                                     kind="ExternalInput").ap()
    declare_consts(nc, io)

    g = alloc_common(nc, T)
    for n, shp, dt in [("xn_dram", [NCH, 128, T], BF16),
                       ("xm_dram", [NCH, 128, T], BF16),
                       ("z_dram", [NCH, 128, T], BF16),
                       ("xbc_dram", [10, 128, T], BF16),
                       ("xsil_dram", [NCH, 128, T], BF16),
                       ("y_dram", [NCH, 128, T], BF16),
                       ("cat_dram", [NCH, 128, T], BF16)]:
        g[n] = nc.dram_tensor(n.replace("dram", "scr"), shp, dt).ap()
    g["normw"] = nc.alloc_sbuf_tensor("normw", [128, 8], F32).ap()
    g["ident_bf"] = nc.alloc_sbuf_tensor("ident_bf", [128, 128],
                                         BF16).ap()
    g["trimask"] = nc.alloc_sbuf_tensor("trimask", [128, 256], BF16).ap()
    g["a_tile"] = g["ln_xb"]
    g["a_sq"] = g["ln_sq"]
    g["xg_t"] = nc.alloc_sbuf_tensor("xg_t", [128, NCH, TT], BF16).ap()
    g["f_xt"] = nc.alloc_sbuf_tensor("f_xt", [128, 2, 4, TT], BF16).ap()
    wu = nc.alloc_sbuf_tensor("m_wu", [128, 9280], BF16)
    waddr = nc.lookup_mloc(wu).addr
    g["m_win"] = nc.alloc_sbuf_tensor_at(
        "m_win", [128, 4, 2320], BF16, offset=waddr).ap()
    g["m_conv"] = nc.alloc_sbuf_tensor_at(
        "m_conv", [128, 40, 128], BF16, offset=waddr).ap()
    g["m_convb"] = nc.alloc_sbuf_tensor("m_convb", [128, 10], F32).ap()
    g["m_dtb"] = nc.alloc_sbuf_tensor("m_dtb", [16, 1], F32).ap()
    g["m_alog"] = nc.alloc_sbuf_tensor("m_alog", [16, 1], F32).ap()
    g["m_drep"] = nc.alloc_sbuf_tensor("m_drep", [128, 8], F32).ap()
    g["m_gn"] = nc.alloc_sbuf_tensor("m_gn", [128, 8], F32).ap()
    g["m_wout"] = nc.alloc_sbuf_tensor_at(
        "m_wout", [128, 8, 512], BF16, offset=waddr).ap()
    g["m_wop"] = nc.alloc_sbuf_tensor_at(
        "m_wop", [128, 8, 1024], BF16, offset=waddr).ap()
    g["m_opb"] = nc.alloc_sbuf_tensor("m_opb", [128, 8], F32).ap()
    g["m_evt"] = nc.alloc_sbuf_tensor("m_evt", [128, 2, TT], BF16).ap()
    g["m_sc16"] = nc.alloc_sbuf_tensor("m_sc16", [16, T], F32).ap()
    g["m_dt"] = nc.alloc_sbuf_tensor("m_dt", [16, T], F32).ap()
    g["m_ldA"] = nc.alloc_sbuf_tensor("m_ldA", [16, T], F32).ap()
    g["m_a16"] = nc.alloc_sbuf_tensor("m_a16", [16, 2], F32).ap()
    g["m_xbt"] = nc.alloc_sbuf_tensor("m_xbt", [128, 1, 10, 515],
                                      BF16).ap()
    g["m_B"] = nc.alloc_sbuf_tensor("m_B", [128, T], BF16).ap()
    g["m_C"] = nc.alloc_sbuf_tensor("m_C", [128, T], BF16).ap()
    g["m_xh"] = nc.alloc_sbuf_tensor("m_xh", [128, 2, NCH, TT],
                                     BF16).ap()
    g["m_cum"] = nc.alloc_sbuf_tensor("m_cum", [16, 2, 128], F32).ap()
    g["m_z16"] = nc.alloc_sbuf_tensor("m_z16", [16, 128], F32).ap()
    g["m_cumT"] = nc.alloc_sbuf_tensor("m_cumT", [128, 32], F32).ap()
    g["m_dtT"] = nc.alloc_sbuf_tensor("m_dtT", [128, 16], F32).ap()
    g["m_BT"] = nc.alloc_sbuf_tensor("m_BT", [128, 128], BF16).ap()
    g["m_Bw"] = nc.alloc_sbuf_tensor("m_Bw", [128, 2, 128], BF16).ap()
    g["m_G"] = nc.alloc_sbuf_tensor("m_G", [128, 128], BF16).ap()
    g["m_D"] = nc.alloc_sbuf_tensor("m_D", [128, 2, 128], BF16).ap()
    g["m_P"] = nc.alloc_sbuf_tensor("m_P", [128, 2, 128], BF16).ap()
    g["m_n1"] = nc.alloc_sbuf_tensor("m_n1", [128, 2, 128], F32).ap()
    g["m_w"] = nc.alloc_sbuf_tensor("m_w", [128, 2], F32).ap()
    g["m_e"] = nc.alloc_sbuf_tensor("m_e", [1, 2, 128], BF16).ap()
    g["m_eb"] = nc.alloc_sbuf_tensor("m_eb", [128, 2, 128], F32).ap()
    g["m_dtxT"] = nc.alloc_sbuf_tensor("m_dtxT", [128, 2, 64], BF16).ap()
    g["m_hT"] = nc.alloc_sbuf_tensor("m_hT", [128, 16, 64], F32).ap()
    g["m_hTb"] = nc.alloc_sbuf_tensor("m_hTb", [128, 16, 64], BF16).ap()
    g["m_y1"] = nc.alloc_sbuf_tensor("m_y1", [128, 128], F32).ap()
    g["m_y2"] = nc.alloc_sbuf_tensor("m_y2", [128, 128], F32).ap()
    g["m_ycol"] = nc.alloc_sbuf_tensor("m_ycol", [128, NCH, 128],
                                       BF16).ap()
    xbt_addr = nc.lookup_mloc(g["m_xbt"].tensor).addr
    g["m_yg"] = nc.alloc_sbuf_tensor_at(
        "m_yg", [128, NCH, TT], BF16, offset=xbt_addr).ap()

    k = KB(nc)
    k.io = io

    def body(name):
        load_consts(k, g)
        outs = emit_mamba(k, g, io, T)
        finish_program(k, outs)

    build_block(nc, k, body)
    return nc


def prep_mamba_inputs(xT, W, T):
    d = {
        "x_in": np.ascontiguousarray(xT.reshape(NCH, 128, T)),
        "norm_fm": fm(W["norm_w"]),
        "op_wT": np.ascontiguousarray(W["op_w"].T),
        "op_b_fm": fm(W["op_b"]),
    }
    for pre in ("f_", "b_"):
        cw = W[pre + "conv_w"]
        cd = np.zeros((10, 4, 128, 128), np.float32)
        for cch in range(10):
            for kk in range(4):
                np.fill_diagonal(cd[cch, kk], cw[cch * 128:(cch + 1) * 128,
                                                kk])
        d[pre + "in_wT"] = np.ascontiguousarray(W[pre + "in_w"].T)
        d[pre + "conv_diag"] = cd
        d[pre + "conv_b_fm"] = fm(W[pre + "conv_b"].reshape(1280)[
            :1280]).reshape(128, 10) if False else np.ascontiguousarray(
            W[pre + "conv_b"].reshape(10, 128).T)
        d[pre + "dt_bias"] = W[pre + "dt_bias"].reshape(16, 1)
        d[pre + "A_log"] = W[pre + "A_log"].reshape(16, 1)
        d[pre + "D_fm"] = np.ascontiguousarray(
            np.repeat(W[pre + "D"], 64).reshape(8, 128).T)
        d[pre + "gnorm_fm"] = fm(W[pre + "gnorm_w"])
        d[pre + "outp_wT"] = np.ascontiguousarray(W[pre + "outp_w"].T)
    tri = np.zeros((128, 256), np.float32)
    s_ = np.arange(128)[:, None]
    t_ = np.arange(128)[None, :]
    tri[:, 0:128] = (s_ <= t_)
    tri[:, 128:256] = (s_ >= t_)
    d["c_trimask"] = tri.astype(BF)
    d["c_ident_bf"] = np.eye(128, dtype=np.float32).astype(BF)
    d = {k_: np.ascontiguousarray(v).astype(
        v.dtype if v.dtype == BF else np.float32) for k_, v in d.items()}
    d.update(const_inputs())
    return d



def alloc_common(nc, T):
    g = {}
    g["PS"] = [nc.alloc_psum_tensor(f"ps{i}", [128, 512], F32).ap()
               for i in range(7)]
    g["PSB"] = nc.alloc_psum_tensor("psb", [128, 1024], BF16).ap()
    g["ones_col"] = nc.alloc_sbuf_tensor("ones_col", [128, 1], BF16).ap()
    g["ones_row"] = nc.alloc_sbuf_tensor("ones_row", [1, 128], BF16).ap()
    g["ident_f32"] = nc.alloc_sbuf_tensor("ident_f32", [128, 128], F32).ap()
    g["ln_xf"] = nc.alloc_sbuf_tensor("ln_xf", [128, 2, NCH, TT], F32).ap()
    g["ln_sq"] = nc.alloc_sbuf_tensor("ln_sq", [128, NCH, TT], BF16).ap()
    g["ln_xb"] = nc.alloc_sbuf_tensor("ln_xb", [128, NCH, TT], BF16).ap()
    g["ln_rows"] = nc.alloc_sbuf_tensor("ln_rows", [1, 7, TT], F32).ap()
    g["ln_rowb"] = nc.alloc_sbuf_tensor("ln_rowb", [1, 2, TT], BF16).ap()
    g["ccol"] = nc.alloc_sbuf_tensor("ccol", [128, 8], F32).ap()
    return g


def declare_consts(nc, io):
    io["c_ones_col"] = nc.dram_tensor("c_ones_col", [128, 1], BF16,
                                      kind="ExternalInput").ap()
    io["c_ones_row"] = nc.dram_tensor("c_ones_row", [1, 128], BF16,
                                      kind="ExternalInput").ap()
    io["c_ident_f32"] = nc.dram_tensor("c_ident_f32", [128, 128], F32,
                                       kind="ExternalInput").ap()


def const_inputs():
    return {
        "c_ones_col": np.ones((128, 1), BF),
        "c_ones_row": np.ones((1, 128), BF),
        "c_ident_f32": np.eye(128, dtype=np.float32),
    }


CCVALS = [1e-5, 1e-6, -8.0, 1.0, 0.5, -0.5]


def load_consts(k, g):
    k.dma("sync", g["ones_col"][:], k.io["c_ones_col"][:])
    k.dma("sync", g["ones_row"][:], k.io["c_ones_row"][:])
    k.dma("sync", g["ident_f32"][:], k.io["c_ident_f32"][:])
    g["const_tok"] = k.qend("sync")
    g["cc"] = {}
    for i, v in enumerate(CCVALS):
        tok = k.pool("memset", g["ccol"][:, i:i + 1], v)
        g["cc"][v] = (g["ccol"][:, i:i + 1], tok)


def finish_program(k, outs):
    if k.cur == "sync":
        for s, c in k.counts.items():
            if c:
                k.eng.wait_ge(k.sem_h[s], c)


def build_block(nc, k, emit_body):
    def run(name, eng):
        k.begin(name, eng)
        emit_body(name)

    with nc.Block() as blk:
        blk.sync(lambda e: run("sync", e))
        blk.tensor(lambda e: run("tensor", e))
        blk.scalar(lambda e: run("scalar", e))
        blk.vector(lambda e: run("vector", e))
        blk.gpsimd(lambda e: run("gpsimd", e))



def build_ffn1_program(T):
    nc = bass.Bass("TRN2", target_bir_lowering=False)
    io = {}
    io["x_in"] = nc.dram_tensor("x_in", [NCH, 128, T], F32,
                                kind="ExternalInput").ap()
    io["x_out"] = nc.dram_tensor("x_out", [NCH, 128, T], F32,
                                 kind="ExternalOutput").ap()
    for n, shp in [("gate_w", [D, 8]), ("gate_b128", [128, 8]),
                   ("w1", [8, D, 4096]), ("b1_r", [8, 128, 32]),
                   ("w2", [8, 4096, D]), ("b2", [8, D]),
                   ("ws1", [D, 4096]), ("bs1_r", [128, 32]),
                   ("ws2", [4096, D]), ("bs2_fm", [128, 8])]:
        io[n] = nc.dram_tensor(n, shp, F32, kind="ExternalInput").ap()
    declare_consts(nc, io)

    g = alloc_common(nc, T)
    g["xn_dram"] = nc.dram_tensor("xn_scr", [NCH, 128, T], BF16).ap()
    g["acc_dram"] = nc.dram_tensor("acc_scr", [NCH, 128, T], BF16).ap()
    g["cmb_dram"] = nc.dram_tensor("cmb_scr", [8, T], F32).ap()
    g["f_gw"] = nc.alloc_sbuf_tensor("f_gw", [128, NCH, 8], BF16).ap()
    g["f_gb"] = nc.alloc_sbuf_tensor("f_gb", [128, 8], F32).ap()
    g["f_r8"] = nc.alloc_sbuf_tensor("f_r8", [128, 10, 8], F32).ap()
    g["f_ct8"] = nc.alloc_sbuf_tensor("f_ct8", [8, 128], F32).ap()
    g["f_cmbTb"] = nc.alloc_sbuf_tensor("f_cmbTb", [8, T], BF16).ap()
    g["f_xt"] = nc.alloc_sbuf_tensor("f_xt", [128, 2, NCH, TT], BF16).ap()
    g["f_w1b"] = nc.alloc_sbuf_tensor("f_w1b", [128, NCH, 2048], BF16).ap()
    g["f_w2b"] = nc.alloc_sbuf_tensor("f_w2b", [128, 16, D], BF16).ap()
    g["f_b1b"] = nc.alloc_sbuf_tensor("f_b1b", [128, 16], F32).ap()
    g["f_crow"] = nc.alloc_sbuf_tensor("f_crow", [1, T], BF16).ap()
    g["f_cbrd"] = nc.alloc_sbuf_tensor("f_cbrd", [128, T], BF16).ap()
    g["f_Hg"] = nc.alloc_sbuf_tensor("f_Hg", [128, 16, TT], BF16).ap()
    g["f_stg"] = nc.alloc_sbuf_tensor("f_stg", [128, 4, 4, TT], BF16).ap()
    g["f_b2s"] = nc.alloc_sbuf_tensor("f_b2s", [8, D], BF16).ap()
    g["f_accb"] = g["f_Hg"][:, 0:NCH, :]
    g["f_bs2"] = nc.alloc_sbuf_tensor("f_bs2", [128, 8], F32).ap()

    k = KB(nc)
    k.io = io

    def body(name):
        load_consts(k, g)
        outs = emit_ffn1(k, g, io, T)
        finish_program(k, outs)

    build_block(nc, k, body)
    return nc


def prep_ffn1_inputs(xT, W, T):
    d = {
        "x_in": np.ascontiguousarray(xT.reshape(NCH, 128, T)),
        "gate_w": np.ascontiguousarray(W["gate_w"]),
        "gate_b128": np.ascontiguousarray(
            np.broadcast_to(W["gate_b"], (128, 8))).astype(np.float32),
        "w1": np.ascontiguousarray(W["w1"]),
        "b1_r": np.ascontiguousarray(
            W["b1"].reshape(8, 32, 128).transpose(0, 2, 1)),
        "w2": np.ascontiguousarray(W["w2"]),
        "b2": np.ascontiguousarray(W["b2"]),
        "ws1": np.ascontiguousarray(W["ws1"]),
        "bs1_r": np.ascontiguousarray(W["bs1"].reshape(32, 128).T),
        "ws2": np.ascontiguousarray(W["ws2"]),
        "bs2_fm": fm(W["bs2"]),
    }
    d.update(const_inputs())
    return d


_PROG_CACHE = {}


def _prog(name, T):
    key = (name, T)
    if key not in _PROG_CACHE:
        _PROG_CACHE[key] = {"ffn0": build_ffn0_program,
                            "ffn1": build_ffn1_program,
                            "attn": build_attn_program,
                            "mamba": build_mamba_program}[name](T)
    return _PROG_CACHE[key]


def kernel(**inputs):
    T = 4096
    hidden = np.asarray(inputs["L0_attn_hidden"], dtype=np.float32)
    B = hidden.shape[0]
    xs = [np.ascontiguousarray(hidden[b].T).reshape(NCH, 128, T)
          for b in range(B)]

    w_ffn0 = {k_[len("L0_ffn_"):]: np.asarray(v, np.float32)
              for k_, v in inputs.items() if k_.startswith("L0_ffn_")}
    w_attn = {k_[len("L0_attn_"):]: np.asarray(v, np.float32)
              for k_, v in inputs.items() if k_.startswith("L0_attn_")
              and k_ != "L0_attn_hidden"}
    w_ffn1 = {k_[len("L1_ffn_"):]: np.asarray(v, np.float32)
              for k_, v in inputs.items() if k_.startswith("L1_ffn_")}
    w_mamba = {k_[len("L1_attn_"):]: np.asarray(v, np.float32)
               for k_, v in inputs.items() if k_.startswith("L1_attn_")}
    _ = (inputs["L1_attn_f_in_w"], inputs["L1_attn_f_conv_w"],
         inputs["L1_attn_f_conv_b"], inputs["L1_attn_f_dt_bias"],
         inputs["L1_attn_f_A_log"], inputs["L1_attn_f_D"],
         inputs["L1_attn_f_gnorm_w"], inputs["L1_attn_f_outp_w"],
         inputs["L1_attn_b_in_w"], inputs["L1_attn_b_conv_w"],
         inputs["L1_attn_b_conv_b"], inputs["L1_attn_b_dt_bias"],
         inputs["L1_attn_b_A_log"], inputs["L1_attn_b_D"],
         inputs["L1_attn_b_gnorm_w"], inputs["L1_attn_b_outp_w"],
         inputs["L1_attn_norm_w"], inputs["L1_attn_op_w"],
         inputs["L1_attn_op_b"])

    if FUSED:
        key = ("fused", T)
        if key not in _PROG_CACHE:
            _PROG_CACHE[key] = build_fused_program(T)
        nc = _PROG_CACHE[key]
        shared = prep_fused_inputs(w_ffn0, w_attn, w_ffn1, w_mamba, T)
        in_maps = [dict(shared, x_in=xs[b]) for b in range(B)]
        res = run_bass_kernel_spmd(nc, in_maps, core_ids=list(range(B)))
        xs = [np.asarray(r["x_out"]) for r in res.results]
        out = np.stack([xs[b].reshape(D, T).T for b in range(B)], axis=0)
        return np.ascontiguousarray(out.astype(np.float32))

    stage_list = [("ffn0", w_ffn0, prep_ffn0_inputs),
                  ("attn", w_attn, prep_attn_inputs),
                  ("ffn0", w_ffn0, prep_ffn0_inputs),
                  ("ffn1", w_ffn1, prep_ffn1_inputs),
                  ("mamba", w_mamba, prep_mamba_inputs),
                  ("ffn1", w_ffn1, prep_ffn1_inputs)]

    shared_cache = {}
    for name, W, prep in stage_list:
        if name not in shared_cache:
            d = prep(np.zeros((D, T), np.float32), W, T)
            del d["x_in"]
            shared_cache[name] = d

    for name, W, prep in stage_list:
        nc = _prog(name, T)
        shared = shared_cache[name]
        in_maps = [dict(shared, x_in=xs[b]) for b in range(B)]
        res = run_bass_kernel_spmd(nc, in_maps, core_ids=list(range(B)))
        xs = [np.asarray(r["x_out"]) for r in res.results]

    out = np.stack([xs[b].reshape(D, T).T for b in range(B)], axis=0)
    return np.ascontiguousarray(out.astype(np.float32))



def _sb_factory(nc, prefix):
    def sb(name, shape, dtype):
        return nc.alloc_sbuf_tensor(prefix + name, shape, dtype).ap()
    return sb


class Arena:

    def __init__(self, nc, nbytes):
        self.nc = nc
        self.h = nc.alloc_sbuf_tensor("stage_arena", [128, nbytes],
                                      mybir.dt.uint8)
        self.base = nc.lookup_mloc(self.h).addr
        self.nbytes = nbytes
        self.off = 0
        self.n = 0

    def reset(self):
        self.off = 0

    def sb(self, name, shape, dtype):
        per = int(np.prod(shape[1:])) * mybir.dt.size(dtype)
        per = (per + 31) // 32 * 32
        assert self.off + per <= self.nbytes, (
            f"stage arena overflow at {name}: {self.off + per} > "
            f"{self.nbytes}")
        self.n += 1
        ap = self.nc.alloc_sbuf_tensor_at(
            f"ar{self.n}_{name}", shape, dtype,
            offset=self.base + self.off).ap()
        self.off += per
        return ap


def alloc_ffn1_g(nc, g, io, T, sb):
    if "xn_dram" not in g:
        g["xn_dram"] = nc.dram_tensor("xn_scr", [NCH, 128, T], BF16).ap()
    if "acc_dram" not in g:
        g["acc_dram"] = nc.dram_tensor("acc_scr", [NCH, 128, T],
                                       BF16).ap()
    if "cmb_dram" not in g:
        g["cmb_dram"] = nc.dram_tensor("cmb_scr", [8, T], F32).ap()
    g["f_gw"] = sb("f_gw", [128, NCH, 8], BF16)
    g["f_gb"] = sb("f_gb", [128, 8], F32)
    g["f_r8"] = sb("f_r8", [128, 10, 8], F32)
    g["f_ct8"] = sb("f_ct8", [8, 128], F32)
    g["f_cmbTb"] = sb("f_cmbTb", [8, T], BF16)
    g["f_xt"] = sb("f_xt", [128, 2, NCH, TT], BF16)
    g["f_w1b"] = sb("f_w1b", [128, NCH, 2048], BF16)
    g["f_w2b"] = sb("f_w2b", [128, 16, D], BF16)
    g["f_b1b"] = sb("f_b1b", [128, 16], F32)
    g["f_crow"] = sb("f_crow", [1, T], BF16)
    g["f_cbrd"] = sb("f_cbrd", [128, T], BF16)
    g["f_Hg"] = sb("f_Hg", [128, 16, TT], BF16)
    g["f_stg"] = sb("f_stg", [128, 4, 4, TT], BF16)
    g["f_b2s"] = sb("f_b2s", [8, D], BF16)
    g["f_accb"] = g["f_Hg"][:, 0:NCH, :]
    g["f_bs2"] = sb("f_bs2", [128, 8], F32)


def alloc_ffn0_g(nc, g, io, T, sb):
    if "xn_dram" not in g:
        g["xn_dram"] = nc.dram_tensor("xn_scr", [NCH, 128, T], BF16).ap()
    if "acc_dram" not in g:
        g["acc_dram"] = nc.dram_tensor("acc_scr", [NCH, 128, T],
                                       BF16).ap()
    if "cmb_dram" not in g:
        g["cmb_dram"] = nc.dram_tensor("cmb_scr", [8, T], F32).ap()
    g["f_gw"] = sb("f_gw", [128, NCH, 8], BF16)
    g["f0_gwf"] = sb("f0_gwf", [128, NCH, 8], F32)
    g["f0_rsc"] = sb("f0_rsc", [128, 8], F32)
    g["f0_var"] = sb("f0_var", [1, T], F32)
    g["f0_rs"] = sb("f0_rs", [1, 4, TT], F32)
    g["f0_rcol"] = sb("f0_rcol", [128, T // 128], F32)
    g["f0_pes"] = sb("f0_pes", [1, 8], BF16)
    g["f0_pesb"] = sb("f0_pesb", [128, 8], F32)
    g["f_r8"] = sb("f_r8", [128, 10, 8], F32)
    g["f_ct8"] = sb("f_ct8", [8, 128], F32)
    g["f_xt"] = sb("f_xt", [128, 2, NCH, TT], BF16)
    g["f0_wg0"] = sb("f0_wg0", [128, NCH, 1024], BF16)
    g["f0_wg1"] = sb("f0_wg1", [128, NCH, 1024], BF16)
    g["f_w2b"] = sb("f_w2b", [128, 8, D], BF16)
    g["f_crow"] = sb("f_crow", [1, T], BF16)
    g["f_cbrd"] = sb("f_cbrd", [128, T], BF16)
    g["f_Hg"] = sb("f_Hg", [128, 8, TT], BF16)
    g["f0_gt"] = sb("f0_gt", [128, TT], BF16)
    g["ntmp"] = 8
    g["f_tmp"] = sb("f_tmp", [128, 8, TT], BF16)
    g["f_accb"] = sb("f_accb", [128, NCH, TT], BF16)


def alloc_attn_g(nc, g, io, T, sb):
    if "xn_dram" not in g:
        g["xn_dram"] = nc.dram_tensor("xn_scr", [NCH, 128, T], BF16).ap()
    if "q_dram" not in g:
        g["q_dram"] = nc.dram_tensor("q_scr", [NCH, 128, T], BF16).ap()
    g["norm1"] = sb("norm1", [128, 8], F32)
    g["norm2"] = sb("norm2", [128, 8], F32)
    g["ind2"] = sb("ind2", [128, 2], BF16)
    g["ind2T"] = sb("ind2T", [2, 128], BF16)
    g["rotM"] = sb("rotM", [128, 128], BF16)
    g["ident_bf"] = sb("ident_bf", [128, 128], BF16)
    g["cos_t"] = sb("cos_t", [128, TT], BF16)
    g["sin_t"] = sb("sin_t", [128, TT], BF16)
    g["amask"] = sb("amask", [128, 10, TT], BF16)
    g["a_tile"] = g["ln_xb"]
    g["a_sq"] = g["ln_sq"]
    g["xg_t"] = sb("xg_t", [128, NCH, TT], BF16)
    g["row2"] = sb("row2", [2, 4, TT], F32)
    g["row2b"] = sb("row2b", [2, 2, TT], BF16)
    g["qraw"] = sb("qraw", [128, TT], BF16)
    g["sqt"] = sb("sqt", [128, TT], BF16)
    g["rp1"] = sb("rp1", [128, TT], F32)
    g["rp2"] = sb("rp2", [128, TT], F32)
    g["qst"] = sb("qst", [128, 2, TT], BF16)
    g["q_tile"] = sb("q_tile", [128, TT], BF16)
    g["k_sb"] = sb("k_sb", [128, 2, T], BF16)
    g["v_t"] = sb("v_t", [128, 2, TT], BF16)
    g["lat_t"] = sb("lat_t", [128, 2, TT], BF16)
    g["vT_sb"] = sb("vT_sb", [128, T // 128, 4, 65], BF16)
    g["E_t"] = sb("E_t", [128, 3, TT], BF16)
    g["brd_sb"] = sb("brd_sb", [64, TT], F32)
    g["o_col"] = sb("o_col", [128, NCH, TT], BF16)
    g["ntmp"] = 4
    g["f_tmp"] = sb("f_tmp", [128, 4, TT], BF16)
    g["f_accb"] = g["xg_t"]
    g["w_q"] = sb("w_q", [128, NCH, D], BF16)
    g["w_kvd"] = sb("w_kvd", [128, NCH, 256], BF16)
    g["w_kup"] = sb("w_kup", [128, 2, 256], BF16)
    g["w_vup"] = sb("w_vup", [128, 2, 256], BF16)
    g["w_kl"] = sb("w_kl", [128, NCH, 256], BF16)
    g["w_vl"] = sb("w_vl", [128, NCH, 256], BF16)
    g["w_o"] = sb("w_o", [128, NCH, D], BF16)


def alloc_mamba_g(nc, g, io, T, sb, arena=None):
    for n, shp, dt in [("xn_dram", [NCH, 128, T], BF16),
                       ("xm_dram", [NCH, 128, T], BF16),
                       ("z_dram", [NCH, 128, T], BF16),
                       ("xbc_dram", [10, 128, T], BF16),
                       ("xsil_dram", [NCH, 128, T], BF16),
                       ("y_dram", [NCH, 128, T], BF16),
                       ("cat_dram", [NCH, 128, T], BF16)]:
        if n not in g:
            g[n] = nc.dram_tensor(n.replace("dram", "scr"), shp, dt).ap()
    g["normw"] = sb("normw", [128, 8], F32)
    g["ident_bf"] = sb("ident_bf", [128, 128], BF16)
    g["trimask"] = sb("trimask", [128, 256], BF16)
    g["a_tile"] = g["ln_xb"]
    g["a_sq"] = g["ln_sq"]
    g["xg_t"] = sb("xg_t", [128, NCH, TT], BF16)
    g["f_xt"] = sb("f_xt", [128, 2, 4, TT], BF16)
    wu = sb("m_wu", [128, 9280], BF16)
    if arena is None:
        waddr = nc.lookup_mloc(wu.tensor).addr
        at = lambda n_, s_, d_: nc.alloc_sbuf_tensor_at(
            n_, s_, d_, offset=waddr).ap()
    else:
        waddr = nc.lookup_mloc(wu.tensor).addr
        at = lambda n_, s_, d_: nc.alloc_sbuf_tensor_at(
            "arm_" + n_, s_, d_, offset=waddr).ap()
    g["m_win"] = at("m_win", [128, 4, 2320], BF16)
    g["m_conv"] = at("m_conv", [128, 40, 128], BF16)
    g["m_wout"] = at("m_wout", [128, 8, 512], BF16)
    g["m_wop"] = at("m_wop", [128, 8, 1024], BF16)
    g["m_convb"] = sb("m_convb", [128, 10], F32)
    g["m_dtb"] = sb("m_dtb", [16, 1], F32)
    g["m_alog"] = sb("m_alog", [16, 1], F32)
    g["m_drep"] = sb("m_drep", [128, 8], F32)
    g["m_gn"] = sb("m_gn", [128, 8], F32)
    g["m_opb"] = sb("m_opb", [128, 8], F32)
    g["m_evt"] = sb("m_evt", [128, 2, TT], BF16)
    g["m_sc16"] = sb("m_sc16", [16, T], F32)
    g["m_dt"] = sb("m_dt", [16, T], F32)
    g["m_ldA"] = sb("m_ldA", [16, T], F32)
    g["m_a16"] = sb("m_a16", [16, 2], F32)
    g["m_xbt"] = sb("m_xbt", [128, 1, 10, 515], BF16)
    g["m_B"] = sb("m_B", [128, T], BF16)
    g["m_C"] = sb("m_C", [128, T], BF16)
    g["m_xh"] = sb("m_xh", [128, 2, NCH, TT], BF16)
    g["m_cum"] = sb("m_cum", [16, 2, 128], F32)
    g["m_z16"] = sb("m_z16", [16, 128], F32)
    g["m_cumT"] = sb("m_cumT", [128, 32], F32)
    g["m_dtT"] = sb("m_dtT", [128, 16], F32)
    g["m_BT"] = sb("m_BT", [128, 128], BF16)
    g["m_Bw"] = sb("m_Bw", [128, 2, 128], BF16)
    g["m_G"] = sb("m_G", [128, 128], BF16)
    g["m_D"] = sb("m_D", [128, 2, 128], BF16)
    g["m_P"] = sb("m_P", [128, 2, 128], BF16)
    g["m_n1"] = sb("m_n1", [128, 2, 128], F32)
    g["m_w"] = sb("m_w", [128, 2], F32)
    g["m_e"] = sb("m_e", [1, 2, 128], BF16)
    g["m_eb"] = sb("m_eb", [128, 2, 128], F32)
    g["m_dtxT"] = sb("m_dtxT", [128, 2, 64], BF16)
    g["m_hT"] = sb("m_hT", [128, 16, 64], F32)
    g["m_hTb"] = sb("m_hTb", [128, 16, 64], BF16)
    g["m_y1"] = sb("m_y1", [128, 128], F32)
    g["m_y2"] = sb("m_y2", [128, 128], F32)
    g["m_ycol"] = sb("m_ycol", [128, NCH, 128], BF16)
    xbt_addr = nc.lookup_mloc(g["m_xbt"].tensor).addr
    g["m_yg"] = nc.alloc_sbuf_tensor_at(
        "arm_m_yg", [128, NCH, TT], BF16, offset=xbt_addr).ap()


FFN0_IO = [("router_w", [D, 8]), ("router_scale_fm", [128, 8]),
           ("gating_wT", [8, 2, D, 2048]), ("linear_w", [8, 2048, D]),
           ("pes_row", [1, 8])]
FFN1_IO = [("gate_w", [D, 8]), ("gate_b128", [128, 8]),
           ("w1", [8, D, 4096]), ("b1_r", [8, 128, 32]),
           ("w2", [8, 4096, D]), ("b2", [8, D]), ("ws1", [D, 4096]),
           ("bs1_r", [128, 32]), ("ws2", [4096, D]), ("bs2_fm", [128, 8])]
ATTN_IO = [("norm1_fm", [128, 8]), ("norm2_fm", [128, 8]),
           ("g_wq", [D, D]), ("g_kv_down", [D, 256]),
           ("g_k_up", [256, 256]), ("g_v_up", [256, 256]),
           ("g_wo", [D, D]), ("l_wq", [D, D]), ("l_wk", [D, 256]),
           ("l_wv", [D, 256]), ("l_wo", [D, D])]


def MAMBA_IO():
    specs = [("norm_fm", [128, 8]), ("op_wT", [D, D]),
             ("op_b_fm", [128, 8])]
    for pre in ("f_", "b_"):
        specs += [(pre + "in_wT", [512, 2320]),
                  (pre + "conv_diag", [10, 4, 128, 128]),
                  (pre + "conv_b_fm", [128, 10]),
                  (pre + "dt_bias", [16, 1]), (pre + "A_log", [16, 1]),
                  (pre + "D_fm", [128, 8]), (pre + "gnorm_fm", [128, 8]),
                  (pre + "outp_wT", [D, 512])]
    return specs


def ATTN_CONST(T):
    return [("c_cos_g", [128, T]), ("c_sin_g", [128, T]),
            ("c_cos_l", [128, T]), ("c_sin_l", [128, T]),
            ("c_rot", [128, 128]), ("c_ind2", [128, 2]),
            ("c_ind2T", [2, 128]), ("c_amask", [128, 10, TT]),
            ("c_ident_bf", [128, 128])]


def build_fused_program(T):
    nc = bass.Bass("TRN2", target_bir_lowering=False)
    xin = nc.dram_tensor("x_in", [NCH, 128, T], F32,
                         kind="ExternalInput").ap()
    xout = nc.dram_tensor("x_out", [NCH, 128, T], F32,
                          kind="ExternalOutput").ap()
    X0 = nc.dram_tensor("Xping", [NCH, 128, T], F32).ap()
    X1 = nc.dram_tensor("Xpong", [NCH, 128, T], F32).ap()

    def decl(prefix, specs, dt=F32):
        d = {}
        for n, shp in specs:
            d[n] = nc.dram_tensor(prefix + n, shp, dt,
                                  kind="ExternalInput").ap()
        return d

    io_f0 = decl("f0$", FFN0_IO)
    io_f1 = decl("f1$", FFN1_IO)
    io_at = decl("at$", ATTN_IO)
    io_at.update(decl("at$", ATTN_CONST(T), BF16))
    io_mb = decl("mb$", MAMBA_IO())
    io_mb["c_ident_bf"] = io_at["c_ident_bf"]
    io_mb["c_trimask"] = nc.dram_tensor("mb$c_trimask", [128, 256], BF16,
                                        kind="ExternalInput").ap()
    io_all = {}
    declare_consts(nc, io_all)

    g = alloc_common(nc, T)
    g["xn_dram"] = nc.dram_tensor("xn_scr", [NCH, 128, T], BF16).ap()
    g["acc_dram"] = nc.dram_tensor("acc_scr", [NCH, 128, T], BF16).ap()
    g["cmb_dram"] = nc.dram_tensor("cmb_scr", [8, T], F32).ap()
    g["q_dram"] = nc.dram_tensor("q_scr", [NCH, 128, T], BF16).ap()
    for n in ["xm_dram", "z_dram", "xsil_dram", "y_dram", "cat_dram"]:
        g[n] = nc.dram_tensor(n.replace("dram", "scr"),
                              [NCH, 128, T], BF16).ap()
    g["xbc_dram"] = nc.dram_tensor("xbc_scr", [10, 128, T], BF16).ap()

    arena = Arena(nc, 143 * 1024)
    stage_g = {}
    for name, alloc in [("ffn0", alloc_ffn0_g), ("ffn1", alloc_ffn1_g),
                        ("attn", alloc_attn_g), ("mamba", alloc_mamba_g)]:
        arena.reset()
        sg = dict(g)
        io = {"ffn0": io_f0, "ffn1": io_f1, "attn": io_at,
              "mamba": io_mb}[name]
        io.update(io_all)
        alloc(nc, sg, io, T, arena.sb)
        stage_g[name] = (sg, io)

    k = KB(nc)
    k.io = io_all

    seq = [("ffn0", emit_ffn0), ("attn", emit_attn), ("ffn0", emit_ffn0),
           ("ffn1", emit_ffn1), ("mamba", emit_mamba),
           ("ffn1", emit_ffn1)]

    def body(name):
        sg0, _ = stage_g["ffn0"]
        load_consts(k, sg0)
        cc = sg0["cc"]
        ctk = sg0["const_tok"]
        outs = None
        for si, (snm, emitter) in enumerate(seq):
            sg, sio = stage_g[snm]
            sg["cc"] = cc
            sg["const_tok"] = ctk
            sio = dict(sio)
            sio["x_in"] = xin if si == 0 else (X0 if si % 2 == 1 else X1)
            sio["x_out"] = xout if si == len(seq) - 1 else (
                X0 if si % 2 == 0 else X1)
            outs = emitter(k, sg, sio, T)
            k.barrier()
        finish_program(k, outs)

    build_block(nc, k, body)
    return nc


def prep_fused_inputs(W0, Wat, W1, Wmb, T):
    z = np.zeros((D, T), np.float32)
    d = {}
    for prefix, prep, W in [("f0$", prep_ffn0_inputs, W0),
                            ("at$", prep_attn_inputs, Wat),
                            ("f1$", prep_ffn1_inputs, W1),
                            ("mb$", prep_mamba_inputs, Wmb)]:
        dd = prep(z, W, T)
        del dd["x_in"]
        for k_, v in dd.items():
            if k_.startswith("c_ones") or k_ == "c_ident_f32":
                d[k_] = v
            elif prefix == "mb$" and k_ == "c_ident_bf":
                d["at$c_ident_bf"] = v
            else:
                d[prefix + k_] = v
    return d
